# version 01
# baseline (speedup 1.0000x reference)
import os
import numpy as np
from contextlib import ExitStack

import concourse.bass as bass
import concourse.mybir as mybir
from concourse.bass_utils import run_bass_kernel_spmd

F32 = mybir.dt.float32
BF16 = mybir.dt.bfloat16
AF = mybir.ActivationFunctionType
ALU = mybir.AluOpType
AX = mybir.AxisListType

L = 8192
DM = 1024
DI = 2048
DS = 16
DTR = 64
OWN = 2048
PREF = 6144
TILE = 512
G = 16
CHK = 1024
NTOK = PREF + OWN
EPS_LN = 1e-5
EPS_RMS = 1.1920929e-07

_STAGE = int(os.environ.get("MK_STAGE", "4"))
_DBG = os.environ.get("MK_DBG", "0") == "1"
_NT_OWN = int(os.environ.get("MK_NT_OWN", "5"))
_F_XNT = os.environ.get("MK_F_XNT", "1") == "1"
_F_GATE = os.environ.get("MK_F_GATE", "1") == "1"
_F_Z = os.environ.get("MK_F_Z", "1") == "1"
_F_SCAN = os.environ.get("MK_F_SCAN", "1") == "1"
_F_XDBL = os.environ.get("MK_F_XDBL", "1") == "1"
_F_DT = os.environ.get("MK_F_DT", "1") == "1"
_F_DTSUB = int(os.environ.get("MK_F_DTSUB", "4"))
_NT_PREF = int(os.environ.get("MK_NT_PREF", "12"))
CONSTS = [EPS_LN, EPS_RMS, 1.0, 0.5, -0.5, -1.0, 0.0, 2.0]


class Sem:
    def __init__(self, es, nc, name, auto=False):
        self.h = es.enter_context(nc.semaphore(name))
        self.v = 0
        self.auto = auto

    def tag(self, instr, n=1):
        if self.auto:
            return self.v
        instr.then_inc(self.h, n)
        self.v += n
        return self.v


class SelfSync:

    _PASSTHRU = {"wait_ge", "dma_start"}

    def __init__(self, eng, sem):
        self._eng = eng
        self._sem = sem

    def wait_ge(self, h, v):
        return self._eng.wait_ge(h, v)

    def dma_start(self, *a, **k):
        return self._eng.dma_start(*a, **k)

    def __getattr__(self, name):
        f = getattr(self._eng, name)
        if not callable(f):
            return f
        sem = self._sem
        eng = self._eng

        def wrapped(*a, **k):
            if sem.v:
                eng.wait_ge(sem.h, sem.v)
            instr = f(*a, **k)
            if hasattr(instr, "then_inc"):
                instr.then_inc(sem.h, 1)
                sem.v += 1
            return instr

        return wrapped


class Rot:

    def __init__(self, bufs, ldsems=None, stsems=None):
        self.bufs = bufs if isinstance(bufs, list) else [bufs]
        self.free = [[] for _ in self.bufs]
        self.i = -1
        self.ldsems = ldsems
        self.stsems = stsems

    def acquire(self, writer):
        self.i += 1
        j = self.i % len(self.bufs)
        for sem, val in self.free[j]:
            if val:
                writer.wait_ge(sem.h, val)
        self.free[j] = []
        return self.bufs[j]

    def ldsem(self):
        return self.ldsems[self.i % len(self.bufs)]

    def stsem(self):
        return self.stsems[self.i % len(self.bufs)]

    def cur(self):
        return self.bufs[self.i % len(self.bufs)]

    def release(self, *marks):
        j = self.i % len(self.bufs)
        self.free[j].extend(marks)


class Builder:
    def __init__(self, nc):
        self.nc = nc
        self.es = ExitStack()
        es = self.es
        self.SY = nc.sync
        self.SC = nc.scalar
        self.VE = nc.vector
        self.SE = nc.scalar
        self.PE = nc.tensor
        self.GP = nc.gpsimd

        self.sLd = Sem(es, nc, "sLd")
        self.sSt = Sem(es, nc, "sSt")
        self.sGp = Sem(es, nc, "sGp")
        self.sPe = Sem(es, nc, "sPe")
        self.sVe = Sem(es, nc, "sVe", auto=True)
        self.sSe = Sem(es, nc, "sSe", auto=True)
        self.VE = SelfSync(self.VE, self.sVe)
        self.SE = SelfSync(self.SE, self.sSe)

        self.all_dma_sems = []
        self.dram = {}
        self.decl_dram()
        self.persist = {}
        P = self.persist
        P["A"] = self.sb(es, "A_sb", (128, G * DS))
        P["convw"] = self.sb(es, "convw_sb", (128, G * 4))
        P["convb"] = self.sb(es, "convb_sb", (128, G))
        P["dtb"] = self.sb(es, "dtb_sb", (128, G))
        P["D"] = self.sb(es, "d_sb", (128, G))
        P["ident"] = self.sb(es, "ident_sb", (128, 128))
        P["ones"] = self.sb(es, "ones_sb", (128, 128))
        P["ccol"] = self.sb(es, "ccol_sb", (128, 16 * 8))
        P["b1"] = self.sb(es, "b1_sb", (128, 8 * 32))
        P["cv"] = self.sb(es, "cv_sb", (128, 8))

    def decl_dram(self):
        nc = self.nc
        D = self.dram

        def inp(name, shape, dtype=F32):
            D[name] = nc.dram_tensor(name, list(shape), dtype, kind="ExternalInput")

        inp("x_pref", (PREF, DM))
        inp("x_own", (TILE + OWN, DM))
        inp("mask_rep", (128, PREF))
        inp("wxp_t", (DM, DI))
        inp("wz_t", (DM, DI))
        inp("convw_c", (128, G * 4))
        inp("convb_c", (128, G))
        inp("xproj_t", (DI, 96))
        inp("dtw_t", (DTR, DI))
        inp("dtb_c", (128, G))
        inp("alog_c", (128, G * DS))
        inp("d_c", (128, G))
        inp("outproj_t", (DI, DM))
        inp("mlpw1_t", (DM, 4 * DM))
        inp("mlpw2_t", (4 * DM, DM))
        inp("gate_w", (DM, 8))
        inp("gate_b_r", (128, 8))
        inp("moe_w1", (8, DM, 4 * DM))
        inp("moe_b1_c", (128, 8 * 32))
        inp("moe_w2", (8, 4 * DM, DM))
        inp("moe_b2_r", (1, 8 * DM))
        inp("const_v", (128, 8))
        inp("ident", (128, 128))
        inp("ones_r", (128, 128))
        inp("lower_r", (128, 64))

        D["o_out"] = nc.dram_tensor("o_out", [OWN, DM], F32, kind="ExternalOutput")
        if _DBG:
            for nm, shape, dt in [("d_dtp", (DI, 2048), F32), ("d_dto", (DI, OWN), F32),
                                  ("d_xc", (DI, OWN), F32), ("d_z", (DI, OWN), F32),
                                  ("d_xn", (OWN, DM), F32), ("d_B", (DS, NTOK), F32),
                                  ("d_C", (DS, OWN), F32), ("d_yg", (DI, OWN), BF16),
                                  ("d_g", (OWN, DM), F32), ("d_cc", (128, 128), F32)]:
                D[nm] = nc.dram_tensor(nm, list(shape), dt, kind="ExternalOutput")

        def scr(name, shape, dtype=F32):
            D[name] = nc.dram_tensor(name, list(shape), dtype, kind="Internal")

        scr("dtD", (DI, NTOK))
        scr("dtuD", (DI, NTOK))
        scr("BD", (DS, NTOK))
        scr("CD", (DS, OWN))
        scr("xcD", (DI, NTOK))
        scr("zD", (DI, OWN))
        scr("xnD", (OWN, DM))
        scr("xnTD", (DM, OWN), BF16)
        scr("ygD", (DI, OWN), BF16)
        scr("gD", (OWN, DM))
        scr("wxp_bfD", (DM, DI), BF16)
        scr("wz_bfD", (DM, DI), BF16)
        scr("xproj_bfD", (DI, 96), BF16)
        scr("dtw_bfD", (DTR, DI), BF16)
        scr("outproj_bfD", (DI, DM), BF16)
        scr("mlpw1_bfD", (DM, 4 * DM), BF16)
        scr("mlpw2_bfD", (4 * DM, DM), BF16)
        scr("gate_bfD", (DM, 8), BF16)
        scr("w1bfD", (8, DM, 4 * DM), BF16)
        scr("w2bfD", (8, 4 * DM, DM), BF16)

    def sb(self, es, name, shape, dtype=F32):
        return es.enter_context(self.nc.sbuf_tensor(name, list(shape), dtype))

    def psum(self, es, name, shape, dtype=F32):
        return es.enter_context(self.nc.psum_tensor(name, list(shape), dtype))

    def dsem(self, name):
        s = Sem(self.es, self.nc, name)
        self.all_dma_sems.append(s)
        return s

    def rot(self, es, name, n, shape, dtype=F32, ld=True, st=False):
        bufs = [self.sb(es, f"{name}{i}", shape, dtype) for i in range(n)]
        lds = [self.dsem(f"{name}{i}_ld") for i in range(n)] if ld else None
        sts = [self.dsem(f"{name}{i}_st") for i in range(n)] if st else None
        return Rot(bufs, lds, sts)

    def join_all(self):
        vals = [(s.h, s.v) for s in (self.sLd, self.sSt, self.sGp, self.sPe, self.sVe, self.sSe)]
        vals += [(s.h, s.v) for s in self.all_dma_sems if s.v]
        for eng in (self.SY, self.SC, self.VE, self.PE, self.GP):
            for h, v in vals:
                if v:
                    eng.wait_ge(h, v)

    def ld(self, out_ap, in_ap, sem=None):
        sem = sem or self.sLd
        self.SY.dma_start(out=out_ap, in_=in_ap).then_inc(sem.h, 16)
        sem.v += 16
        return (sem, sem.v)

    def st(self, out_ap, in_ap, sem=None):
        sem = sem or self.sSt
        self.SC.dma_start(out=out_ap, in_=in_ap).then_inc(sem.h, 16)
        sem.v += 16
        return (sem, sem.v)

    @staticmethod
    def wt(eng, *marks):
        for m in marks:
            if m:
                eng.wait_ge(m[0].h, m[1])

    def ph0(self):
        D, P = self.dram, self.persist
        GP, SE = self.GP, self.SE

        def cast(dst, src, nsplit=2):
            rows = src.shape[0]
            step = rows // nsplit
            for i in range(nsplit):
                GP.dma_start(out=dst.ap()[i * step:(i + 1) * step, :],
                             in_=src.ap()[i * step:(i + 1) * step, :]).then_inc(self.sGp.h, 16)
                self.sGp.v += 16

        cast(D["wxp_bfD"], D["wxp_t"])
        cast(D["wz_bfD"], D["wz_t"])
        cast(D["xproj_bfD"], D["xproj_t"], 1)
        cast(D["dtw_bfD"], D["dtw_t"], 1)
        cast(D["outproj_bfD"], D["outproj_t"])
        cast(D["mlpw1_bfD"], D["mlpw1_t"])
        cast(D["mlpw2_bfD"], D["mlpw2_t"])
        cast(D["gate_bfD"], D["gate_w"], 1)
        for e in range(8):
            GP.dma_start(out=D["w1bfD"].ap()[e], in_=D["moe_w1"].ap()[e]).then_inc(self.sGp.h, 16)
            self.sGp.v += 16
            GP.dma_start(out=D["w2bfD"].ap()[e], in_=D["moe_w2"].ap()[e]).then_inc(self.sGp.h, 16)
            self.sGp.v += 16
        self.gp_casts = self.sGp.v

        self.cA = self.dsem("cA")
        self.ld(P["cv"][:, :], D["const_v"].ap(), self.cA)
        for i, v in enumerate(CONSTS):
            self.nc.const_aps.register(F32, v, P["cv"][:, i:i + 1])
        for dst, src in [(P["convw"], "convw_c"), (P["convb"], "convb_c"),
                         (P["dtb"], "dtb_c"), (P["D"], "d_c"),
                         (P["ident"], "ident"), (P["ones"], "ones_r"),
                         (P["b1"], "moe_b1_c"), (P["A"], "alog_c")]:
            self.ld(dst[:, :], D[src].ap(), self.cA)
        SE.wait_ge(self.cA.h, self.cA.v)
        SE.activation(P["A"][:, :], P["A"][:, :], AF.Exp)
        self.sSe.tag(SE.activation(P["A"][:, :], P["A"][:, :], AF.Copy, scale=-1.0))

    def ph1(self):
        D, P = self.dram, self.persist
        es = ExitStack()
        SY, SC, VE, SE, PE = self.SY, self.SC, self.VE, self.SE, self.PE
        sLd, sSt, sVe, sSe, sPe = self.sLd, self.sSt, self.sVe, self.sSe, self.sPe

        xr_r = self.rot(es, "t_xr", 1, (128, 4 * DM))
        xn_sb = self.sb(es, "t_xn", (128, 4 * DM))
        xn_st = self.dsem("t_xn_st")
        st_sb = self.sb(es, "t_st", (128, 64))
        rn0T_r = Rot(self.sb(es, "t_rn0T", (128, 8 * TILE), BF16))
        xpg = [self.sb(es, f"t_xpg{g}", (128, TILE + 3)) for g in range(G)]
        xpg_free = [[] for _ in range(G)]
        xcg_r = self.rot(es, "t_xcg", 2, (128, TILE), ld=False, st=True)
        xcbf_r = Rot([self.sb(es, f"t_xcbf{i}", (128, TILE), BF16) for i in range(2)])
        zev_r = self.rot(es, "t_zev", 2, (128, TILE), ld=False, st=True)
        xdbl_r = self.rot(es, "t_xdbl", 2, (96, TILE), ld=False, st=True)
        xdblbf_r = Rot([self.sb(es, f"t_xdblbf{i}", (64, TILE), BF16) for i in range(2)])
        dtg_r = self.rot(es, "t_dtg", 2, (128, TILE), ld=False, st=True)
        dtug_r = self.rot(es, "t_dtug", 2, (128, TILE), ld=False, st=True)
        xcl_r = self.rot(es, "t_xcl", 2, (128, TILE))
        spm_sb = self.sb(es, "t_spm", (128, TILE))
        spl_sb = self.sb(es, "t_spl", (128, TILE))
        w_r = self.rot(es, "t_w", 2, (128, 8 * DM), BF16)
        mask_r = self.rot(es, "t_mask", 2, (128, TILE))
        gsc = self.sb(es, "t_gsc", (128, 128))
        lower_sb = self.sb(es, "t_lower", (128, 64))
        gateb_sb = self.sb(es, "t_gateb", (128, 8))
        gatew_sb = self.sb(es, "t_gatew", (128, 8 * 8), BF16)
        xnT_r = self.rot(es, "t_xnT", 1, (128, 8 * TILE), BF16, ld=False, st=True)
        xprojw_sb = self.sb(es, "t_xprojw", (128, G * 96), BF16)
        dtww_sb = self.sb(es, "t_dtww", (64, DI), BF16)

        pa_r = Rot([self.psum(es, f"t_pa{i}", (128, TILE)) for i in range(2)])
        px = self.psum(es, "t_px", (128, TILE))
        px_free = []
        pt_r = Rot([self.psum(es, f"t_pt{i}", (128, 128)) for i in range(2)])
        pg_r = Rot(self.psum(es, "t_pg", (128, 8)))

        SY.wait_ge(self.sGp.h, self.gp_casts)
        cB = self.dsem("cB")
        self.ld(xprojw_sb[:, :].rearrange("p (g r) -> p g r", r=96), D["xproj_bfD"].ap().rearrange("(g p) r -> p g r", p=128), cB)
        self.ld(dtww_sb[0:64, :], D["dtw_bfD"].ap(), cB)
        self.ld(gatew_sb[:, :].rearrange("p (kk e) -> p kk e", e=8), D["gate_bfD"].ap().rearrange("(kk p) e -> p kk e", p=128), cB)
        self.ld(gateb_sb[:, :], D["gate_b_r"].ap(), cB)
        self.ld(lower_sb[:, :], D["lower_r"].ap(), cB)
        ld_w0 = (cB, cB.v)
        cA2 = (self.cA, self.cA.v)

        self.wt(VE, ld_w0, cA2)
        self.wt(PE, ld_w0, cA2)
        for g in range(G):
            VE.memset(xpg[g][:, 0:3], 0.0)
        self.sVe.tag(VE.memset(st_sb[:, 0:1], 0.0))
        prev = {"se_rn0": 0, "ve_dt": 0, "st_xn": 0}

        xc_spill_marks = [None] * G

        def tile(src_ap, spill_col, own_ti, mask_col, do_scan, do_z):
            xr = xr_r.acquire(SY)
            lx = self.ld(xr[:, :].rearrange("p (a d) -> p a d", a=4), src_ap.rearrange("(a p) d -> p a d", p=128),
                         xr_r.ldsem())
            if mask_col is not None:
                mk = mask_r.acquire(SY)
                lm = self.ld(mk[:, :], D["mask_rep"].ap()[:, mask_col:mask_col + TILE], mask_r.ldsem())

            self.wt(VE, lx)
            VE.tensor_reduce(st_sb[:, 0:4], xr[:, :].rearrange("p (a d) -> p a d", a=4), AX.X, ALU.add)
            self.sVe.tag(VE.tensor_scalar(st_sb[:, 8:12], st_sb[:, 0:4], 1.0 / DM, None, ALU.mult))
            self.wt(SE, lx)
            if prev["st_xn"]:
                self.wt(SE, prev["st_xn"])
            se_sq = self.sSe.tag(SE.activation(xn_sb[:, :], xr[:, :], AF.Square))
            VE.wait_ge(sSe.h, se_sq)
            VE.tensor_reduce(st_sb[:, 4:8], xn_sb[:, :].rearrange("p (a d) -> p a d", a=4), AX.X, ALU.add)
            VE.tensor_scalar(st_sb[:, 12:16], st_sb[:, 4:8], 1.0 / DM, None, ALU.mult)
            VE.tensor_tensor(st_sb[:, 16:20], st_sb[:, 8:12], st_sb[:, 8:12], ALU.mult)
            ve_var = self.sVe.tag(VE.tensor_tensor(st_sb[:, 20:24], st_sb[:, 12:16], st_sb[:, 16:20], ALU.subtract))
            SE.wait_ge(sVe.h, ve_var)
            SE.activation(st_sb[:, 24:28], st_sb[:, 20:24], AF.Ln, bias=EPS_LN)
            se_r = self.sSe.tag(SE.activation(st_sb[:, 24:28], st_sb[:, 24:28], AF.Exp, scale=-0.5))
            VE.wait_ge(sSe.h, se_r)
            last = None
            for a in range(4):
                last = VE.tensor_scalar(xn_sb[:, a * DM:(a + 1) * DM], xr[:, a * DM:(a + 1) * DM],
                                        st_sb[:, 8 + a:9 + a], st_sb[:, 24 + a:25 + a],
                                        ALU.subtract, ALU.mult)
            se_xn = self.sVe.tag(last)
            SE.wait_ge(sVe.h, se_xn)
            se_sq2 = self.sSe.tag(SE.activation(xr[:, :], xn_sb[:, :], AF.Square))
            VE.wait_ge(sSe.h, se_sq2)
            VE.tensor_reduce(st_sb[:, 36:40], xr[:, :].rearrange("p (a d) -> p a d", a=4), AX.X, ALU.add)
            ve_q2 = self.sVe.tag(VE.tensor_scalar(st_sb[:, 40:44], st_sb[:, 36:40], 1.0 / DM, None, ALU.mult))
            SE.wait_ge(sVe.h, ve_q2)
            SE.activation(st_sb[:, 44:48], st_sb[:, 40:44], AF.Ln, bias=EPS_RMS)
            se_r2 = self.sSe.tag(SE.activation(st_sb[:, 44:48], st_sb[:, 44:48], AF.Exp, scale=-0.5))
            VE.wait_ge(sSe.h, se_r2)
            last = None
            for a in range(4):
                last = VE.tensor_scalar(xr[:, a * DM:(a + 1) * DM], xn_sb[:, a * DM:(a + 1) * DM],
                                        st_sb[:, 44 + a:45 + a], None, ALU.mult)
            se_rn0 = self.sVe.tag(last)
            prev["se_rn0"] = se_rn0

            if own_ti is not None:
                SC.wait_ge(sVe.h, se_xn)
                prev["st_xn"] = self.st(
                    D["xnD"].ap()[own_ti * TILE:(own_ti + 1) * TILE, :].rearrange("(a p) d -> p a d", p=128),
                    xn_sb[:, :].rearrange("p (a d) -> p a d", a=4), xn_st)

            rn0T = rn0T_r.acquire(SE)
            PE.wait_ge(sVe.h, se_rn0)
            for a in range(4):
                for dmt in range(8):
                    pt = pt_r.acquire(PE)
                    pe_tr = self.sPe.tag(PE.transpose(
                        pt[:, :], xr[:, a * DM + dmt * 128: a * DM + (dmt + 1) * 128], P["ident"][:, :]))
                    SE.wait_ge(sPe.h, pe_tr)
                    se_ev = self.sSe.tag(SE.activation(
                        rn0T[:, dmt * TILE + a * 128: dmt * TILE + (a + 1) * 128], pt[:, :], AF.Copy))
                    pt_r.release((self.sSe, se_ev))
            se_rn0T = self.sSe.v
            xr_r.release((self.sSe, se_rn0T))

            if own_ti is not None and _F_XNT:
                xnT = xnT_r.acquire(SE)
                for a in range(4):
                    for dmt in range(8):
                        pt = pt_r.acquire(PE)
                        PE.wait_ge(sVe.h, se_xn)
                        pe_tr = self.sPe.tag(PE.transpose(
                            pt[:, :], xn_sb[:, a * DM + dmt * 128: a * DM + (dmt + 1) * 128], P["ident"][:, :]))
                        SE.wait_ge(sPe.h, pe_tr)
                        se_ev = self.sSe.tag(SE.activation(
                            xnT[:, dmt * TILE + a * 128: dmt * TILE + (a + 1) * 128], pt[:, :], AF.Copy))
                        pt_r.release((self.sSe, se_ev))
                se_xnT = self.sSe.v
                SC.wait_ge(sSe.h, se_xnT)
                st_xnT = self.st(
                    D["xnTD"].ap()[:, own_ti * TILE:(own_ti + 1) * TILE].rearrange("(kk p) t -> p kk t", p=128),
                    xnT[:, :].rearrange("p (kk t) -> p kk t", kk=8), xnT_r.stsem())
                self.wt(PE, ld_w0)
                for a in (range(4) if _F_GATE else []):
                    pg = pg_r.acquire(PE)
                    last = None
                    for kk in range(8):
                        last = PE.matmul(pg[:, :],
                                         xnT[:, kk * TILE + a * 128: kk * TILE + (a + 1) * 128],
                                         gatew_sb[:, kk * 8:(kk + 1) * 8],
                                         start=(kk == 0), stop=(kk == 7))
                    pe_g = self.sPe.tag(last)
                    VE.wait_ge(sPe.h, pe_g)
                    VE.tensor_tensor(gsc[:, 0:8], pg[:, :], gateb_sb[:, :], ALU.add)
                    ve_pg = self.sVe.tag(VE.reduce_max(gsc[:, 16:17], gsc[:, 0:8], AX.X))
                    pg_r.release((self.sVe, ve_pg))
                    ve_m = self.sVe.tag(VE.tensor_scalar(gsc[:, 17:18], gsc[:, 16:17], -1.0, None, ALU.mult))
                    SE.wait_ge(sVe.h, ve_m)
                    se_e = self.sSe.tag(SE.activation(gsc[:, 8:16], gsc[:, 0:8], AF.Exp, bias=gsc[:, 17:18]))
                    VE.wait_ge(sSe.h, se_e)
                    VE.tensor_reduce(gsc[:, 19:20], gsc[:, 8:16], AX.X, ALU.add)
                    VE.reciprocal(gsc[:, 20:21], gsc[:, 19:20])
                    VE.tensor_scalar(gsc[:, 8:16], gsc[:, 8:16], gsc[:, 20:21], None, ALU.mult)
                    gw = gsc[:, 8:16]
                    in_j = gw.rearrange("p (o j) -> p o j", o=1).broadcast_to([128, 8, 8])
                    in_e = gw.rearrange("p (e o) -> p e o", o=1).broadcast_to([128, 8, 8])
                    pair = gsc[:, 32:96].rearrange("p (e j) -> p e j", e=8)
                    VE.tensor_tensor(pair, in_j, in_e, ALU.is_gt)
                    VE.tensor_reduce(gsc[:, 96:104], pair, AX.X, ALU.add)
                    VE.tensor_tensor(pair, in_j, in_e, ALU.is_equal)
                    VE.tensor_tensor(gsc[:, 32:96], gsc[:, 32:96], lower_sb[:, :], ALU.mult)
                    VE.tensor_reduce(gsc[:, 104:112], pair, AX.X, ALU.add)
                    VE.tensor_tensor(gsc[:, 96:104], gsc[:, 96:104], gsc[:, 104:112], ALU.add)
                    VE.tensor_scalar(gsc[:, 96:104], gsc[:, 96:104], 2.0, None, ALU.is_lt)
                    tm = own_ti * 4 + a
                    self.sVe.tag(VE.tensor_tensor(P["ccol"][:, tm * 8:(tm + 1) * 8], gw, gsc[:, 96:104], ALU.mult))
                xnT_r.release((self.sPe, self.sPe.v), st_xnT)

            specs = [("xp", D["wxp_bfD"], True)]
            if do_z and _F_Z:
                specs.append(("z", D["wz_bfD"], False))
            pe_xd = None
            for name, wsrc, is_xp in specs:
                for half in range(2):
                    w = w_r.acquire(SY)
                    lw = self.ld(w[:, :].rearrange("p (kk c) -> p kk c", kk=8),
                                 wsrc.ap()[:, half * DM:(half + 1) * DM]
                                 .rearrange("(kk p) c -> p kk c", p=128), w_r.ldsem())
                    self.wt(PE, lw)
                    PE.wait_ge(sSe.h, se_rn0T)
                    last_mm_w = None
                    for gh in range(8):
                        g = half * 8 + gh
                        psx = pa_r.acquire(PE)
                        last = None
                        for kk in range(8):
                            last = PE.matmul(psx[:, :],
                                             w[:, kk * DM + gh * 128: kk * DM + (gh + 1) * 128],
                                             rn0T[:, kk * TILE:(kk + 1) * TILE],
                                             start=(kk == 0), stop=(kk == 7))
                        pe_xp = self.sPe.tag(last)
                        last_mm_w = pe_xp
                        SE.wait_ge(sPe.h, pe_xp)
                        if is_xp:
                            for sem, val in xpg_free[g]:
                                SE.wait_ge(sem.h, val)
                            xpg_free[g] = []
                            se_ev = self.sSe.tag(SE.activation(xpg[g][:, 3:3 + TILE], psx[:, :], AF.Copy))
                            pa_r.release((self.sSe, se_ev))
                            VE.wait_ge(sSe.h, se_ev)
                            xc = xcg_r.acquire(VE)
                            VE.tensor_scalar(xc[:, :], xpg[g][:, 0:TILE], P["convw"][:, g * 4:g * 4 + 1], None, ALU.mult)
                            for t in range(1, 4):
                                VE.scalar_tensor_tensor(xc[:, :], xpg[g][:, t:t + TILE],
                                                        P["convw"][:, g * 4 + t:g * 4 + t + 1], xc[:, :],
                                                        ALU.mult, ALU.add)
                            VE.tensor_scalar(xc[:, :], xc[:, :], P["convb"][:, g:g + 1], None, ALU.add)
                            ve_tail = self.sVe.tag(VE.tensor_copy(xpg[g][:, 0:3], xpg[g][:, TILE:TILE + 3]))
                            SE.wait_ge(sVe.h, ve_tail)
                            se_th = self.sSe.tag(SE.activation(xpg[g][:, 3:3 + TILE], xc[:, :], AF.Tanh, scale=0.5))
                            VE.wait_ge(sSe.h, se_th)
                            VE.tensor_scalar(xpg[g][:, 3:3 + TILE], xpg[g][:, 3:3 + TILE], 0.5, 0.5, ALU.mult, ALU.add)
                            ve_xc = self.sVe.tag(VE.tensor_tensor(xc[:, :], xc[:, :], xpg[g][:, 3:3 + TILE], ALU.mult))
                            xpg_free[g] = [(self.sVe, ve_xc)]
                            if spill_col is not None and _F_XDBL:
                                xcbf = xcbf_r.acquire(VE)
                                ve_xcb = self.sVe.tag(VE.tensor_copy(xcbf[:, :], xc[:, :]))
                                SC.wait_ge(sVe.h, ve_xc)
                                st_xc = self.st(D["xcD"].ap()[g * 128:(g + 1) * 128, spill_col:spill_col + TILE],
                                                xc[:, :], xcg_r.stsem())
                                xc_spill_marks[g] = st_xc
                                xcg_r.release(st_xc)
                                PE.wait_ge(sVe.h, ve_xcb)
                                if g == 0:
                                    for sem, val in px_free:
                                        PE.wait_ge(sem.h, val)
                                    del px_free[:]
                                pe_xd = self.sPe.tag(PE.matmul(px[0:96, :], xprojw_sb[:, g * 96:(g + 1) * 96],
                                                               xcbf[:, :], start=(g == 0), stop=(g == G - 1),
                                                               skip_group_check=True))
                                xcbf_r.release((self.sPe, pe_xd))
                            else:
                                xcg_r.release((self.sVe, ve_xc))
                        else:
                            zev = zev_r.acquire(SE)
                            se_ev = self.sSe.tag(SE.activation(zev[:, :], psx[:, :], AF.Copy))
                            pa_r.release((self.sSe, se_ev))
                            SC.wait_ge(sSe.h, se_ev)
                            st_z = self.st(D["zD"].ap()[g * 128:(g + 1) * 128,
                                                        spill_col - PREF:spill_col - PREF + TILE],
                                           zev[:, :], zev_r.stsem())
                            zev_r.release(st_z)
                    w_r.release((self.sPe, last_mm_w))
            rn0T_r.release((self.sPe, self.sPe.v))

            if not do_scan or not _F_SCAN:
                return

            if not _F_XDBL:
                return
            xdbl = xdbl_r.acquire(SE)
            SE.wait_ge(sPe.h, pe_xd)
            se_xd = self.sSe.tag(SE.activation(xdbl[0:96, :], px[0:96, :], AF.Copy))
            px_free.append((self.sSe, se_xd))
            SC.wait_ge(sSe.h, se_xd)
            st_b = self.st(D["BD"].ap()[:, spill_col:spill_col + TILE], xdbl[64:80, :], xdbl_r.stsem())
            if own_ti is not None:
                st_b = self.st(D["CD"].ap()[:, own_ti * TILE:(own_ti + 1) * TILE], xdbl[80:96, :], xdbl_r.stsem())
            xdblbf = xdblbf_r.acquire(VE)
            VE.wait_ge(sSe.h, se_xd)
            ve_xdb = self.sVe.tag(VE.tensor_copy(xdblbf[0:64, :], xdbl[0:64, :]))
            xdbl_r.release((self.sVe, ve_xdb), st_b)

            for g in (range(G) if _F_DT else []):
                psx = pa_r.acquire(PE)
                PE.wait_ge(sVe.h, ve_xdb)
                pe_dt = self.sPe.tag(PE.matmul(psx[:, :], dtww_sb[0:64, g * 128:(g + 1) * 128],
                                               xdblbf[0:64, :], start=True, stop=True))
                if g == G - 1:
                    xdblbf_r.release((self.sPe, pe_dt))
                if _F_DTSUB < 2:
                    se_sp = self.sSe.v
                    ve_dt0 = self.sVe.v
                    pa_r.release((self.sPe, pe_dt))
                    continue
                VE.wait_ge(sPe.h, pe_dt)
                ve_max = self.sVe.tag(VE.tensor_scalar(spm_sb[:, :], psx[:, :], P["dtb"][:, g:g + 1],
                                                       0.0, ALU.add, ALU.max))
                SE.wait_ge(sVe.h, ve_max)
                SE.activation(spl_sb[:, :], psx[:, :], AF.Abs, bias=P["dtb"][:, g:g + 1])
                SE.activation(spl_sb[:, :], spl_sb[:, :], AF.Exp, scale=-1.0)
                se_sp = self.sSe.tag(SE.activation(spl_sb[:, :], spl_sb[:, :], AF.Ln, bias=1.0))
                if _F_DTSUB < 3:
                    pa_r.release((self.sSe, se_sp))
                    continue
                dtg = dtg_r.acquire(VE)
                VE.wait_ge(sSe.h, se_sp)
                VE.tensor_tensor(dtg[:, :], spm_sb[:, :], spl_sb[:, :], ALU.add)
                if mask_col is not None:
                    self.wt(VE, lm)
                    VE.tensor_tensor(dtg[:, :], dtg[:, :], mk[:, :], ALU.mult)
                if _F_DTSUB < 4:
                    prev["ve_dt"] = self.sVe.tag(VE.tensor_copy(dtg[:, 0:1], dtg[:, 0:1]))
                    pa_r.release((self.sSe, se_sp), (self.sVe, prev["ve_dt"]))
                    dtg_r.release((self.sVe, prev["ve_dt"]))
                    continue
                xcl = xcl_r.acquire(SY)
                self.wt(SY, xc_spill_marks[g])
                lxc = self.ld(xcl[:, :], D["xcD"].ap()[g * 128:(g + 1) * 128, spill_col:spill_col + TILE],
                              xcl_r.ldsem())
                dtug = dtug_r.acquire(VE)
                self.wt(VE, lxc)
                ve_dt = self.sVe.tag(VE.tensor_tensor(dtug[:, :], dtg[:, :], xcl[:, :], ALU.mult))
                prev["ve_dt"] = ve_dt
                pa_r.release((self.sSe, se_sp), (self.sVe, ve_dt))
                xcl_r.release((self.sVe, ve_dt))
                SC.wait_ge(sVe.h, ve_dt)
                st1 = self.st(D["dtD"].ap()[g * 128:(g + 1) * 128, spill_col:spill_col + TILE], dtg[:, :], dtg_r.stsem())
                st2 = self.st(D["dtuD"].ap()[g * 128:(g + 1) * 128, spill_col:spill_col + TILE], dtug[:, :], dtug_r.stsem())
                dtg_r.release(st1)
                dtug_r.release(st2)
            if mask_col is not None:
                mask_r.release((self.sVe, self.sVe.v))

        for ti in range(_NT_OWN):
            src = D["x_own"].ap()[ti * TILE:(ti + 1) * TILE, :]
            if ti == 0:
                tile(src, None, None, None, do_scan=False, do_z=False)
            else:
                tile(src, PREF + (ti - 1) * TILE, ti - 1, None, do_scan=True, do_z=True)
        for g in range(G):
            for sem, val in xpg_free[g]:
                if sem is not self.sVe:
                    VE.wait_ge(sem.h, val)
            xpg_free[g] = []
            VE.memset(xpg[g][:, 0:3], 0.0)
        self.sVe.tag(VE.memset(st_sb[:, 0:1], 0.0))
        for ti in range(_NT_PREF):
            src = D["x_pref"].ap()[ti * TILE:(ti + 1) * TILE, :]
            tile(src, ti * TILE, None, ti * TILE, do_scan=True, do_z=False)

        self.join_all()
        es.close()

    def ph2(self):
        D, P = self.dram, self.persist
        es = ExitStack()
        SY, SC, VE, SE, PE, GP = self.SY, self.SC, self.VE, self.SE, self.PE, self.GP
        sLd, sSt, sVe, sSe, sPe, sGp = self.sLd, self.sSt, self.sVe, self.sSe, self.sPe, self.sGp

        dtq_r = self.rot(es, "s_dtq", 2, (128, 4 * CHK))
        dtuq_r = self.rot(es, "s_dtuq", 2, (128, 4 * CHK))
        Bs_r = self.rot(es, "s_B", 1, (128, CHK))
        Cs_r = self.rot(es, "s_C", 1, (128, CHK))
        ab_r = Rot([self.sb(es, f"s_a{i}", (128, CHK)) for i in range(2)])
        bb = self.sb(es, "s_b", (128, CHK))
        hb = self.sb(es, "s_h", (128, CHK))
        tmp = self.sb(es, "s_tmp", (128, CHK))
        yacc = [self.sb(es, f"s_y{i}", (128, CHK)) for i in range(4)]
        carry = self.sb(es, "s_carry", (128, G * DS))
        ygbf_r = self.rot(es, "s_ygbf", 1, (128, CHK), BF16, ld=False, st=True)
        xcg2_r = self.rot(es, "s_xcg", 1, (128, CHK))
        zg2_r = self.rot(es, "s_zg", 1, (128, CHK))

        m_xnT_r = self.rot(es, "m_xnT", 2, (128, 8 * TILE), BF16)
        m_w1_r = self.rot(es, "m_w1", 1, (128, 8 * CHK), BF16)
        m_w2_r = self.rot(es, "m_w2", 2, (128, DM), BF16)
        m_hT = self.sb(es, "m_hT", (128, 32 * TILE), BF16)
        m_hT_free = []
        m_ctb_r = self.rot(es, "m_ctb", 2, (128, DM), ld=False, st=True)
        m_b2_r = self.rot(es, "m_b2", 2, (1, DM))
        m_ps1_r = Rot([self.psum(es, f"m_ps1{i}", (128, TILE)) for i in range(2)])
        m_ps2_r = Rot([self.psum(es, f"m_ps2{i}", (128, DM)) for i in range(2)])

        self.sVe.tag(VE.memset(carry[:, :], 0.0))
        gacc = {}

        def moe_unit(e, nq):
            xnT = m_xnT_r.acquire(SY)
            lxn = self.ld(xnT[:, :].rearrange("p (kk t) -> p kk t", kk=8),
                          D["xnTD"].ap()[:, nq * TILE:(nq + 1) * TILE]
                          .rearrange("(kk p) t -> p kk t", p=128), m_xnT_r.ldsem())
            b2 = m_b2_r.acquire(SY)
            lb2 = self.ld(b2[0:1, :], D["moe_b2_r"].ap()[0:1, e * DM:(e + 1) * DM], m_b2_r.ldsem())
            for sem, val in m_hT_free:
                SE.wait_ge(sem.h, val)
            del m_hT_free[:]
            for hq in range(4):
                w1 = m_w1_r.acquire(SY)
                lw1 = self.ld(w1[:, :].rearrange("p (kk h) -> p kk h", kk=8),
                              D["w1bfD"].ap()[e][:, hq * CHK:(hq + 1) * CHK]
                              .rearrange("(kk p) h -> p kk h", p=128), m_w1_r.ldsem())
                self.wt(PE, lw1, lxn)
                last_w1 = None
                for hm in range(8):
                    hmg = hq * 8 + hm
                    ps1 = m_ps1_r.acquire(PE)
                    last = None
                    for kk in range(8):
                        last = PE.matmul(ps1[:, :],
                                         w1[:, kk * CHK + hm * 128: kk * CHK + (hm + 1) * 128],
                                         xnT[:, kk * TILE:(kk + 1) * TILE],
                                         start=(kk == 0), stop=(kk == 7))
                    pe1 = self.sPe.tag(last)
                    last_w1 = pe1
                    SE.wait_ge(sPe.h, pe1)
                    se_ev = self.sSe.tag(SE.activation(
                        m_hT[:, hmg * TILE:(hmg + 1) * TILE],
                        ps1[:, :], AF.Relu, bias=P["b1"][:, e * 32 + hmg:e * 32 + hmg + 1]))
                    m_ps1_r.release((self.sSe, se_ev))
                m_w1_r.release((self.sPe, last_w1))
            se_h_done = self.sSe.v
            PE.wait_ge(sSe.h, se_h_done)
            self.wt(PE, lb2)
            for tl in range(4):
                tokg = nq * 4 + tl
                ps2 = m_ps2_r.acquire(PE)
                last = None
                for hk in range(32):
                    w2 = m_w2_r.acquire(SY)
                    lw2 = self.ld(w2[:, :], D["w2bfD"].ap()[e][hk * 128:(hk + 1) * 128, :], m_w2_r.ldsem())
                    self.wt(PE, lw2)
                    for half in range(2):
                        last = PE.matmul(ps2[:, half * TILE:(half + 1) * TILE],
                                         m_hT[:, hk * TILE + tl * 128:hk * TILE + (tl + 1) * 128],
                                         w2[:, half * TILE:(half + 1) * TILE],
                                         start=(hk == 0), stop=False)
                    pe_w2 = self.sPe.tag(last)
                    m_w2_r.release((self.sPe, pe_w2))
                for half in range(2):
                    last = PE.matmul(ps2[:, half * TILE:(half + 1) * TILE],
                                     P["ones"][0:1, 0:128],
                                     b2[0:1, half * TILE:(half + 1) * TILE],
                                     start=False, stop=(half == 1))
                pe2 = self.sPe.tag(last)
                ctb = m_ctb_r.acquire(VE)
                VE.wait_ge(sPe.h, pe2)
                ve_ct = self.sVe.tag(VE.tensor_scalar(
                    ctb[:, :], ps2[:, :], P["ccol"][:, tokg * 8 + e:tokg * 8 + e + 1], None, ALU.mult))
                m_ps2_r.release((self.sVe, ve_ct))
                if tokg in gacc:
                    self.wt(GP, gacc[tokg])
                GP.wait_ge(sVe.h, ve_ct)
                csem = m_ctb_r.stsem()
                GP.dma_start(out=D["gD"].ap()[tokg * 128:(tokg + 1) * 128, :], in_=ctb[:, :],
                             accum_op=(ALU.add if e > 0 else ALU.bypass)).then_inc(csem.h, 16)
                csem.v += 16
                gacc[tokg] = (csem, csem.v)
                m_ctb_r.release((csem, csem.v))
            m_hT_free.append((self.sPe, self.sPe.v))
            m_xnT_r.release((self.sPe, self.sPe.v))
            m_b2_r.release((self.sPe, self.sPe.v))

        moe_units = iter([(e, nq) for e in range(8) for nq in range(4)] if _STAGE >= 3 else [])

        n_chunks = NTOK // CHK
        for c in range(n_chunks):
            own_c = c >= PREF // CHK
            for q in range(4):
                dtq = dtq_r.acquire(SY)
                ldt = self.ld(dtq[:, :].rearrange("p (g t) -> p g t", g=4),
                              D["dtD"].ap()[q * TILE:(q + 1) * TILE, c * CHK:(c + 1) * CHK]
                              .rearrange("(g p) t -> p g t", p=128), dtq_r.ldsem())
                dtuq = dtuq_r.acquire(SY)
                ldtu = self.ld(dtuq[:, :].rearrange("p (g t) -> p g t", g=4),
                               D["dtuD"].ap()[q * TILE:(q + 1) * TILE, c * CHK:(c + 1) * CHK]
                               .rearrange("(g p) t -> p g t", p=128), dtuq_r.ldsem())
                ve_last_y = 0
                for si in range(DS):
                    Bsb = Bs_r.acquire(SY)
                    lB = self.ld(Bsb[:, :], D["BD"].ap()[si:si + 1, c * CHK:(c + 1) * CHK].squeeze(0)
                                 .partition_broadcast(128), Bs_r.ldsem())
                    if own_c:
                        Csb = Cs_r.acquire(SY)
                        lC = self.ld(Csb[:, :], D["CD"].ap()[si:si + 1, (c - 6) * CHK:(c - 5) * CHK].squeeze(0)
                                     .partition_broadcast(128), Cs_r.ldsem())
                    for gi in range(4):
                        gg = q * 4 + gi
                        col = gg * DS + si
                        ab = ab_r.acquire(SE)
                        self.wt(SE, ldt)
                        se_a = self.sSe.tag(SE.activation(ab[:, :], dtq[:, gi * CHK:(gi + 1) * CHK],
                                                          AF.Exp, scale=P["A"][:, col:col + 1]))
                        self.wt(VE, ldtu, lB)
                        VE.tensor_tensor(bb[:, :], dtuq[:, gi * CHK:(gi + 1) * CHK], Bsb[:, :], ALU.mult)
                        VE.wait_ge(sSe.h, se_a)
                        ve_sc = self.sVe.tag(VE.tensor_tensor_scan(
                            hb[:, :], ab[:, :], bb[:, :], carry[:, col:col + 1], ALU.mult, ALU.add))
                        ab_r.release((self.sVe, ve_sc))
                        VE.tensor_copy(carry[:, col:col + 1], hb[:, CHK - 1:CHK])
                        if own_c:
                            self.wt(VE, lC)
                            VE.tensor_tensor(tmp[:, :], hb[:, :], Csb[:, :], ALU.mult)
                            if si == 0:
                                ve_last_y = self.sVe.tag(VE.tensor_copy(yacc[gi][:, :], tmp[:, :]))
                            else:
                                ve_last_y = self.sVe.tag(VE.tensor_tensor(yacc[gi][:, :], yacc[gi][:, :],
                                                                          tmp[:, :], ALU.add))
                        else:
                            ve_last_y = self.sVe.v
                    Bs_r.release((self.sVe, self.sVe.v))
                    if own_c:
                        Cs_r.release((self.sVe, self.sVe.v))
                dtq_r.release((self.sSe, self.sSe.v))
                dtuq_r.release((self.sVe, self.sVe.v))
                if own_c:
                    for gi in range(4):
                        gg = q * 4 + gi
                        ocol = (c - 6) * CHK
                        xcg2 = xcg2_r.acquire(SY)
                        lxc = self.ld(xcg2[:, :], D["xcD"].ap()[gg * 128:(gg + 1) * 128,
                                                                PREF + ocol:PREF + ocol + CHK], xcg2_r.ldsem())
                        zg2 = zg2_r.acquire(SY)
                        lz = self.ld(zg2[:, :], D["zD"].ap()[gg * 128:(gg + 1) * 128, ocol:ocol + CHK], zg2_r.ldsem())
                        self.wt(VE, lxc, lz)
                        VE.scalar_tensor_tensor(yacc[gi][:, :], xcg2[:, :], P["D"][:, gg:gg + 1],
                                                yacc[gi][:, :], ALU.mult, ALU.add)
                        SE.wait_ge(sVe.h, self.sVe.v)
                        se_t = self.sSe.tag(SE.activation(tmp[:, :], zg2[:, :], AF.Tanh, scale=0.5))
                        VE.wait_ge(sSe.h, se_t)
                        VE.tensor_scalar(tmp[:, :], tmp[:, :], 0.5, 0.5, ALU.mult, ALU.add)
                        ve_z = self.sVe.tag(VE.tensor_tensor(yacc[gi][:, :], yacc[gi][:, :], zg2[:, :], ALU.mult))
                        xcg2_r.release((self.sVe, ve_z))
                        zg2_r.release((self.sVe, ve_z))
                        ygbf = ygbf_r.acquire(VE)
                        ve_yg = self.sVe.tag(VE.tensor_tensor(ygbf[:, :], yacc[gi][:, :], tmp[:, :], ALU.mult))
                        SC.wait_ge(sVe.h, ve_yg)
                        st_yg = self.st(D["ygD"].ap()[gg * 128:(gg + 1) * 128, ocol:ocol + CHK], ygbf[:, :],
                                        ygbf_r.stsem())
                        ygbf_r.release(st_yg)
                nxt = next(moe_units, None)
                if nxt is not None:
                    moe_unit(*nxt)
        for nxt in moe_units:
            moe_unit(*nxt)

        self.join_all()
        es.close()

    def ph3(self):
        D, P = self.dram, self.persist
        es = ExitStack()
        SY, SC, VE, SE, PE = self.SY, self.SC, self.VE, self.SE, self.PE
        sLd, sSt, sVe, sSe, sPe = self.sLd, self.sSt, self.sVe, self.sSe, self.sPe

        opw = self.sb(es, "f_opw", (128, G * DM), BF16)
        opw_ld = self.dsem("f_opw_ld")
        ygT_r = self.rot(es, "f_ygT", 1, (128, G * TILE), BF16)
        xn4_r = self.rot(es, "f_xn", 1, (128, 4 * DM))
        x1 = self.sb(es, "f_x1", (128, 4 * DM))
        stt = self.sb(es, "f_st", (128, 16))
        rn1_r = Rot(self.sb(es, "f_rn1", (128, DM)))
        rn1T_r = Rot(self.sb(es, "f_rn1T", (128, 8 * TILE), BF16))
        w1c_r = self.rot(es, "f_w1c", 2, (128, 8 * 128), BF16)
        hT = self.sb(es, "f_hT", (128, 32 * TILE), BF16)
        hT_free = []
        w2c_r = self.rot(es, "f_w2c", 2, (128, DM), BF16)
        qq = self.sb(es, "f_q", (128, DM))
        x2_r = Rot(self.sb(es, "f_x2", (128, DM)))
        gg_r = self.rot(es, "f_g", 2, (128, DM))
        xo_r = self.rot(es, "f_xo", 2, (128, DM))
        oo_r = self.rot(es, "f_oo", 2, (128, DM), ld=False, st=True)
        pA_r = Rot([self.psum(es, f"f_pA{i}", (128, DM)) for i in range(2)])
        pB_r = Rot([self.psum(es, f"f_pB{i}", (128, TILE)) for i in range(2)])
        pT_r = Rot(self.psum(es, "f_pT", (128, 128)))

        lop = self.ld(opw[:, :].rearrange("p (kk d) -> p kk d", kk=G),
                      D["outproj_bfD"].ap().rearrange("(kk p) d -> p kk d", p=128), opw_ld)

        for nt in range(4):
            ygT = ygT_r.acquire(SY)
            lyg = self.ld(ygT[:, :].rearrange("p (kk t) -> p kk t", kk=G),
                          D["ygD"].ap()[:, nt * TILE:(nt + 1) * TILE]
                          .rearrange("(kk p) t -> p kk t", p=128), ygT_r.ldsem())
            xn4 = xn4_r.acquire(SY)
            lxn = self.ld(xn4[:, :].rearrange("p (a d) -> p a d", a=4),
                          D["xnD"].ap()[nt * TILE:(nt + 1) * TILE, :]
                          .rearrange("(a p) d -> p a d", p=128), xn4_r.ldsem())
            self.wt(PE, lyg, lxn, lop)
            for a in range(4):
                pA = pA_r.acquire(PE)
                last = None
                for kk in range(G):
                    for half in range(2):
                        last = PE.matmul(pA[:, half * TILE:(half + 1) * TILE],
                                         ygT[:, kk * TILE + a * 128: kk * TILE + (a + 1) * 128],
                                         opw[:, kk * DM + half * TILE: kk * DM + (half + 1) * TILE],
                                         start=(kk == 0), stop=(kk == G - 1 and half == 1))
                pe_op = self.sPe.tag(last)
                VE.wait_ge(sPe.h, pe_op)
                self.wt(VE, lxn)
                ve_x1 = self.sVe.tag(VE.tensor_tensor(
                    x1[:, a * DM:(a + 1) * DM], xn4[:, a * DM:(a + 1) * DM], pA[:, :], ALU.add))
                pA_r.release((self.sVe, ve_x1))
            ve_x1_all = self.sVe.v
            ygT_r.release((self.sPe, self.sPe.v))
            xn4_r.release((self.sVe, ve_x1_all))

            rn1T = rn1T_r.acquire(SE)
            for a in range(4):
                rn1 = rn1_r.acquire(SE)
                SE.wait_ge(sVe.h, ve_x1_all)
                se_sq = self.sSe.tag(SE.activation(rn1[:, :], x1[:, a * DM:(a + 1) * DM], AF.Square))
                VE.wait_ge(sSe.h, se_sq)
                ve_q = self.sVe.tag(VE.tensor_reduce(
                    stt[:, 0:1], rn1[:, :].rearrange("p (o d) -> p o d", o=1), AX.X, ALU.add))
                SE.wait_ge(sVe.h, ve_q)
                SE.activation(stt[:, 1:2], stt[:, 0:1], AF.Ln, bias=EPS_RMS, scale=1.0 / DM)
                self.sSe.tag(SE.activation(stt[:, 1:2], stt[:, 1:2], AF.Exp, scale=-0.5))
                se_r1 = self.sSe.v
                VE.wait_ge(sSe.h, se_r1)
                se_rn = self.sVe.tag(VE.tensor_scalar(rn1[:, :], x1[:, a * DM:(a + 1) * DM],
                                                      stt[:, 1:2], None, ALU.mult))
                PE.wait_ge(sVe.h, se_rn)
                for dmt in range(8):
                    pT = pT_r.acquire(PE)
                    pe_tr = self.sPe.tag(PE.transpose(pT[:, :], rn1[:, dmt * 128:(dmt + 1) * 128],
                                                      P["ident"][:, :]))
                    SE.wait_ge(sPe.h, pe_tr)
                    se_ev = self.sSe.tag(SE.activation(
                        rn1T[:, dmt * TILE + a * 128: dmt * TILE + (a + 1) * 128], pT[:, :], AF.Copy))
                    pT_r.release((self.sSe, se_ev))
                rn1_r.release((self.sPe, self.sPe.v))
            se_rn1T = self.sSe.v

            for sem, val in hT_free:
                VE.wait_ge(sem.h, val)
            del hT_free[:]
            PE.wait_ge(sSe.h, se_rn1T)
            for hm in range(32):
                w1c = w1c_r.acquire(SY)
                lw1 = self.ld(w1c[:, :].rearrange("p (kk h) -> p kk h", kk=8),
                              D["mlpw1_bfD"].ap()[:, hm * 128:(hm + 1) * 128]
                              .rearrange("(kk p) h -> p kk h", p=128), w1c_r.ldsem())
                self.wt(PE, lw1)
                pB = pB_r.acquire(PE)
                last = None
                for kk in range(8):
                    last = PE.matmul(pB[:, :], w1c[:, kk * 128:(kk + 1) * 128],
                                     rn1T[:, kk * TILE:(kk + 1) * TILE],
                                     start=(kk == 0), stop=(kk == 7))
                pe_m1 = self.sPe.tag(last)
                w1c_r.release((self.sPe, pe_m1))
                SE.wait_ge(sPe.h, pe_m1)
                se_t = self.sSe.tag(SE.activation(qq[:, 0:TILE], pB[:, :], AF.Tanh, scale=0.5))
                VE.wait_ge(sSe.h, se_t)
                VE.tensor_scalar(qq[:, 0:TILE], qq[:, 0:TILE], 0.5, 0.5, ALU.mult, ALU.add)
                ve_h = self.sVe.tag(VE.tensor_tensor(
                    hT[:, hm * TILE:(hm + 1) * TILE], pB[:, :], qq[:, 0:TILE], ALU.mult))
                pB_r.release((self.sVe, ve_h))
            ve_hT = self.sVe.v
            rn1T_r.release((self.sPe, self.sPe.v))

            PE.wait_ge(sVe.h, ve_hT)
            for a in range(4):
                tokg = nt * 4 + a
                pA = pA_r.acquire(PE)
                last = None
                for hk in range(32):
                    w2c = w2c_r.acquire(SY)
                    lw2 = self.ld(w2c[:, :], D["mlpw2_bfD"].ap()[hk * 128:(hk + 1) * 128, :], w2c_r.ldsem())
                    self.wt(PE, lw2)
                    for half in range(2):
                        last = PE.matmul(pA[:, half * TILE:(half + 1) * TILE],
                                         hT[:, hk * TILE + a * 128: hk * TILE + (a + 1) * 128],
                                         w2c[:, half * TILE:(half + 1) * TILE],
                                         start=(hk == 0), stop=(hk == 31 and half == 1))
                    pe_w2 = self.sPe.tag(last)
                    w2c_r.release((self.sPe, pe_w2))
                pe_m2 = self.sPe.v
                gg = gg_r.acquire(SY)
                lg = self.ld(gg[:, :], D["gD"].ap()[tokg * 128:(tokg + 1) * 128, :], gg_r.ldsem())
                xo = xo_r.acquire(SY)
                lx = self.ld(xo[:, :], D["x_own"].ap()[TILE + tokg * 128: TILE + (tokg + 1) * 128, :], xo_r.ldsem())
                x2 = x2_r.acquire(VE)
                VE.wait_ge(sPe.h, pe_m2)
                ve_x2 = self.sVe.tag(VE.tensor_tensor(x2[:, :], x1[:, a * DM:(a + 1) * DM], pA[:, :], ALU.add))
                pA_r.release((self.sVe, ve_x2))
                oo = oo_r.acquire(SE)
                self.wt(SE, lg)
                se_t = self.sSe.tag(SE.activation(oo[:, :], gg[:, :], AF.Tanh, scale=0.5))
                VE.wait_ge(sSe.h, se_t)
                VE.tensor_scalar(oo[:, :], oo[:, :], 0.5, 0.5, ALU.mult, ALU.add)
                ve_g = self.sVe.tag(VE.tensor_tensor(oo[:, :], oo[:, :], gg[:, :], ALU.mult))
                gg_r.release((self.sVe, ve_g))
                VE.tensor_tensor(oo[:, :], oo[:, :], x2[:, :], ALU.mult)
                x2_r.release((self.sVe, self.sVe.v))
                self.wt(VE, lx)
                ve_o = self.sVe.tag(VE.tensor_tensor(oo[:, :], oo[:, :], xo[:, :], ALU.add))
                xo_r.release((self.sVe, ve_o))
                SC.wait_ge(sVe.h, ve_o)
                st_o = self.st(D["o_out"].ap()[tokg * 128:(tokg + 1) * 128, :], oo[:, :], oo_r.stsem())
                oo_r.release(st_o)

        if self.sSt.v:
            SC.wait_ge(self.sSt.h, self.sSt.v)
        for s in self.all_dma_sems:
            if s.v:
                SC.wait_ge(s.h, s.v)
        es.close()

    def finish(self):
        self.es.close()


def build_program():
    nc = bass.Bass("TRN2", target_bir_lowering=False)
    b = Builder(nc)
    b.ph0()
    if _STAGE >= 1:
        b.ph1()
    if _STAGE >= 2:
        b.ph2()
    if _STAGE >= 3:
        b.ph3()
    else:
        with nc.sbuf_tensor("z_oo", [128, DM], F32) as zoo:
            zv = b.sVe.tag(b.VE.memset(zoo[:, :], 0.0))
            b.SC.wait_ge(b.sVe.h, zv)
            for t in range(OWN // 128):
                b.st(b.dram["o_out"].ap()[t * 128:(t + 1) * 128, :], zoo[:, :])
            b.SC.wait_ge(b.sSt.h, b.sSt.v)
    if _DBG:
        D = b.dram
        b.join_all()
        pairs = [("d_dtp", D["dtD"].ap()[:, 0:2048]), ("d_dto", D["dtD"].ap()[:, PREF:]),
                 ("d_xc", D["xcD"].ap()[:, PREF:]), ("d_z", D["zD"].ap()),
                 ("d_xn", D["xnD"].ap()), ("d_B", D["BD"].ap()), ("d_C", D["CD"].ap())]
        if _STAGE >= 2:
            pairs.append(("d_yg", D["ygD"].ap()))
        if _STAGE >= 3:
            pairs.append(("d_g", D["gD"].ap()))
        for nm, srcap in pairs:
            b.st(D[nm].ap(), srcap)
        zv = b.sVe.tag(b.VE.tensor_copy(b.persist["ccol"][:, 0:1], b.persist["ccol"][:, 0:1]))
        b.SC.wait_ge(b.sVe.h, zv)
        b.st(D["d_cc"].ap()[:, 0:128], b.persist["ccol"][:, 0:128])
        b.SC.wait_ge(b.sSt.h, b.sSt.v)
    b.finish()
    return nc


def make_inmaps(inputs):
    x = np.ascontiguousarray(inputs["L0_attn_x"], dtype=np.float32)
    ipw = np.ascontiguousarray(inputs["L0_attn_in_proj_w"], dtype=np.float32)
    conv_w = np.ascontiguousarray(inputs["L0_attn_conv_w"], dtype=np.float32)
    conv_b = np.ascontiguousarray(inputs["L0_attn_conv_b"], dtype=np.float32)
    xpw = np.ascontiguousarray(inputs["L0_attn_x_proj_w"], dtype=np.float32)
    dtw = np.ascontiguousarray(inputs["L0_attn_dt_proj_w"], dtype=np.float32)
    dtb = np.ascontiguousarray(inputs["L0_attn_dt_proj_b"], dtype=np.float32)
    alog = np.ascontiguousarray(inputs["L0_attn_A_log"], dtype=np.float32)
    dd = np.ascontiguousarray(inputs["L0_attn_D"], dtype=np.float32)
    opw = np.ascontiguousarray(inputs["L0_attn_out_proj_w"], dtype=np.float32)
    mw1 = np.ascontiguousarray(inputs["L0_attn_mlp_w1"], dtype=np.float32)
    mw2 = np.ascontiguousarray(inputs["L0_attn_mlp_w2"], dtype=np.float32)
    gw = np.ascontiguousarray(inputs["L0_ffn_gate_w"], dtype=np.float32)
    gb = np.ascontiguousarray(inputs["L0_ffn_gate_b"], dtype=np.float32)
    w1 = np.ascontiguousarray(inputs["L0_ffn_w1"], dtype=np.float32)
    b1 = np.ascontiguousarray(inputs["L0_ffn_b1"], dtype=np.float32)
    w2 = np.ascontiguousarray(inputs["L0_ffn_w2"], dtype=np.float32)
    b2 = np.ascontiguousarray(inputs["L0_ffn_b2"], dtype=np.float32)

    low = np.zeros((64,), np.float32)
    for e in range(8):
        for j in range(8):
            low[e * 8 + j] = 1.0 if j < e else 0.0

    shared = dict(
        wxp_t=np.ascontiguousarray(ipw[:DI].T),
        wz_t=np.ascontiguousarray(ipw[DI:].T),
        convw_c=np.ascontiguousarray(conv_w.reshape(G, 128, 4).transpose(1, 0, 2).reshape(128, G * 4)),
        convb_c=np.ascontiguousarray(conv_b.reshape(G, 128).T),
        xproj_t=np.ascontiguousarray(xpw.T),
        dtw_t=np.ascontiguousarray(dtw.T),
        dtb_c=np.ascontiguousarray(dtb.reshape(G, 128).T),
        alog_c=np.ascontiguousarray(alog.reshape(G, 128, DS).transpose(1, 0, 2).reshape(128, G * DS)),
        d_c=np.ascontiguousarray(dd.reshape(G, 128).T),
        outproj_t=np.ascontiguousarray(opw.T),
        mlpw1_t=np.ascontiguousarray(mw1.T),
        mlpw2_t=np.ascontiguousarray(mw2.T),
        gate_w=gw,
        gate_b_r=np.ascontiguousarray(np.repeat(gb[None, :], 128, 0)),
        moe_w1=w1,
        moe_b1_c=np.ascontiguousarray(b1.reshape(8, 32, 128).transpose(2, 0, 1).reshape(128, 8 * 32)),
        moe_w2=w2,
        moe_b2_r=np.ascontiguousarray(b2.reshape(1, 8 * DM)),
        const_v=np.ascontiguousarray(np.repeat(np.array([CONSTS], np.float32), 128, 0)),
        ident=np.eye(128, dtype=np.float32),
        ones_r=np.ones((128, 128), np.float32),
        lower_r=np.ascontiguousarray(np.repeat(low[None, :], 128, 0)),
    )

    in_maps = []
    for c in range(8):
        b_, k = c // 4, c % 4
        Pk = k * OWN
        ext = np.zeros((TILE, DM), np.float32) if k == 0 else x[b_, Pk - TILE:Pk]
        m = dict(shared)
        m["x_pref"] = np.ascontiguousarray(x[b_, :PREF])
        m["x_own"] = np.ascontiguousarray(np.concatenate([ext, x[b_, Pk:Pk + OWN]], axis=0))
        m["mask_rep"] = np.ascontiguousarray(
            np.repeat((np.arange(PREF) < Pk).astype(np.float32)[None, :], 128, 0))
        in_maps.append(m)
    return in_maps


_NC_CACHE = [None]


def kernel(**inputs):
    in_maps = make_inmaps(inputs)
    if _NC_CACHE[0] is None:
        _NC_CACHE[0] = build_program()
    nc = _NC_CACHE[0]
    res = run_bass_kernel_spmd(nc, in_maps, core_ids=list(range(8)))
    out = np.empty((2, L, DM), np.float32)
    for c in range(8):
        b_, k = c // 4, c % 4
        out[b_, k * OWN:(k + 1) * OWN] = np.asarray(res.results[c]["o_out"])
    return out


# version 02
# speedup vs baseline: 1.3971x; 1.3971x over previous
import os
import numpy as np
from contextlib import ExitStack

import concourse.bass as bass
import concourse.mybir as mybir
from concourse.bass_utils import run_bass_kernel_spmd

F32 = mybir.dt.float32
BF16 = mybir.dt.bfloat16
AF = mybir.ActivationFunctionType
ALU = mybir.AluOpType
AX = mybir.AxisListType

L = 8192
DM = 1024
DI = 2048
DS = 16
DTR = 64
OWN = 2048
PREF = 6144
TILE = 512
G = 16
CHK = 1024
NTOK = PREF + OWN
EPS_LN = 1e-5
EPS_RMS = 1.1920929e-07

_STAGE = int(os.environ.get("MK_STAGE", "4"))
_DBG = os.environ.get("MK_DBG", "0") == "1"
_NT_OWN = int(os.environ.get("MK_NT_OWN", "5"))
_F_XNT = os.environ.get("MK_F_XNT", "1") == "1"
_F_GATE = os.environ.get("MK_F_GATE", "1") == "1"
_F_Z = os.environ.get("MK_F_Z", "1") == "1"
_F_SCAN = os.environ.get("MK_F_SCAN", "1") == "1"
_F_XDBL = os.environ.get("MK_F_XDBL", "1") == "1"
_F_DT = os.environ.get("MK_F_DT", "1") == "1"
_F_DTSUB = int(os.environ.get("MK_F_DTSUB", "4"))
_NT_PREF = int(os.environ.get("MK_NT_PREF", "12"))
CONSTS = [EPS_LN, EPS_RMS, 1.0, 0.5, -0.5, -1.0, 0.0, 2.0]


class Sem:
    def __init__(self, es, nc, name, auto=False):
        self.h = es.enter_context(nc.semaphore(name))
        self.v = 0
        self.auto = auto

    def tag(self, instr, n=1):
        if self.auto:
            return self.v
        instr.then_inc(self.h, n)
        self.v += n
        return self.v


class SelfSync:

    _PASSTHRU = {"wait_ge", "dma_start"}

    def __init__(self, eng, sem):
        self._eng = eng
        self._sem = sem

    def wait_ge(self, h, v):
        return self._eng.wait_ge(h, v)

    def dma_start(self, *a, **k):
        return self._eng.dma_start(*a, **k)

    def __getattr__(self, name):
        f = getattr(self._eng, name)
        if not callable(f):
            return f
        sem = self._sem
        eng = self._eng

        def wrapped(*a, **k):
            if sem.v:
                eng.wait_ge(sem.h, sem.v)
            instr = f(*a, **k)
            if hasattr(instr, "then_inc"):
                instr.then_inc(sem.h, 1)
                sem.v += 1
            return instr

        return wrapped


class Rot:

    def __init__(self, bufs, ldsems=None, stsems=None):
        self.bufs = bufs if isinstance(bufs, list) else [bufs]
        self.free = [[] for _ in self.bufs]
        self.i = -1
        self.ldsems = ldsems
        self.stsems = stsems

    def acquire(self, writer):
        self.i += 1
        j = self.i % len(self.bufs)
        for sem, val in self.free[j]:
            if val:
                writer.wait_ge(sem.h, val)
        self.free[j] = []
        return self.bufs[j]

    def ldsem(self):
        return self.ldsems[self.i % len(self.bufs)]

    def stsem(self):
        return self.stsems[self.i % len(self.bufs)]

    def cur(self):
        return self.bufs[self.i % len(self.bufs)]

    def release(self, *marks):
        j = self.i % len(self.bufs)
        self.free[j].extend(marks)


class Builder:
    def __init__(self, nc):
        self.nc = nc
        self.es = ExitStack()
        es = self.es
        self.SY = nc.sync
        self.SC = nc.scalar
        self.VE = nc.vector
        self.SE = nc.scalar
        self.PE = nc.tensor
        self.GP = nc.gpsimd

        self.sLd = Sem(es, nc, "sLd")
        self.sSt = Sem(es, nc, "sSt")
        self.sGp = Sem(es, nc, "sGp")
        self.sPe = Sem(es, nc, "sPe")
        self.sVe = Sem(es, nc, "sVe", auto=True)
        self.sSe = Sem(es, nc, "sSe", auto=True)
        self.VE = SelfSync(self.VE, self.sVe)
        self.SE = SelfSync(self.SE, self.sSe)

        self.all_dma_sems = []
        self.dram = {}
        self.decl_dram()
        self.persist = {}
        P = self.persist
        P["A"] = self.sb(es, "A_sb", (128, G * DS))
        P["convw"] = self.sb(es, "convw_sb", (128, G * 4))
        P["convb"] = self.sb(es, "convb_sb", (128, G))
        P["dtb"] = self.sb(es, "dtb_sb", (128, G))
        P["D"] = self.sb(es, "d_sb", (128, G))
        P["ident"] = self.sb(es, "ident_sb", (128, 128))
        P["ones"] = self.sb(es, "ones_sb", (128, 128))
        P["ccol"] = self.sb(es, "ccol_sb", (128, 16 * 8))
        P["b1"] = self.sb(es, "b1_sb", (128, 8 * 32))
        P["cv"] = self.sb(es, "cv_sb", (128, 8))

    def decl_dram(self):
        nc = self.nc
        D = self.dram

        def inp(name, shape, dtype=F32):
            D[name] = nc.dram_tensor(name, list(shape), dtype, kind="ExternalInput")

        inp("x_pref", (PREF, DM))
        inp("x_own", (TILE + OWN, DM))
        inp("mask_rep", (128, PREF))
        inp("wxp_t", (DM, DI))
        inp("wz_t", (DM, DI))
        inp("convw_c", (128, G * 4))
        inp("convb_c", (128, G))
        inp("xproj_t", (DI, 96))
        inp("dtw_t", (DTR, DI))
        inp("dtb_c", (128, G))
        inp("alog_c", (128, G * DS))
        inp("d_c", (128, G))
        inp("outproj_t", (DI, DM))
        inp("mlpw1_t", (DM, 4 * DM))
        inp("mlpw2_t", (4 * DM, DM))
        inp("gate_w", (DM, 8))
        inp("gate_b_r", (128, 8))
        inp("moe_w1", (8, DM, 4 * DM))
        inp("moe_b1_c", (128, 8 * 32))
        inp("moe_w2", (8, 4 * DM, DM))
        inp("moe_b2_r", (1, 8 * DM))
        inp("const_v", (128, 8))
        inp("ident", (128, 128))
        inp("ones_r", (128, 128))
        inp("lower_r", (128, 64))

        D["o_out"] = nc.dram_tensor("o_out", [OWN, DM], F32, kind="ExternalOutput")
        if _DBG:
            for nm, shape, dt in [("d_dtp", (DI, 2048), F32), ("d_dto", (DI, OWN), F32),
                                  ("d_xc", (DI, OWN), F32), ("d_z", (DI, OWN), F32),
                                  ("d_xn", (OWN, DM), F32), ("d_B", (DS, NTOK), F32),
                                  ("d_C", (DS, OWN), F32), ("d_yg", (DI, OWN), BF16),
                                  ("d_g", (OWN, DM), F32), ("d_cc", (128, 128), F32)]:
                D[nm] = nc.dram_tensor(nm, list(shape), dt, kind="ExternalOutput")

        def scr(name, shape, dtype=F32):
            D[name] = nc.dram_tensor(name, list(shape), dtype, kind="Internal")

        scr("dtD", (DI, NTOK))
        scr("dtuD", (DI, NTOK))
        scr("BD", (DS, NTOK))
        scr("CD", (DS, OWN))
        scr("xcD", (DI, NTOK))
        scr("zD", (DI, OWN))
        scr("xnD", (OWN, DM))
        scr("xnTD", (DM, OWN), BF16)
        scr("ygD", (DI, OWN), BF16)
        scr("gD", (OWN, DM))
        scr("wxp_bfD", (DM, DI), BF16)
        scr("wz_bfD", (DM, DI), BF16)
        scr("xproj_bfD", (DI, 96), BF16)
        scr("dtw_bfD", (DTR, DI), BF16)
        scr("outproj_bfD", (DI, DM), BF16)
        scr("mlpw1_bfD", (DM, 4 * DM), BF16)
        scr("mlpw2_bfD", (4 * DM, DM), BF16)
        scr("gate_bfD", (DM, 8), BF16)
        scr("w1bfD", (8, DM, 4 * DM), BF16)
        scr("w2bfD", (8, 4 * DM, DM), BF16)

    def sb(self, es, name, shape, dtype=F32):
        return es.enter_context(self.nc.sbuf_tensor(name, list(shape), dtype))

    def psum(self, es, name, shape, dtype=F32):
        return es.enter_context(self.nc.psum_tensor(name, list(shape), dtype))

    def dsem(self, name):
        s = Sem(self.es, self.nc, name)
        self.all_dma_sems.append(s)
        return s

    def rot(self, es, name, n, shape, dtype=F32, ld=True, st=False):
        bufs = [self.sb(es, f"{name}{i}", shape, dtype) for i in range(n)]
        lds = [self.dsem(f"{name}{i}_ld") for i in range(n)] if ld else None
        sts = [self.dsem(f"{name}{i}_st") for i in range(n)] if st else None
        return Rot(bufs, lds, sts)

    def join_all(self):
        vals = [(s.h, s.v) for s in (self.sLd, self.sSt, self.sGp, self.sPe, self.sVe, self.sSe)]
        vals += [(s.h, s.v) for s in self.all_dma_sems if s.v]
        for eng in (self.SY, self.SC, self.VE, self.PE, self.GP):
            for h, v in vals:
                if v:
                    eng.wait_ge(h, v)

    def ld(self, out_ap, in_ap, sem=None):
        sem = sem or self.sLd
        self.SY.dma_start(out=out_ap, in_=in_ap).then_inc(sem.h, 16)
        sem.v += 16
        return (sem, sem.v)

    def st(self, out_ap, in_ap, sem=None):
        sem = sem or self.sSt
        self.SC.dma_start(out=out_ap, in_=in_ap).then_inc(sem.h, 16)
        sem.v += 16
        return (sem, sem.v)

    @staticmethod
    def wt(eng, *marks):
        for m in marks:
            if m:
                eng.wait_ge(m[0].h, m[1])

    def ph0(self):
        D, P = self.dram, self.persist
        GP, SE = self.GP, self.SE

        def cast(dst, src, nsplit=2):
            rows = src.shape[0]
            step = rows // nsplit
            for i in range(nsplit):
                GP.dma_start(out=dst.ap()[i * step:(i + 1) * step, :],
                             in_=src.ap()[i * step:(i + 1) * step, :]).then_inc(self.sGp.h, 16)
                self.sGp.v += 16

        cast(D["wxp_bfD"], D["wxp_t"])
        cast(D["wz_bfD"], D["wz_t"])
        cast(D["xproj_bfD"], D["xproj_t"], 1)
        cast(D["dtw_bfD"], D["dtw_t"], 1)
        cast(D["outproj_bfD"], D["outproj_t"])
        cast(D["mlpw1_bfD"], D["mlpw1_t"])
        cast(D["mlpw2_bfD"], D["mlpw2_t"])
        cast(D["gate_bfD"], D["gate_w"], 1)
        for e in range(8):
            GP.dma_start(out=D["w1bfD"].ap()[e], in_=D["moe_w1"].ap()[e]).then_inc(self.sGp.h, 16)
            self.sGp.v += 16
            GP.dma_start(out=D["w2bfD"].ap()[e], in_=D["moe_w2"].ap()[e]).then_inc(self.sGp.h, 16)
            self.sGp.v += 16
        self.gp_casts = self.sGp.v

        self.cA = self.dsem("cA")
        self.ld(P["cv"][:, :], D["const_v"].ap(), self.cA)
        for i, v in enumerate(CONSTS):
            self.nc.const_aps.register(F32, v, P["cv"][:, i:i + 1])
        for dst, src in [(P["convw"], "convw_c"), (P["convb"], "convb_c"),
                         (P["dtb"], "dtb_c"), (P["D"], "d_c"),
                         (P["ident"], "ident"), (P["ones"], "ones_r"),
                         (P["b1"], "moe_b1_c"), (P["A"], "alog_c")]:
            self.ld(dst[:, :], D[src].ap(), self.cA)
        SE.wait_ge(self.cA.h, self.cA.v)
        SE.activation(P["A"][:, :], P["A"][:, :], AF.Exp)
        self.sSe.tag(SE.activation(P["A"][:, :], P["A"][:, :], AF.Copy, scale=-1.0))

    def ph1(self):
        D, P = self.dram, self.persist
        es = ExitStack()
        SY, SC, VE, SE, PE = self.SY, self.SC, self.VE, self.SE, self.PE
        sLd, sSt, sVe, sSe, sPe = self.sLd, self.sSt, self.sVe, self.sSe, self.sPe

        xr_r = self.rot(es, "t_xr", 1, (128, 4 * DM))
        xn_sb = self.sb(es, "t_xn", (128, 4 * DM))
        xn_st = self.dsem("t_xn_st")
        st_sb = self.sb(es, "t_st", (128, 64))
        rn0T_r = Rot(self.sb(es, "t_rn0T", (128, 8 * TILE), BF16))
        xpg = [self.sb(es, f"t_xpg{g}", (128, TILE + 3)) for g in range(G)]
        xpg_free = [[] for _ in range(G)]
        xcg_r = self.rot(es, "t_xcg", 2, (128, TILE), ld=False, st=True)
        xcbf_r = Rot([self.sb(es, f"t_xcbf{i}", (128, TILE), BF16) for i in range(2)])
        zev_r = self.rot(es, "t_zev", 2, (128, TILE), ld=False, st=True)
        xdbl_r = self.rot(es, "t_xdbl", 2, (96, TILE), ld=False, st=True)
        xdblbf_r = Rot([self.sb(es, f"t_xdblbf{i}", (64, TILE), BF16) for i in range(2)])
        dtg_r = self.rot(es, "t_dtg", 2, (128, TILE), ld=False, st=True)
        dtug_r = self.rot(es, "t_dtug", 2, (128, TILE), ld=False, st=True)
        xcl_r = self.rot(es, "t_xcl", 2, (128, TILE))
        spm_sb = self.sb(es, "t_spm", (128, TILE))
        spl_sb = self.sb(es, "t_spl", (128, TILE))
        w_r = self.rot(es, "t_w", 2, (128, 8 * DM), BF16)
        mask_r = self.rot(es, "t_mask", 2, (128, TILE))
        gsc = self.sb(es, "t_gsc", (128, 128))
        lower_sb = self.sb(es, "t_lower", (128, 64))
        gateb_sb = self.sb(es, "t_gateb", (128, 8))
        gatew_sb = self.sb(es, "t_gatew", (128, 8 * 8), BF16)
        xnT_r = self.rot(es, "t_xnT", 1, (128, 8 * TILE), BF16, ld=False, st=True)
        xprojw_sb = self.sb(es, "t_xprojw", (128, G * 96), BF16)
        dtww_sb = self.sb(es, "t_dtww", (64, DI), BF16)

        pa_r = Rot([self.psum(es, f"t_pa{i}", (128, TILE)) for i in range(2)])
        px = self.psum(es, "t_px", (128, TILE))
        px_free = []
        pt_r = Rot([self.psum(es, f"t_pt{i}", (128, 128)) for i in range(2)])
        pg_r = Rot(self.psum(es, "t_pg", (128, 8)))

        SY.wait_ge(self.sGp.h, self.gp_casts)
        cB = self.dsem("cB")
        self.ld(xprojw_sb[:, :].rearrange("p (g r) -> p g r", r=96), D["xproj_bfD"].ap().rearrange("(g p) r -> p g r", p=128), cB)
        self.ld(dtww_sb[0:64, :], D["dtw_bfD"].ap(), cB)
        self.ld(gatew_sb[:, :].rearrange("p (kk e) -> p kk e", e=8), D["gate_bfD"].ap().rearrange("(kk p) e -> p kk e", p=128), cB)
        self.ld(gateb_sb[:, :], D["gate_b_r"].ap(), cB)
        self.ld(lower_sb[:, :], D["lower_r"].ap(), cB)
        ld_w0 = (cB, cB.v)
        cA2 = (self.cA, self.cA.v)

        self.wt(VE, ld_w0, cA2)
        self.wt(PE, ld_w0, cA2)
        for g in range(G):
            VE.memset(xpg[g][:, 0:3], 0.0)
        self.sVe.tag(VE.memset(st_sb[:, 0:1], 0.0))
        prev = {"se_rn0": 0, "ve_dt": 0, "st_xn": 0}

        xc_spill_marks = [None] * G

        def tile(src_ap, spill_col, own_ti, mask_col, do_scan, do_z):
            xr = xr_r.acquire(SY)
            lx = self.ld(xr[:, :].rearrange("p (a d) -> p a d", a=4), src_ap.rearrange("(a p) d -> p a d", p=128),
                         xr_r.ldsem())
            if mask_col is not None:
                mk = mask_r.acquire(SY)
                lm = self.ld(mk[:, :], D["mask_rep"].ap()[:, mask_col:mask_col + TILE], mask_r.ldsem())

            self.wt(VE, lx)
            VE.tensor_reduce(st_sb[:, 0:4], xr[:, :].rearrange("p (a d) -> p a d", a=4), AX.X, ALU.add)
            self.sVe.tag(VE.tensor_scalar(st_sb[:, 8:12], st_sb[:, 0:4], 1.0 / DM, None, ALU.mult))
            self.wt(SE, lx)
            if prev["st_xn"]:
                self.wt(SE, prev["st_xn"])
            se_sq = self.sSe.tag(SE.activation(xn_sb[:, :], xr[:, :], AF.Square))
            VE.wait_ge(sSe.h, se_sq)
            VE.tensor_reduce(st_sb[:, 4:8], xn_sb[:, :].rearrange("p (a d) -> p a d", a=4), AX.X, ALU.add)
            VE.tensor_scalar(st_sb[:, 12:16], st_sb[:, 4:8], 1.0 / DM, None, ALU.mult)
            VE.tensor_tensor(st_sb[:, 16:20], st_sb[:, 8:12], st_sb[:, 8:12], ALU.mult)
            ve_var = self.sVe.tag(VE.tensor_tensor(st_sb[:, 20:24], st_sb[:, 12:16], st_sb[:, 16:20], ALU.subtract))
            SE.wait_ge(sVe.h, ve_var)
            SE.activation(st_sb[:, 24:28], st_sb[:, 20:24], AF.Ln, bias=EPS_LN)
            se_r = self.sSe.tag(SE.activation(st_sb[:, 24:28], st_sb[:, 24:28], AF.Exp, scale=-0.5))
            VE.wait_ge(sSe.h, se_r)
            last = None
            for a in range(4):
                last = VE.tensor_scalar(xn_sb[:, a * DM:(a + 1) * DM], xr[:, a * DM:(a + 1) * DM],
                                        st_sb[:, 8 + a:9 + a], st_sb[:, 24 + a:25 + a],
                                        ALU.subtract, ALU.mult)
            se_xn = self.sVe.tag(last)
            SE.wait_ge(sVe.h, se_xn)
            se_sq2 = self.sSe.tag(SE.activation(xr[:, :], xn_sb[:, :], AF.Square))
            VE.wait_ge(sSe.h, se_sq2)
            VE.tensor_reduce(st_sb[:, 36:40], xr[:, :].rearrange("p (a d) -> p a d", a=4), AX.X, ALU.add)
            ve_q2 = self.sVe.tag(VE.tensor_scalar(st_sb[:, 40:44], st_sb[:, 36:40], 1.0 / DM, None, ALU.mult))
            SE.wait_ge(sVe.h, ve_q2)
            SE.activation(st_sb[:, 44:48], st_sb[:, 40:44], AF.Ln, bias=EPS_RMS)
            se_r2 = self.sSe.tag(SE.activation(st_sb[:, 44:48], st_sb[:, 44:48], AF.Exp, scale=-0.5))
            VE.wait_ge(sSe.h, se_r2)
            last = None
            for a in range(4):
                last = VE.tensor_scalar(xr[:, a * DM:(a + 1) * DM], xn_sb[:, a * DM:(a + 1) * DM],
                                        st_sb[:, 44 + a:45 + a], None, ALU.mult)
            se_rn0 = self.sVe.tag(last)
            prev["se_rn0"] = se_rn0

            if own_ti is not None:
                SC.wait_ge(sVe.h, se_xn)
                prev["st_xn"] = self.st(
                    D["xnD"].ap()[own_ti * TILE:(own_ti + 1) * TILE, :].rearrange("(a p) d -> p a d", p=128),
                    xn_sb[:, :].rearrange("p (a d) -> p a d", a=4), xn_st)

            rn0T = rn0T_r.acquire(SE)
            PE.wait_ge(sVe.h, se_rn0)
            for a in range(4):
                for dmt in range(8):
                    pt = pt_r.acquire(PE)
                    pe_tr = self.sPe.tag(PE.transpose(
                        pt[:, :], xr[:, a * DM + dmt * 128: a * DM + (dmt + 1) * 128], P["ident"][:, :]))
                    SE.wait_ge(sPe.h, pe_tr)
                    se_ev = self.sSe.tag(SE.activation(
                        rn0T[:, dmt * TILE + a * 128: dmt * TILE + (a + 1) * 128], pt[:, :], AF.Copy))
                    pt_r.release((self.sSe, se_ev))
            se_rn0T = self.sSe.v
            xr_r.release((self.sSe, se_rn0T))

            if own_ti is not None and _F_XNT:
                xnT = xnT_r.acquire(SE)
                for a in range(4):
                    for dmt in range(8):
                        pt = pt_r.acquire(PE)
                        PE.wait_ge(sVe.h, se_xn)
                        pe_tr = self.sPe.tag(PE.transpose(
                            pt[:, :], xn_sb[:, a * DM + dmt * 128: a * DM + (dmt + 1) * 128], P["ident"][:, :]))
                        SE.wait_ge(sPe.h, pe_tr)
                        se_ev = self.sSe.tag(SE.activation(
                            xnT[:, dmt * TILE + a * 128: dmt * TILE + (a + 1) * 128], pt[:, :], AF.Copy))
                        pt_r.release((self.sSe, se_ev))
                se_xnT = self.sSe.v
                SC.wait_ge(sSe.h, se_xnT)
                st_xnT = self.st(
                    D["xnTD"].ap()[:, own_ti * TILE:(own_ti + 1) * TILE].rearrange("(kk p) t -> p kk t", p=128),
                    xnT[:, :].rearrange("p (kk t) -> p kk t", kk=8), xnT_r.stsem())
                self.wt(PE, ld_w0)
                for a in (range(4) if _F_GATE else []):
                    pg = pg_r.acquire(PE)
                    last = None
                    for kk in range(8):
                        last = PE.matmul(pg[:, :],
                                         xnT[:, kk * TILE + a * 128: kk * TILE + (a + 1) * 128],
                                         gatew_sb[:, kk * 8:(kk + 1) * 8],
                                         start=(kk == 0), stop=(kk == 7))
                    pe_g = self.sPe.tag(last)
                    VE.wait_ge(sPe.h, pe_g)
                    VE.tensor_tensor(gsc[:, 0:8], pg[:, :], gateb_sb[:, :], ALU.add)
                    ve_pg = self.sVe.tag(VE.reduce_max(gsc[:, 16:17], gsc[:, 0:8], AX.X))
                    pg_r.release((self.sVe, ve_pg))
                    ve_m = self.sVe.tag(VE.tensor_scalar(gsc[:, 17:18], gsc[:, 16:17], -1.0, None, ALU.mult))
                    SE.wait_ge(sVe.h, ve_m)
                    se_e = self.sSe.tag(SE.activation(gsc[:, 8:16], gsc[:, 0:8], AF.Exp, bias=gsc[:, 17:18]))
                    VE.wait_ge(sSe.h, se_e)
                    VE.tensor_reduce(gsc[:, 19:20], gsc[:, 8:16], AX.X, ALU.add)
                    VE.reciprocal(gsc[:, 20:21], gsc[:, 19:20])
                    VE.tensor_scalar(gsc[:, 8:16], gsc[:, 8:16], gsc[:, 20:21], None, ALU.mult)
                    gw = gsc[:, 8:16]
                    in_j = gw.rearrange("p (o j) -> p o j", o=1).broadcast_to([128, 8, 8])
                    in_e = gw.rearrange("p (e o) -> p e o", o=1).broadcast_to([128, 8, 8])
                    pair = gsc[:, 32:96].rearrange("p (e j) -> p e j", e=8)
                    VE.tensor_tensor(pair, in_j, in_e, ALU.is_gt)
                    VE.tensor_reduce(gsc[:, 96:104], pair, AX.X, ALU.add)
                    VE.tensor_tensor(pair, in_j, in_e, ALU.is_equal)
                    VE.tensor_tensor(gsc[:, 32:96], gsc[:, 32:96], lower_sb[:, :], ALU.mult)
                    VE.tensor_reduce(gsc[:, 104:112], pair, AX.X, ALU.add)
                    VE.tensor_tensor(gsc[:, 96:104], gsc[:, 96:104], gsc[:, 104:112], ALU.add)
                    VE.tensor_scalar(gsc[:, 96:104], gsc[:, 96:104], 2.0, None, ALU.is_lt)
                    tm = own_ti * 4 + a
                    self.sVe.tag(VE.tensor_tensor(P["ccol"][:, tm * 8:(tm + 1) * 8], gw, gsc[:, 96:104], ALU.mult))
                xnT_r.release((self.sPe, self.sPe.v), st_xnT)

            specs = [("xp", D["wxp_bfD"], True)]
            if do_z and _F_Z:
                specs.append(("z", D["wz_bfD"], False))
            pe_xd = None
            for name, wsrc, is_xp in specs:
                for half in range(2):
                    w = w_r.acquire(SY)
                    lw = self.ld(w[:, :].rearrange("p (kk c) -> p kk c", kk=8),
                                 wsrc.ap()[:, half * DM:(half + 1) * DM]
                                 .rearrange("(kk p) c -> p kk c", p=128), w_r.ldsem())
                    self.wt(PE, lw)
                    PE.wait_ge(sSe.h, se_rn0T)
                    last_mm_w = None
                    for gh in range(8):
                        g = half * 8 + gh
                        psx = pa_r.acquire(PE)
                        last = None
                        for kk in range(8):
                            last = PE.matmul(psx[:, :],
                                             w[:, kk * DM + gh * 128: kk * DM + (gh + 1) * 128],
                                             rn0T[:, kk * TILE:(kk + 1) * TILE],
                                             start=(kk == 0), stop=(kk == 7))
                        pe_xp = self.sPe.tag(last)
                        last_mm_w = pe_xp
                        SE.wait_ge(sPe.h, pe_xp)
                        if is_xp:
                            for sem, val in xpg_free[g]:
                                SE.wait_ge(sem.h, val)
                            xpg_free[g] = []
                            se_ev = self.sSe.tag(SE.activation(xpg[g][:, 3:3 + TILE], psx[:, :], AF.Copy))
                            pa_r.release((self.sSe, se_ev))
                            VE.wait_ge(sSe.h, se_ev)
                            xc = xcg_r.acquire(VE)
                            VE.tensor_scalar(xc[:, :], xpg[g][:, 0:TILE], P["convw"][:, g * 4:g * 4 + 1], None, ALU.mult)
                            for t in range(1, 4):
                                VE.scalar_tensor_tensor(xc[:, :], xpg[g][:, t:t + TILE],
                                                        P["convw"][:, g * 4 + t:g * 4 + t + 1], xc[:, :],
                                                        ALU.mult, ALU.add)
                            VE.tensor_scalar(xc[:, :], xc[:, :], P["convb"][:, g:g + 1], None, ALU.add)
                            ve_tail = self.sVe.tag(VE.tensor_copy(xpg[g][:, 0:3], xpg[g][:, TILE:TILE + 3]))
                            SE.wait_ge(sVe.h, ve_tail)
                            se_th = self.sSe.tag(SE.activation(xpg[g][:, 3:3 + TILE], xc[:, :], AF.Tanh, scale=0.5))
                            VE.wait_ge(sSe.h, se_th)
                            VE.tensor_scalar(xpg[g][:, 3:3 + TILE], xpg[g][:, 3:3 + TILE], 0.5, 0.5, ALU.mult, ALU.add)
                            ve_xc = self.sVe.tag(VE.tensor_tensor(xc[:, :], xc[:, :], xpg[g][:, 3:3 + TILE], ALU.mult))
                            xpg_free[g] = [(self.sVe, ve_xc)]
                            if spill_col is not None and _F_XDBL:
                                xcbf = xcbf_r.acquire(VE)
                                ve_xcb = self.sVe.tag(VE.tensor_copy(xcbf[:, :], xc[:, :]))
                                SC.wait_ge(sVe.h, ve_xc)
                                st_xc = self.st(D["xcD"].ap()[g * 128:(g + 1) * 128, spill_col:spill_col + TILE],
                                                xc[:, :], xcg_r.stsem())
                                xc_spill_marks[g] = st_xc
                                xcg_r.release(st_xc)
                                PE.wait_ge(sVe.h, ve_xcb)
                                if g == 0:
                                    for sem, val in px_free:
                                        PE.wait_ge(sem.h, val)
                                    del px_free[:]
                                pe_xd = self.sPe.tag(PE.matmul(px[0:96, :], xprojw_sb[:, g * 96:(g + 1) * 96],
                                                               xcbf[:, :], start=(g == 0), stop=(g == G - 1),
                                                               skip_group_check=True))
                                xcbf_r.release((self.sPe, pe_xd))
                            else:
                                xcg_r.release((self.sVe, ve_xc))
                        else:
                            zev = zev_r.acquire(SE)
                            se_ev = self.sSe.tag(SE.activation(zev[:, :], psx[:, :], AF.Copy))
                            pa_r.release((self.sSe, se_ev))
                            SC.wait_ge(sSe.h, se_ev)
                            st_z = self.st(D["zD"].ap()[g * 128:(g + 1) * 128,
                                                        spill_col - PREF:spill_col - PREF + TILE],
                                           zev[:, :], zev_r.stsem())
                            zev_r.release(st_z)
                    w_r.release((self.sPe, last_mm_w))
            rn0T_r.release((self.sPe, self.sPe.v))

            if not do_scan or not _F_SCAN:
                return

            if not _F_XDBL:
                return
            xdbl = xdbl_r.acquire(SE)
            SE.wait_ge(sPe.h, pe_xd)
            se_xd = self.sSe.tag(SE.activation(xdbl[0:96, :], px[0:96, :], AF.Copy))
            px_free.append((self.sSe, se_xd))
            SC.wait_ge(sSe.h, se_xd)
            st_b = self.st(D["BD"].ap()[:, spill_col:spill_col + TILE], xdbl[64:80, :], xdbl_r.stsem())
            if own_ti is not None:
                st_b = self.st(D["CD"].ap()[:, own_ti * TILE:(own_ti + 1) * TILE], xdbl[80:96, :], xdbl_r.stsem())
            xdblbf = xdblbf_r.acquire(VE)
            VE.wait_ge(sSe.h, se_xd)
            ve_xdb = self.sVe.tag(VE.tensor_copy(xdblbf[0:64, :], xdbl[0:64, :]))
            xdbl_r.release((self.sVe, ve_xdb), st_b)

            for g in (range(G) if _F_DT else []):
                psx = pa_r.acquire(PE)
                PE.wait_ge(sVe.h, ve_xdb)
                pe_dt = self.sPe.tag(PE.matmul(psx[:, :], dtww_sb[0:64, g * 128:(g + 1) * 128],
                                               xdblbf[0:64, :], start=True, stop=True))
                if g == G - 1:
                    xdblbf_r.release((self.sPe, pe_dt))
                if _F_DTSUB < 2:
                    se_sp = self.sSe.v
                    ve_dt0 = self.sVe.v
                    pa_r.release((self.sPe, pe_dt))
                    continue
                VE.wait_ge(sPe.h, pe_dt)
                ve_max = self.sVe.tag(VE.tensor_scalar(spm_sb[:, :], psx[:, :], P["dtb"][:, g:g + 1],
                                                       0.0, ALU.add, ALU.max))
                SE.wait_ge(sVe.h, ve_max)
                SE.activation(spl_sb[:, :], psx[:, :], AF.Abs, bias=P["dtb"][:, g:g + 1])
                SE.activation(spl_sb[:, :], spl_sb[:, :], AF.Exp, scale=-1.0)
                se_sp = self.sSe.tag(SE.activation(spl_sb[:, :], spl_sb[:, :], AF.Ln, bias=1.0))
                if _F_DTSUB < 3:
                    pa_r.release((self.sSe, se_sp))
                    continue
                dtg = dtg_r.acquire(VE)
                VE.wait_ge(sSe.h, se_sp)
                VE.tensor_tensor(dtg[:, :], spm_sb[:, :], spl_sb[:, :], ALU.add)
                if mask_col is not None:
                    self.wt(VE, lm)
                    VE.tensor_tensor(dtg[:, :], dtg[:, :], mk[:, :], ALU.mult)
                if _F_DTSUB < 4:
                    prev["ve_dt"] = self.sVe.tag(VE.tensor_copy(dtg[:, 0:1], dtg[:, 0:1]))
                    pa_r.release((self.sSe, se_sp), (self.sVe, prev["ve_dt"]))
                    dtg_r.release((self.sVe, prev["ve_dt"]))
                    continue
                xcl = xcl_r.acquire(SY)
                self.wt(SY, xc_spill_marks[g])
                lxc = self.ld(xcl[:, :], D["xcD"].ap()[g * 128:(g + 1) * 128, spill_col:spill_col + TILE],
                              xcl_r.ldsem())
                dtug = dtug_r.acquire(VE)
                self.wt(VE, lxc)
                ve_dt = self.sVe.tag(VE.tensor_tensor(dtug[:, :], dtg[:, :], xcl[:, :], ALU.mult))
                prev["ve_dt"] = ve_dt
                pa_r.release((self.sSe, se_sp), (self.sVe, ve_dt))
                xcl_r.release((self.sVe, ve_dt))
                SC.wait_ge(sVe.h, ve_dt)
                st1 = self.st(D["dtD"].ap()[g * 128:(g + 1) * 128, spill_col:spill_col + TILE], dtg[:, :], dtg_r.stsem())
                st2 = self.st(D["dtuD"].ap()[g * 128:(g + 1) * 128, spill_col:spill_col + TILE], dtug[:, :], dtug_r.stsem())
                dtg_r.release(st1)
                dtug_r.release(st2)
            if mask_col is not None:
                mask_r.release((self.sVe, self.sVe.v))

        for ti in range(_NT_OWN):
            src = D["x_own"].ap()[ti * TILE:(ti + 1) * TILE, :]
            if ti == 0:
                tile(src, None, None, None, do_scan=False, do_z=False)
            else:
                tile(src, PREF + (ti - 1) * TILE, ti - 1, None, do_scan=True, do_z=True)
        for g in range(G):
            for sem, val in xpg_free[g]:
                if sem is not self.sVe:
                    VE.wait_ge(sem.h, val)
            xpg_free[g] = []
            VE.memset(xpg[g][:, 0:3], 0.0)
        self.sVe.tag(VE.memset(st_sb[:, 0:1], 0.0))
        for ti in range(_NT_PREF):
            src = D["x_pref"].ap()[ti * TILE:(ti + 1) * TILE, :]
            tile(src, ti * TILE, None, ti * TILE, do_scan=True, do_z=False)

        self.join_all()
        es.close()

    def ph2(self):
        D, P = self.dram, self.persist
        es = ExitStack()
        SY, SC, VE, SE, PE, GP = self.SY, self.SC, self.VE, self.SE, self.PE, self.GP
        sLd, sSt, sVe, sSe, sPe, sGp = self.sLd, self.sSt, self.sVe, self.sSe, self.sPe, self.sGp

        dtq_r = self.rot(es, "s_dtq", 2, (128, 4 * CHK))
        dtuq_r = self.rot(es, "s_dtuq", 2, (128, 4 * CHK))
        Bs_r = self.rot(es, "s_B", 2, (128, CHK))
        Cs_r = self.rot(es, "s_C", 1, (128, CHK))
        ab_r = Rot([self.sb(es, f"s_a{i}", (128, CHK)) for i in range(2)])
        bb = self.sb(es, "s_b", (128, CHK))
        hb = self.sb(es, "s_h", (128, CHK))
        tmp = self.sb(es, "s_tmp", (128, CHK))
        yacc = [self.sb(es, f"s_y{i}", (128, CHK)) for i in range(4)]
        carry = self.sb(es, "s_carry", (128, G * DS))
        ygbf_r = self.rot(es, "s_ygbf", 1, (128, CHK), BF16, ld=False, st=True)
        xcg2_r = self.rot(es, "s_xcg", 1, (128, CHK))
        zg2_r = self.rot(es, "s_zg", 1, (128, CHK))

        m_xnT_r = self.rot(es, "m_xnT", 1, (128, 8 * TILE), BF16)
        m_w1_r = self.rot(es, "m_w1", 1, (128, 8 * CHK), BF16)
        m_hT = self.sb(es, "m_hT", (128, 32 * TILE), BF16)
        m_hT_free = []
        m_ctb_r = self.rot(es, "m_ctb", 2, (128, TILE), ld=False, st=True)
        m_b2_r = self.rot(es, "m_b2", 1, (1, TILE))
        m_ps1_r = Rot([self.psum(es, f"m_ps1{i}", (128, TILE)) for i in range(2)])
        m_ps2h = [self.psum(es, f"m_ps2h{i}", (128, TILE)) for i in range(4)]
        m_ps2h_free = [[] for _ in range(4)]
        m_w2b_r = self.rot(es, "m_w2b", 2, (128, 8 * TILE), BF16)

        self.sVe.tag(VE.memset(carry[:, :], 0.0))
        gacc = {}

        def moe_unit(e, nq):
            xnT = m_xnT_r.acquire(SY)
            lxn = self.ld(xnT[:, :].rearrange("p (kk t) -> p kk t", kk=8),
                          D["xnTD"].ap()[:, nq * TILE:(nq + 1) * TILE]
                          .rearrange("(kk p) t -> p kk t", p=128), m_xnT_r.ldsem())

            for sem, val in m_hT_free:
                SE.wait_ge(sem.h, val)
            del m_hT_free[:]
            for hq in range(4):
                w1 = m_w1_r.acquire(SY)
                lw1 = self.ld(w1[:, :].rearrange("p (kk h) -> p kk h", kk=8),
                              D["w1bfD"].ap()[e][:, hq * CHK:(hq + 1) * CHK]
                              .rearrange("(kk p) h -> p kk h", p=128), m_w1_r.ldsem())
                self.wt(PE, lw1, lxn)
                last_w1 = None
                for hm in range(8):
                    hmg = hq * 8 + hm
                    ps1 = m_ps1_r.acquire(PE)
                    last = None
                    for kk in range(8):
                        last = PE.matmul(ps1[:, :],
                                         w1[:, kk * CHK + hm * 128: kk * CHK + (hm + 1) * 128],
                                         xnT[:, kk * TILE:(kk + 1) * TILE],
                                         start=(kk == 0), stop=(kk == 7))
                    pe1 = self.sPe.tag(last)
                    last_w1 = pe1
                    SE.wait_ge(sPe.h, pe1)
                    se_ev = self.sSe.tag(SE.activation(
                        m_hT[:, hmg * TILE:(hmg + 1) * TILE],
                        ps1[:, :], AF.Relu, bias=P["b1"][:, e * 32 + hmg:e * 32 + hmg + 1]))
                    m_ps1_r.release((self.sSe, se_ev))
                m_w1_r.release((self.sPe, last_w1))
            se_h_done = self.sSe.v
            PE.wait_ge(sSe.h, se_h_done)
            for dmh in range(2):
                b2 = m_b2_r.acquire(SY)
                lb2 = self.ld(b2[0:1, :], D["moe_b2_r"].ap()[0:1, e * DM + dmh * TILE:e * DM + (dmh + 1) * TILE],
                              m_b2_r.ldsem())
                self.wt(PE, lb2)
                for hkb in range(4):
                    w2b = m_w2b_r.acquire(SY)
                    lw2 = self.ld(
                        w2b[:, :].rearrange("p (h c) -> p h c", h=8),
                        D["w2bfD"].ap()[e][hkb * 1024:(hkb + 1) * 1024, dmh * TILE:(dmh + 1) * TILE]
                        .rearrange("(h p) c -> p h c", p=128),
                        m_w2b_r.ldsem())
                    self.wt(PE, lw2)
                    last_b = None
                    for tl in range(4):
                        if hkb == 0:
                            for mk in m_ps2h_free[tl]:
                                PE.wait_ge(mk[0].h, mk[1])
                            m_ps2h_free[tl] = []
                        for hkin in range(8):
                            hk = hkb * 8 + hkin
                            last_b = PE.matmul(m_ps2h[tl][:, :],
                                               m_hT[:, hk * TILE + tl * 128:hk * TILE + (tl + 1) * 128],
                                               w2b[:, hkin * TILE:(hkin + 1) * TILE],
                                               start=(hk == 0), stop=False,
                                               skip_group_check=True)
                    pe_w2 = self.sPe.tag(last_b)
                    m_w2b_r.release((self.sPe, pe_w2))
                for tl in range(4):
                    tokg = nq * 4 + tl
                    pe2 = self.sPe.tag(PE.matmul(m_ps2h[tl][:, :],
                                                 P["ones"][0:1, 0:128],
                                                 b2[0:1, 0:TILE],
                                                 start=False, stop=True, skip_group_check=True))
                    ctb = m_ctb_r.acquire(VE)
                    VE.wait_ge(sPe.h, pe2)
                    ve_ct = self.sVe.tag(VE.tensor_scalar(
                        ctb[:, :], m_ps2h[tl][:, :], P["ccol"][:, tokg * 8 + e:tokg * 8 + e + 1],
                        None, ALU.mult))
                    m_ps2h_free[tl].append((self.sVe, ve_ct))
                    key = (tokg, dmh)
                    if key in gacc:
                        self.wt(GP, gacc[key])
                    GP.wait_ge(sVe.h, ve_ct)
                    csem = m_ctb_r.stsem()
                    GP.dma_start(out=D["gD"].ap()[tokg * 128:(tokg + 1) * 128, dmh * TILE:(dmh + 1) * TILE],
                                 in_=ctb[:, :],
                                 accum_op=(ALU.add if e > 0 else ALU.bypass)).then_inc(csem.h, 16)
                    csem.v += 16
                    gacc[key] = (csem, csem.v)
                    m_ctb_r.release((csem, csem.v))
                m_b2_r.release((self.sPe, self.sPe.v))
            m_hT_free.append((self.sPe, self.sPe.v))
            m_xnT_r.release((self.sPe, self.sPe.v))

        moe_units = iter([(e, nq) for e in range(8) for nq in range(4)] if _STAGE >= 3 else [])

        n_chunks = NTOK // CHK
        for c in range(n_chunks):
            own_c = c >= PREF // CHK
            for q in range(4):
                dtq = dtq_r.acquire(SY)
                ldt = self.ld(dtq[:, :].rearrange("p (g t) -> p g t", g=4),
                              D["dtD"].ap()[q * TILE:(q + 1) * TILE, c * CHK:(c + 1) * CHK]
                              .rearrange("(g p) t -> p g t", p=128), dtq_r.ldsem())
                dtuq = dtuq_r.acquire(SY)
                ldtu = self.ld(dtuq[:, :].rearrange("p (g t) -> p g t", g=4),
                               D["dtuD"].ap()[q * TILE:(q + 1) * TILE, c * CHK:(c + 1) * CHK]
                               .rearrange("(g p) t -> p g t", p=128), dtuq_r.ldsem())
                ve_last_y = 0
                for si in range(DS):
                    Bsb = Bs_r.acquire(SY)
                    lB = self.ld(Bsb[:, :], D["BD"].ap()[si:si + 1, c * CHK:(c + 1) * CHK].squeeze(0)
                                 .partition_broadcast(128), Bs_r.ldsem())
                    if own_c:
                        Csb = Cs_r.acquire(SY)
                        lC = self.ld(Csb[:, :], D["CD"].ap()[si:si + 1, (c - 6) * CHK:(c - 5) * CHK].squeeze(0)
                                     .partition_broadcast(128), Cs_r.ldsem())
                    for gi in range(4):
                        gg = q * 4 + gi
                        col = gg * DS + si
                        ab = ab_r.acquire(SE)
                        self.wt(SE, ldt)
                        se_a = self.sSe.tag(SE.activation(ab[:, :], dtq[:, gi * CHK:(gi + 1) * CHK],
                                                          AF.Exp, scale=P["A"][:, col:col + 1]))
                        self.wt(VE, ldtu, lB)
                        VE.tensor_tensor(bb[:, :], dtuq[:, gi * CHK:(gi + 1) * CHK], Bsb[:, :], ALU.mult)
                        VE.wait_ge(sSe.h, se_a)
                        ve_sc = self.sVe.tag(VE.tensor_tensor_scan(
                            hb[:, :], ab[:, :], bb[:, :], carry[:, col:col + 1], ALU.mult, ALU.add))
                        ab_r.release((self.sVe, ve_sc))
                        VE.tensor_copy(carry[:, col:col + 1], hb[:, CHK - 1:CHK])
                        if own_c:
                            self.wt(VE, lC)
                            VE.tensor_tensor(tmp[:, :], hb[:, :], Csb[:, :], ALU.mult)
                            if si == 0:
                                ve_last_y = self.sVe.tag(VE.tensor_copy(yacc[gi][:, :], tmp[:, :]))
                            else:
                                ve_last_y = self.sVe.tag(VE.tensor_tensor(yacc[gi][:, :], yacc[gi][:, :],
                                                                          tmp[:, :], ALU.add))
                        else:
                            ve_last_y = self.sVe.v
                    Bs_r.release((self.sVe, self.sVe.v))
                    if own_c:
                        Cs_r.release((self.sVe, self.sVe.v))
                dtq_r.release((self.sSe, self.sSe.v))
                dtuq_r.release((self.sVe, self.sVe.v))
                if own_c:
                    for gi in range(4):
                        gg = q * 4 + gi
                        ocol = (c - 6) * CHK
                        xcg2 = xcg2_r.acquire(SY)
                        lxc = self.ld(xcg2[:, :], D["xcD"].ap()[gg * 128:(gg + 1) * 128,
                                                                PREF + ocol:PREF + ocol + CHK], xcg2_r.ldsem())
                        zg2 = zg2_r.acquire(SY)
                        lz = self.ld(zg2[:, :], D["zD"].ap()[gg * 128:(gg + 1) * 128, ocol:ocol + CHK], zg2_r.ldsem())
                        self.wt(VE, lxc, lz)
                        VE.scalar_tensor_tensor(yacc[gi][:, :], xcg2[:, :], P["D"][:, gg:gg + 1],
                                                yacc[gi][:, :], ALU.mult, ALU.add)
                        SE.wait_ge(sVe.h, self.sVe.v)
                        se_t = self.sSe.tag(SE.activation(tmp[:, :], zg2[:, :], AF.Tanh, scale=0.5))
                        VE.wait_ge(sSe.h, se_t)
                        VE.tensor_scalar(tmp[:, :], tmp[:, :], 0.5, 0.5, ALU.mult, ALU.add)
                        ve_z = self.sVe.tag(VE.tensor_tensor(yacc[gi][:, :], yacc[gi][:, :], zg2[:, :], ALU.mult))
                        xcg2_r.release((self.sVe, ve_z))
                        zg2_r.release((self.sVe, ve_z))
                        ygbf = ygbf_r.acquire(VE)
                        ve_yg = self.sVe.tag(VE.tensor_tensor(ygbf[:, :], yacc[gi][:, :], tmp[:, :], ALU.mult))
                        SC.wait_ge(sVe.h, ve_yg)
                        st_yg = self.st(D["ygD"].ap()[gg * 128:(gg + 1) * 128, ocol:ocol + CHK], ygbf[:, :],
                                        ygbf_r.stsem())
                        ygbf_r.release(st_yg)
                nxt = next(moe_units, None)
                if nxt is not None:
                    moe_unit(*nxt)
        for nxt in moe_units:
            moe_unit(*nxt)

        self.join_all()
        es.close()

    def ph3(self):
        D, P = self.dram, self.persist
        es = ExitStack()
        SY, SC, VE, SE, PE = self.SY, self.SC, self.VE, self.SE, self.PE
        sLd, sSt, sVe, sSe, sPe = self.sLd, self.sSt, self.sVe, self.sSe, self.sPe

        opw = self.sb(es, "f_opw", (128, G * DM), BF16)
        opw_ld = self.dsem("f_opw_ld")
        ygT_r = self.rot(es, "f_ygT", 1, (128, G * TILE), BF16)
        xn4_r = self.rot(es, "f_xn", 1, (128, 4 * DM))
        x1 = self.sb(es, "f_x1", (128, 4 * DM))
        stt = self.sb(es, "f_st", (128, 16))
        rn1_r = Rot(self.sb(es, "f_rn1", (128, DM)))
        rn1T_r = Rot(self.sb(es, "f_rn1T", (128, 8 * TILE), BF16))
        w1c_r = self.rot(es, "f_w1c", 2, (128, 8 * 128), BF16)
        hT = self.sb(es, "f_hT", (128, 32 * TILE), BF16)
        hT_free = []
        w2c_r = self.rot(es, "f_w2c", 2, (128, 8 * TILE), BF16)
        qq = self.sb(es, "f_q", (128, DM))
        x2_r = Rot(self.sb(es, "f_x2", (128, DM)))
        gg_r = self.rot(es, "f_g", 2, (128, DM))
        xo_r = self.rot(es, "f_xo", 2, (128, DM))
        oo_r = self.rot(es, "f_oo", 2, (128, DM), ld=False, st=True)
        pA_r = Rot([self.psum(es, f"f_pA{i}", (128, DM)) for i in range(2)])
        pB_r = Rot([self.psum(es, f"f_pB{i}", (128, TILE)) for i in range(2)])
        pT_r = Rot(self.psum(es, "f_pT", (128, 128)))

        lop = self.ld(opw[:, :].rearrange("p (kk d) -> p kk d", kk=G),
                      D["outproj_bfD"].ap().rearrange("(kk p) d -> p kk d", p=128), opw_ld)

        for nt in range(4):
            ygT = ygT_r.acquire(SY)
            lyg = self.ld(ygT[:, :].rearrange("p (kk t) -> p kk t", kk=G),
                          D["ygD"].ap()[:, nt * TILE:(nt + 1) * TILE]
                          .rearrange("(kk p) t -> p kk t", p=128), ygT_r.ldsem())
            xn4 = xn4_r.acquire(SY)
            lxn = self.ld(xn4[:, :].rearrange("p (a d) -> p a d", a=4),
                          D["xnD"].ap()[nt * TILE:(nt + 1) * TILE, :]
                          .rearrange("(a p) d -> p a d", p=128), xn4_r.ldsem())
            self.wt(PE, lyg, lxn, lop)
            for a in range(4):
                pA = pA_r.acquire(PE)
                last = None
                for kk in range(G):
                    for half in range(2):
                        last = PE.matmul(pA[:, half * TILE:(half + 1) * TILE],
                                         ygT[:, kk * TILE + a * 128: kk * TILE + (a + 1) * 128],
                                         opw[:, kk * DM + half * TILE: kk * DM + (half + 1) * TILE],
                                         start=(kk == 0), stop=(kk == G - 1 and half == 1))
                pe_op = self.sPe.tag(last)
                VE.wait_ge(sPe.h, pe_op)
                self.wt(VE, lxn)
                ve_x1 = self.sVe.tag(VE.tensor_tensor(
                    x1[:, a * DM:(a + 1) * DM], xn4[:, a * DM:(a + 1) * DM], pA[:, :], ALU.add))
                pA_r.release((self.sVe, ve_x1))
            ve_x1_all = self.sVe.v
            ygT_r.release((self.sPe, self.sPe.v))
            xn4_r.release((self.sVe, ve_x1_all))

            rn1T = rn1T_r.acquire(SE)
            for a in range(4):
                rn1 = rn1_r.acquire(SE)
                SE.wait_ge(sVe.h, ve_x1_all)
                se_sq = self.sSe.tag(SE.activation(rn1[:, :], x1[:, a * DM:(a + 1) * DM], AF.Square))
                VE.wait_ge(sSe.h, se_sq)
                ve_q = self.sVe.tag(VE.tensor_reduce(
                    stt[:, 0:1], rn1[:, :].rearrange("p (o d) -> p o d", o=1), AX.X, ALU.add))
                SE.wait_ge(sVe.h, ve_q)
                SE.activation(stt[:, 1:2], stt[:, 0:1], AF.Ln, bias=EPS_RMS, scale=1.0 / DM)
                self.sSe.tag(SE.activation(stt[:, 1:2], stt[:, 1:2], AF.Exp, scale=-0.5))
                se_r1 = self.sSe.v
                VE.wait_ge(sSe.h, se_r1)
                se_rn = self.sVe.tag(VE.tensor_scalar(rn1[:, :], x1[:, a * DM:(a + 1) * DM],
                                                      stt[:, 1:2], None, ALU.mult))
                PE.wait_ge(sVe.h, se_rn)
                for dmt in range(8):
                    pT = pT_r.acquire(PE)
                    pe_tr = self.sPe.tag(PE.transpose(pT[:, :], rn1[:, dmt * 128:(dmt + 1) * 128],
                                                      P["ident"][:, :]))
                    SE.wait_ge(sPe.h, pe_tr)
                    se_ev = self.sSe.tag(SE.activation(
                        rn1T[:, dmt * TILE + a * 128: dmt * TILE + (a + 1) * 128], pT[:, :], AF.Copy))
                    pT_r.release((self.sSe, se_ev))
                rn1_r.release((self.sPe, self.sPe.v))
            se_rn1T = self.sSe.v

            for sem, val in hT_free:
                VE.wait_ge(sem.h, val)
            del hT_free[:]
            PE.wait_ge(sSe.h, se_rn1T)
            for hm in range(32):
                w1c = w1c_r.acquire(SY)
                lw1 = self.ld(w1c[:, :].rearrange("p (kk h) -> p kk h", kk=8),
                              D["mlpw1_bfD"].ap()[:, hm * 128:(hm + 1) * 128]
                              .rearrange("(kk p) h -> p kk h", p=128), w1c_r.ldsem())
                self.wt(PE, lw1)
                pB = pB_r.acquire(PE)
                last = None
                for kk in range(8):
                    last = PE.matmul(pB[:, :], w1c[:, kk * 128:(kk + 1) * 128],
                                     rn1T[:, kk * TILE:(kk + 1) * TILE],
                                     start=(kk == 0), stop=(kk == 7))
                pe_m1 = self.sPe.tag(last)
                w1c_r.release((self.sPe, pe_m1))
                SE.wait_ge(sPe.h, pe_m1)
                se_t = self.sSe.tag(SE.activation(qq[:, 0:TILE], pB[:, :], AF.Tanh, scale=0.5))
                VE.wait_ge(sSe.h, se_t)
                VE.tensor_scalar(qq[:, 0:TILE], qq[:, 0:TILE], 0.5, 0.5, ALU.mult, ALU.add)
                ve_h = self.sVe.tag(VE.tensor_tensor(
                    hT[:, hm * TILE:(hm + 1) * TILE], pB[:, :], qq[:, 0:TILE], ALU.mult))
                pB_r.release((self.sVe, ve_h))
            ve_hT = self.sVe.v
            rn1T_r.release((self.sPe, self.sPe.v))

            PE.wait_ge(sVe.h, ve_hT)
            for a in range(4):
                tokg = nt * 4 + a
                pA = pA_r.acquire(PE)
                last = None
                for dmh in range(2):
                    for hkb in range(4):
                        w2c = w2c_r.acquire(SY)
                        lw2 = self.ld(
                            w2c[:, :].rearrange("p (h c) -> p h c", h=8),
                            D["mlpw2_bfD"].ap()[hkb * 1024:(hkb + 1) * 1024, dmh * TILE:(dmh + 1) * TILE]
                            .rearrange("(h p) c -> p h c", p=128),
                            w2c_r.ldsem())
                        self.wt(PE, lw2)
                        for hkin in range(8):
                            hk = hkb * 8 + hkin
                            last = PE.matmul(pA[:, dmh * TILE:(dmh + 1) * TILE],
                                             hT[:, hk * TILE + a * 128: hk * TILE + (a + 1) * 128],
                                             w2c[:, hkin * TILE:(hkin + 1) * TILE],
                                             start=(hk == 0), stop=(hk == 31),
                                             skip_group_check=True)
                        pe_w2 = self.sPe.tag(last)
                        w2c_r.release((self.sPe, pe_w2))
                pe_m2 = self.sPe.v
                gg = gg_r.acquire(SY)
                lg = self.ld(gg[:, :], D["gD"].ap()[tokg * 128:(tokg + 1) * 128, :], gg_r.ldsem())
                xo = xo_r.acquire(SY)
                lx = self.ld(xo[:, :], D["x_own"].ap()[TILE + tokg * 128: TILE + (tokg + 1) * 128, :], xo_r.ldsem())
                x2 = x2_r.acquire(VE)
                VE.wait_ge(sPe.h, pe_m2)
                ve_x2 = self.sVe.tag(VE.tensor_tensor(x2[:, :], x1[:, a * DM:(a + 1) * DM], pA[:, :], ALU.add))
                pA_r.release((self.sVe, ve_x2))
                oo = oo_r.acquire(SE)
                self.wt(SE, lg)
                se_t = self.sSe.tag(SE.activation(oo[:, :], gg[:, :], AF.Tanh, scale=0.5))
                VE.wait_ge(sSe.h, se_t)
                VE.tensor_scalar(oo[:, :], oo[:, :], 0.5, 0.5, ALU.mult, ALU.add)
                ve_g = self.sVe.tag(VE.tensor_tensor(oo[:, :], oo[:, :], gg[:, :], ALU.mult))
                gg_r.release((self.sVe, ve_g))
                VE.tensor_tensor(oo[:, :], oo[:, :], x2[:, :], ALU.mult)
                x2_r.release((self.sVe, self.sVe.v))
                self.wt(VE, lx)
                ve_o = self.sVe.tag(VE.tensor_tensor(oo[:, :], oo[:, :], xo[:, :], ALU.add))
                xo_r.release((self.sVe, ve_o))
                SC.wait_ge(sVe.h, ve_o)
                st_o = self.st(D["o_out"].ap()[tokg * 128:(tokg + 1) * 128, :], oo[:, :], oo_r.stsem())
                oo_r.release(st_o)

        if self.sSt.v:
            SC.wait_ge(self.sSt.h, self.sSt.v)
        for s in self.all_dma_sems:
            if s.v:
                SC.wait_ge(s.h, s.v)
        es.close()

    def finish(self):
        self.es.close()


def build_program():
    nc = bass.Bass("TRN2", target_bir_lowering=False)
    b = Builder(nc)
    b.ph0()
    if _STAGE >= 1:
        b.ph1()
    if _STAGE >= 2:
        b.ph2()
    if _STAGE >= 3:
        b.ph3()
    else:
        with nc.sbuf_tensor("z_oo", [128, DM], F32) as zoo:
            zv = b.sVe.tag(b.VE.memset(zoo[:, :], 0.0))
            b.SC.wait_ge(b.sVe.h, zv)
            for t in range(OWN // 128):
                b.st(b.dram["o_out"].ap()[t * 128:(t + 1) * 128, :], zoo[:, :])
            b.SC.wait_ge(b.sSt.h, b.sSt.v)
    if _DBG:
        D = b.dram
        b.join_all()
        pairs = [("d_dtp", D["dtD"].ap()[:, 0:2048]), ("d_dto", D["dtD"].ap()[:, PREF:]),
                 ("d_xc", D["xcD"].ap()[:, PREF:]), ("d_z", D["zD"].ap()),
                 ("d_xn", D["xnD"].ap()), ("d_B", D["BD"].ap()), ("d_C", D["CD"].ap())]
        if _STAGE >= 2:
            pairs.append(("d_yg", D["ygD"].ap()))
        if _STAGE >= 3:
            pairs.append(("d_g", D["gD"].ap()))
        for nm, srcap in pairs:
            b.st(D[nm].ap(), srcap)
        zv = b.sVe.tag(b.VE.tensor_copy(b.persist["ccol"][:, 0:1], b.persist["ccol"][:, 0:1]))
        b.SC.wait_ge(b.sVe.h, zv)
        b.st(D["d_cc"].ap()[:, 0:128], b.persist["ccol"][:, 0:128])
        b.SC.wait_ge(b.sSt.h, b.sSt.v)
    b.finish()
    return nc


def make_inmaps(inputs):
    x = np.ascontiguousarray(inputs["L0_attn_x"], dtype=np.float32)
    ipw = np.ascontiguousarray(inputs["L0_attn_in_proj_w"], dtype=np.float32)
    conv_w = np.ascontiguousarray(inputs["L0_attn_conv_w"], dtype=np.float32)
    conv_b = np.ascontiguousarray(inputs["L0_attn_conv_b"], dtype=np.float32)
    xpw = np.ascontiguousarray(inputs["L0_attn_x_proj_w"], dtype=np.float32)
    dtw = np.ascontiguousarray(inputs["L0_attn_dt_proj_w"], dtype=np.float32)
    dtb = np.ascontiguousarray(inputs["L0_attn_dt_proj_b"], dtype=np.float32)
    alog = np.ascontiguousarray(inputs["L0_attn_A_log"], dtype=np.float32)
    dd = np.ascontiguousarray(inputs["L0_attn_D"], dtype=np.float32)
    opw = np.ascontiguousarray(inputs["L0_attn_out_proj_w"], dtype=np.float32)
    mw1 = np.ascontiguousarray(inputs["L0_attn_mlp_w1"], dtype=np.float32)
    mw2 = np.ascontiguousarray(inputs["L0_attn_mlp_w2"], dtype=np.float32)
    gw = np.ascontiguousarray(inputs["L0_ffn_gate_w"], dtype=np.float32)
    gb = np.ascontiguousarray(inputs["L0_ffn_gate_b"], dtype=np.float32)
    w1 = np.ascontiguousarray(inputs["L0_ffn_w1"], dtype=np.float32)
    b1 = np.ascontiguousarray(inputs["L0_ffn_b1"], dtype=np.float32)
    w2 = np.ascontiguousarray(inputs["L0_ffn_w2"], dtype=np.float32)
    b2 = np.ascontiguousarray(inputs["L0_ffn_b2"], dtype=np.float32)

    low = np.zeros((64,), np.float32)
    for e in range(8):
        for j in range(8):
            low[e * 8 + j] = 1.0 if j < e else 0.0

    shared = dict(
        wxp_t=np.ascontiguousarray(ipw[:DI].T),
        wz_t=np.ascontiguousarray(ipw[DI:].T),
        convw_c=np.ascontiguousarray(conv_w.reshape(G, 128, 4).transpose(1, 0, 2).reshape(128, G * 4)),
        convb_c=np.ascontiguousarray(conv_b.reshape(G, 128).T),
        xproj_t=np.ascontiguousarray(xpw.T),
        dtw_t=np.ascontiguousarray(dtw.T),
        dtb_c=np.ascontiguousarray(dtb.reshape(G, 128).T),
        alog_c=np.ascontiguousarray(alog.reshape(G, 128, DS).transpose(1, 0, 2).reshape(128, G * DS)),
        d_c=np.ascontiguousarray(dd.reshape(G, 128).T),
        outproj_t=np.ascontiguousarray(opw.T),
        mlpw1_t=np.ascontiguousarray(mw1.T),
        mlpw2_t=np.ascontiguousarray(mw2.T),
        gate_w=gw,
        gate_b_r=np.ascontiguousarray(np.repeat(gb[None, :], 128, 0)),
        moe_w1=w1,
        moe_b1_c=np.ascontiguousarray(b1.reshape(8, 32, 128).transpose(2, 0, 1).reshape(128, 8 * 32)),
        moe_w2=w2,
        moe_b2_r=np.ascontiguousarray(b2.reshape(1, 8 * DM)),
        const_v=np.ascontiguousarray(np.repeat(np.array([CONSTS], np.float32), 128, 0)),
        ident=np.eye(128, dtype=np.float32),
        ones_r=np.ones((128, 128), np.float32),
        lower_r=np.ascontiguousarray(np.repeat(low[None, :], 128, 0)),
    )

    in_maps = []
    for c in range(8):
        b_, k = c // 4, c % 4
        Pk = k * OWN
        ext = np.zeros((TILE, DM), np.float32) if k == 0 else x[b_, Pk - TILE:Pk]
        m = dict(shared)
        m["x_pref"] = np.ascontiguousarray(x[b_, :PREF])
        m["x_own"] = np.ascontiguousarray(np.concatenate([ext, x[b_, Pk:Pk + OWN]], axis=0))
        m["mask_rep"] = np.ascontiguousarray(
            np.repeat((np.arange(PREF) < Pk).astype(np.float32)[None, :], 128, 0))
        in_maps.append(m)
    return in_maps


_NC_CACHE = [None]


def kernel(**inputs):
    in_maps = make_inmaps(inputs)
    if _NC_CACHE[0] is None:
        _NC_CACHE[0] = build_program()
    nc = _NC_CACHE[0]
    res = run_bass_kernel_spmd(nc, in_maps, core_ids=list(range(8)))
    out = np.empty((2, L, DM), np.float32)
    for c in range(8):
        b_, k = c // 4, c % 4
        out[b_, k * OWN:(k + 1) * OWN] = np.asarray(res.results[c]["o_out"])
    return out


# version 05
# speedup vs baseline: 1.4042x; 1.0051x over previous
import os
import numpy as np
from contextlib import ExitStack

import concourse.bass as bass
import concourse.mybir as mybir
from concourse.bass_utils import run_bass_kernel_spmd

F32 = mybir.dt.float32
BF16 = mybir.dt.bfloat16
AF = mybir.ActivationFunctionType
ALU = mybir.AluOpType
AX = mybir.AxisListType

L = 8192
DM = 1024
DI = 2048
DS = 16
DTR = 64
OWN = 2048
PREF = 6144
TILE = 512
G = 16
CHK = 1024
NTOK = PREF + OWN
EPS_LN = 1e-5
EPS_RMS = 1.1920929e-07

_STAGE = int(os.environ.get("MK_STAGE", "4"))
_DBG = os.environ.get("MK_DBG", "0") == "1"
_NT_OWN = int(os.environ.get("MK_NT_OWN", "5"))
_F_XNT = os.environ.get("MK_F_XNT", "1") == "1"
_F_GATE = os.environ.get("MK_F_GATE", "1") == "1"
_F_Z = os.environ.get("MK_F_Z", "1") == "1"
_F_SCAN = os.environ.get("MK_F_SCAN", "1") == "1"
_F_XDBL = os.environ.get("MK_F_XDBL", "1") == "1"
_F_DT = os.environ.get("MK_F_DT", "1") == "1"
_F_DTSUB = int(os.environ.get("MK_F_DTSUB", "4"))
_NT_PREF = int(os.environ.get("MK_NT_PREF", "12"))
CONSTS = [EPS_LN, EPS_RMS, 1.0, 0.5, -0.5, -1.0, 0.0, 2.0]


class Sem:
    def __init__(self, es, nc, name, auto=False):
        self.h = es.enter_context(nc.semaphore(name))
        self.v = 0
        self.auto = auto

    def tag(self, instr, n=1):
        if self.auto:
            return self.v
        instr.then_inc(self.h, n)
        self.v += n
        return self.v


class SelfSync:

    _PASSTHRU = {"wait_ge", "dma_start"}

    def __init__(self, eng, sem):
        self._eng = eng
        self._sem = sem

    def wait_ge(self, h, v):
        return self._eng.wait_ge(h, v)

    def dma_start(self, *a, **k):
        return self._eng.dma_start(*a, **k)

    def __getattr__(self, name):
        f = getattr(self._eng, name)
        if not callable(f):
            return f
        sem = self._sem
        eng = self._eng

        def wrapped(*a, **k):
            if sem.v:
                eng.wait_ge(sem.h, sem.v)
            instr = f(*a, **k)
            if hasattr(instr, "then_inc"):
                instr.then_inc(sem.h, 1)
                sem.v += 1
            return instr

        return wrapped


class Rot:

    def __init__(self, bufs, ldsems=None, stsems=None):
        self.bufs = bufs if isinstance(bufs, list) else [bufs]
        self.free = [[] for _ in self.bufs]
        self.i = -1
        self.ldsems = ldsems
        self.stsems = stsems

    def acquire(self, writer):
        self.i += 1
        j = self.i % len(self.bufs)
        for sem, val in self.free[j]:
            if val:
                writer.wait_ge(sem.h, val)
        self.free[j] = []
        return self.bufs[j]

    def ldsem(self):
        return self.ldsems[self.i % len(self.bufs)]

    def stsem(self):
        return self.stsems[self.i % len(self.bufs)]

    def cur(self):
        return self.bufs[self.i % len(self.bufs)]

    def release(self, *marks):
        j = self.i % len(self.bufs)
        self.free[j].extend(marks)


class Builder:
    def __init__(self, nc):
        self.nc = nc
        self.es = ExitStack()
        es = self.es
        self.SY = nc.sync
        self.SC = nc.scalar
        self.VE = nc.vector
        self.SE = nc.scalar
        self.PE = nc.tensor
        self.GP = nc.gpsimd

        self.sLd = Sem(es, nc, "sLd")
        self.sSt = Sem(es, nc, "sSt")
        self.sGp = Sem(es, nc, "sGp")
        self.sPe = Sem(es, nc, "sPe")
        self.sVe = Sem(es, nc, "sVe", auto=True)
        self.sSe = Sem(es, nc, "sSe", auto=True)
        self.VE = SelfSync(self.VE, self.sVe)
        self.SE = SelfSync(self.SE, self.sSe)

        self.all_dma_sems = []
        self.dram = {}
        self.decl_dram()
        self.persist = {}
        P = self.persist
        P["A"] = self.sb(es, "A_sb", (128, G * DS))
        P["convw"] = self.sb(es, "convw_sb", (128, G * 4))
        P["convb"] = self.sb(es, "convb_sb", (128, G))
        P["dtb"] = self.sb(es, "dtb_sb", (128, G))
        P["D"] = self.sb(es, "d_sb", (128, G))
        P["ident"] = self.sb(es, "ident_sb", (128, 128))
        P["ones"] = self.sb(es, "ones_sb", (128, 128))
        P["ccol"] = self.sb(es, "ccol_sb", (128, 16 * 8))
        P["b1"] = self.sb(es, "b1_sb", (128, 8 * 32))
        P["cv"] = self.sb(es, "cv_sb", (128, 8))

    def decl_dram(self):
        nc = self.nc
        D = self.dram

        def inp(name, shape, dtype=F32):
            D[name] = nc.dram_tensor(name, list(shape), dtype, kind="ExternalInput")

        inp("x_pref", (PREF, DM))
        inp("x_own", (TILE + OWN, DM))
        inp("mask_rep", (128, PREF))
        inp("wxp_t", (DM, DI))
        inp("wz_t", (DM, DI))
        inp("convw_c", (128, G * 4))
        inp("convb_c", (128, G))
        inp("xproj_t", (DI, 96))
        inp("dtw_t", (DTR, DI))
        inp("dtb_c", (128, G))
        inp("alog_c", (128, G * DS))
        inp("d_c", (128, G))
        inp("outproj_t", (DI, DM))
        inp("mlpw1_t", (DM, 4 * DM))
        inp("mlpw2_t", (4 * DM, DM))
        inp("gate_w", (DM, 8))
        inp("gate_b_r", (128, 8))
        inp("moe_w1", (8, DM, 4 * DM))
        inp("moe_b1_c", (128, 8 * 32))
        inp("moe_w2", (8, 4 * DM, DM))
        inp("moe_b2_r", (1, 8 * DM))
        inp("const_v", (128, 8))
        inp("ident", (128, 128))
        inp("ones_r", (128, 128))
        inp("lower_r", (128, 64))

        D["o_out"] = nc.dram_tensor("o_out", [OWN, DM], F32, kind="ExternalOutput")
        if _DBG:
            for nm, shape, dt in [("d_dtp", (DI, 2048), F32), ("d_dto", (DI, OWN), F32),
                                  ("d_xc", (DI, OWN), F32), ("d_z", (DI, OWN), F32),
                                  ("d_xn", (OWN, DM), F32), ("d_B", (DS, NTOK), F32),
                                  ("d_C", (DS, OWN), F32), ("d_yg", (DI, OWN), BF16),
                                  ("d_g", (OWN, DM), F32), ("d_cc", (128, 128), F32)]:
                D[nm] = nc.dram_tensor(nm, list(shape), dt, kind="ExternalOutput")

        def scr(name, shape, dtype=F32):
            D[name] = nc.dram_tensor(name, list(shape), dtype, kind="Internal")

        scr("dtD", (DI, NTOK))
        scr("dtuD", (DI, NTOK))
        scr("BD", (DS, NTOK))
        scr("CD", (DS, OWN))
        scr("xcD", (DI, NTOK))
        scr("zD", (DI, OWN))
        scr("xnD", (OWN, DM))
        scr("xnTD", (DM, OWN), BF16)
        scr("ygD", (DI, OWN), BF16)
        scr("gD", (OWN, DM))
        scr("wxp_bfD", (DM, DI), BF16)
        scr("wz_bfD", (DM, DI), BF16)
        scr("xproj_bfD", (DI, 96), BF16)
        scr("dtw_bfD", (DTR, DI), BF16)
        scr("outproj_bfD", (DI, DM), BF16)
        scr("mlpw1_bfD", (DM, 4 * DM), BF16)
        scr("mlpw2_bfD", (4 * DM, DM), BF16)
        scr("gate_bfD", (DM, 8), BF16)
        scr("w1bfD", (8, DM, 4 * DM), BF16)
        scr("w2bfD", (8, 4 * DM, DM), BF16)

    def sb(self, es, name, shape, dtype=F32):
        return es.enter_context(self.nc.sbuf_tensor(name, list(shape), dtype))

    def psum(self, es, name, shape, dtype=F32):
        return es.enter_context(self.nc.psum_tensor(name, list(shape), dtype))

    def dsem(self, name):
        s = Sem(self.es, self.nc, name)
        self.all_dma_sems.append(s)
        return s

    def rot(self, es, name, n, shape, dtype=F32, ld=True, st=False):
        bufs = [self.sb(es, f"{name}{i}", shape, dtype) for i in range(n)]
        lds = [self.dsem(f"{name}{i}_ld") for i in range(n)] if ld else None
        sts = [self.dsem(f"{name}{i}_st") for i in range(n)] if st else None
        return Rot(bufs, lds, sts)

    def join_all(self):
        vals = [(s.h, s.v) for s in (self.sLd, self.sSt, self.sGp, self.sPe, self.sVe, self.sSe)]
        vals += [(s.h, s.v) for s in self.all_dma_sems if s.v]
        for eng in (self.SY, self.SC, self.VE, self.PE, self.GP):
            for h, v in vals:
                if v:
                    eng.wait_ge(h, v)

    def ld(self, out_ap, in_ap, sem=None):
        sem = sem or self.sLd
        self.SY.dma_start(out=out_ap, in_=in_ap).then_inc(sem.h, 16)
        sem.v += 16
        return (sem, sem.v)

    def st(self, out_ap, in_ap, sem=None):
        sem = sem or self.sSt
        self.SC.dma_start(out=out_ap, in_=in_ap).then_inc(sem.h, 16)
        sem.v += 16
        return (sem, sem.v)

    @staticmethod
    def wt(eng, *marks):
        for m in marks:
            if m:
                eng.wait_ge(m[0].h, m[1])

    def ph0(self):
        D, P = self.dram, self.persist
        GP, SE = self.GP, self.SE

        def cast(dst, src, nsplit=2):
            rows = src.shape[0]
            step = rows // nsplit
            for i in range(nsplit):
                GP.dma_start(out=dst.ap()[i * step:(i + 1) * step, :],
                             in_=src.ap()[i * step:(i + 1) * step, :]).then_inc(self.sGp.h, 16)
                self.sGp.v += 16

        cast(D["wxp_bfD"], D["wxp_t"])
        cast(D["wz_bfD"], D["wz_t"])
        cast(D["xproj_bfD"], D["xproj_t"], 1)
        cast(D["dtw_bfD"], D["dtw_t"], 1)
        cast(D["outproj_bfD"], D["outproj_t"])
        cast(D["mlpw1_bfD"], D["mlpw1_t"])
        cast(D["mlpw2_bfD"], D["mlpw2_t"])
        cast(D["gate_bfD"], D["gate_w"], 1)
        for e in range(8):
            GP.dma_start(out=D["w1bfD"].ap()[e], in_=D["moe_w1"].ap()[e]).then_inc(self.sGp.h, 16)
            self.sGp.v += 16
            GP.dma_start(out=D["w2bfD"].ap()[e], in_=D["moe_w2"].ap()[e]).then_inc(self.sGp.h, 16)
            self.sGp.v += 16
        self.gp_casts = self.sGp.v

        self.cA = self.dsem("cA")
        self.ld(P["cv"][:, :], D["const_v"].ap(), self.cA)
        for i, v in enumerate(CONSTS):
            self.nc.const_aps.register(F32, v, P["cv"][:, i:i + 1])
        for dst, src in [(P["convw"], "convw_c"), (P["convb"], "convb_c"),
                         (P["dtb"], "dtb_c"), (P["D"], "d_c"),
                         (P["ident"], "ident"), (P["ones"], "ones_r"),
                         (P["b1"], "moe_b1_c"), (P["A"], "alog_c")]:
            self.ld(dst[:, :], D[src].ap(), self.cA)
        SE.wait_ge(self.cA.h, self.cA.v)
        SE.activation(P["A"][:, :], P["A"][:, :], AF.Exp)
        self.sSe.tag(SE.activation(P["A"][:, :], P["A"][:, :], AF.Copy, scale=-1.0))

    def ph1(self):
        D, P = self.dram, self.persist
        es = ExitStack()
        SY, SC, VE, SE, PE = self.SY, self.SC, self.VE, self.SE, self.PE
        sLd, sSt, sVe, sSe, sPe = self.sLd, self.sSt, self.sVe, self.sSe, self.sPe

        xr_r = self.rot(es, "t_xr", 2, (128, 4 * DM))
        xn_sb = self.sb(es, "t_xn", (128, 4 * DM))
        xn_st = self.dsem("t_xn_st")
        st_sb = self.sb(es, "t_st", (128, 64))
        rn0T_r = Rot([self.sb(es, f"t_rn0T{i}", (128, 8 * TILE), BF16) for i in range(2)])
        xpg = [self.sb(es, f"t_xpg{g}", (128, TILE + 3)) for g in range(G)]
        xpg_free = [[] for _ in range(G)]
        xcg_r = self.rot(es, "t_xcg", 2, (128, TILE), ld=False, st=True)
        xcbf_r = Rot([self.sb(es, f"t_xcbf{i}", (128, TILE), BF16) for i in range(2)])
        zev_r = self.rot(es, "t_zev", 2, (128, TILE), ld=False, st=True)
        xdbl_r = self.rot(es, "t_xdbl", 2, (96, TILE), ld=False, st=True)
        xdblbf_r = Rot([self.sb(es, f"t_xdblbf{i}", (64, TILE), BF16) for i in range(2)])
        dtg_r = self.rot(es, "t_dtg", 2, (128, TILE), ld=False, st=True)
        dtug_r = self.rot(es, "t_dtug", 2, (128, TILE), ld=False, st=True)
        xcl_r = self.rot(es, "t_xcl", 2, (128, TILE))
        spm_sb = self.sb(es, "t_spm", (128, TILE))
        spl_sb = self.sb(es, "t_spl", (128, TILE))
        w_r = self.rot(es, "t_w", 2, (128, 8 * DM), BF16)
        mask_r = self.rot(es, "t_mask", 2, (128, TILE))
        gsc = self.sb(es, "t_gsc", (128, 128))
        lower_sb = self.sb(es, "t_lower", (128, 64))
        gateb_sb = self.sb(es, "t_gateb", (128, 8))
        gatew_sb = self.sb(es, "t_gatew", (128, 8 * 8), BF16)
        xnT_r = self.rot(es, "t_xnT", 1, (128, 8 * TILE), BF16, ld=False, st=True)
        xprojw_sb = self.sb(es, "t_xprojw", (128, G * 96), BF16)
        dtww_sb = self.sb(es, "t_dtww", (64, DI), BF16)

        pa_r = Rot([self.psum(es, f"t_pa{i}", (128, TILE)) for i in range(2)])
        px = self.psum(es, "t_px", (128, TILE))
        px_free = []
        pt_r = Rot([self.psum(es, f"t_pt{i}", (128, 128)) for i in range(2)])
        pg_r = Rot(self.psum(es, "t_pg", (128, 8)))

        SY.wait_ge(self.sGp.h, self.gp_casts)
        cB = self.dsem("cB")
        self.ld(xprojw_sb[:, :].rearrange("p (g r) -> p g r", r=96), D["xproj_bfD"].ap().rearrange("(g p) r -> p g r", p=128), cB)
        self.ld(dtww_sb[0:64, :], D["dtw_bfD"].ap(), cB)
        self.ld(gatew_sb[:, :].rearrange("p (kk e) -> p kk e", e=8), D["gate_bfD"].ap().rearrange("(kk p) e -> p kk e", p=128), cB)
        self.ld(gateb_sb[:, :], D["gate_b_r"].ap(), cB)
        self.ld(lower_sb[:, :], D["lower_r"].ap(), cB)
        ld_w0 = (cB, cB.v)
        cA2 = (self.cA, self.cA.v)

        self.wt(VE, ld_w0, cA2)
        self.wt(PE, ld_w0, cA2)
        for g in range(G):
            VE.memset(xpg[g][:, 0:3], 0.0)
        self.sVe.tag(VE.memset(st_sb[:, 0:1], 0.0))
        prev = {"se_rn0": 0, "ve_dt": 0, "st_xn": 0}

        xc_spill_marks = [None] * G

        def tile(src_ap, spill_col, own_ti, mask_col, do_scan, do_z):
            xr = xr_r.acquire(SY)
            lx = self.ld(xr[:, :].rearrange("p (a d) -> p a d", a=4), src_ap.rearrange("(a p) d -> p a d", p=128),
                         xr_r.ldsem())
            if mask_col is not None:
                mk = mask_r.acquire(SY)
                lm = self.ld(mk[:, :], D["mask_rep"].ap()[:, mask_col:mask_col + TILE], mask_r.ldsem())

            self.wt(VE, lx)
            VE.tensor_reduce(st_sb[:, 0:4], xr[:, :].rearrange("p (a d) -> p a d", a=4), AX.X, ALU.add)
            self.sVe.tag(VE.tensor_scalar(st_sb[:, 8:12], st_sb[:, 0:4], 1.0 / DM, None, ALU.mult))
            self.wt(SE, lx)
            if prev["st_xn"]:
                self.wt(SE, prev["st_xn"])
            se_sq = self.sSe.tag(SE.activation(xn_sb[:, :], xr[:, :], AF.Square))
            VE.wait_ge(sSe.h, se_sq)
            VE.tensor_reduce(st_sb[:, 4:8], xn_sb[:, :].rearrange("p (a d) -> p a d", a=4), AX.X, ALU.add)
            VE.tensor_scalar(st_sb[:, 12:16], st_sb[:, 4:8], 1.0 / DM, None, ALU.mult)
            VE.tensor_tensor(st_sb[:, 16:20], st_sb[:, 8:12], st_sb[:, 8:12], ALU.mult)
            ve_var = self.sVe.tag(VE.tensor_tensor(st_sb[:, 20:24], st_sb[:, 12:16], st_sb[:, 16:20], ALU.subtract))
            SE.wait_ge(sVe.h, ve_var)
            SE.activation(st_sb[:, 24:28], st_sb[:, 20:24], AF.Ln, bias=EPS_LN)
            se_r = self.sSe.tag(SE.activation(st_sb[:, 24:28], st_sb[:, 24:28], AF.Exp, scale=-0.5))
            VE.wait_ge(sSe.h, se_r)
            last = None
            for a in range(4):
                last = VE.tensor_scalar(xn_sb[:, a * DM:(a + 1) * DM], xr[:, a * DM:(a + 1) * DM],
                                        st_sb[:, 8 + a:9 + a], st_sb[:, 24 + a:25 + a],
                                        ALU.subtract, ALU.mult)
            se_xn = self.sVe.tag(last)
            SE.wait_ge(sVe.h, se_xn)
            se_sq2 = self.sSe.tag(SE.activation(xr[:, :], xn_sb[:, :], AF.Square))
            VE.wait_ge(sSe.h, se_sq2)
            VE.tensor_reduce(st_sb[:, 36:40], xr[:, :].rearrange("p (a d) -> p a d", a=4), AX.X, ALU.add)
            ve_q2 = self.sVe.tag(VE.tensor_scalar(st_sb[:, 40:44], st_sb[:, 36:40], 1.0 / DM, None, ALU.mult))
            SE.wait_ge(sVe.h, ve_q2)
            SE.activation(st_sb[:, 44:48], st_sb[:, 40:44], AF.Ln, bias=EPS_RMS)
            se_r2 = self.sSe.tag(SE.activation(st_sb[:, 44:48], st_sb[:, 44:48], AF.Exp, scale=-0.5))
            VE.wait_ge(sSe.h, se_r2)
            last = None
            for a in range(4):
                last = VE.tensor_scalar(xr[:, a * DM:(a + 1) * DM], xn_sb[:, a * DM:(a + 1) * DM],
                                        st_sb[:, 44 + a:45 + a], None, ALU.mult)
            se_rn0 = self.sVe.tag(last)
            prev["se_rn0"] = se_rn0

            if own_ti is not None:
                SC.wait_ge(sVe.h, se_xn)
                prev["st_xn"] = self.st(
                    D["xnD"].ap()[own_ti * TILE:(own_ti + 1) * TILE, :].rearrange("(a p) d -> p a d", p=128),
                    xn_sb[:, :].rearrange("p (a d) -> p a d", a=4), xn_st)

            rn0T = rn0T_r.acquire(SE)
            PE.wait_ge(sVe.h, se_rn0)
            for a in range(4):
                for dmt in range(8):
                    pt = pt_r.acquire(PE)
                    pe_tr = self.sPe.tag(PE.transpose(
                        pt[:, :], xr[:, a * DM + dmt * 128: a * DM + (dmt + 1) * 128], P["ident"][:, :]))
                    SE.wait_ge(sPe.h, pe_tr)
                    se_ev = self.sSe.tag(SE.activation(
                        rn0T[:, dmt * TILE + a * 128: dmt * TILE + (a + 1) * 128], pt[:, :], AF.Copy))
                    pt_r.release((self.sSe, se_ev))
            se_rn0T = self.sSe.v
            xr_r.release((self.sSe, se_rn0T))

            if own_ti is not None and _F_XNT:
                xnT = xnT_r.acquire(SE)
                for a in range(4):
                    for dmt in range(8):
                        pt = pt_r.acquire(PE)
                        PE.wait_ge(sVe.h, se_xn)
                        pe_tr = self.sPe.tag(PE.transpose(
                            pt[:, :], xn_sb[:, a * DM + dmt * 128: a * DM + (dmt + 1) * 128], P["ident"][:, :]))
                        SE.wait_ge(sPe.h, pe_tr)
                        se_ev = self.sSe.tag(SE.activation(
                            xnT[:, dmt * TILE + a * 128: dmt * TILE + (a + 1) * 128], pt[:, :], AF.Copy))
                        pt_r.release((self.sSe, se_ev))
                se_xnT = self.sSe.v
                SC.wait_ge(sSe.h, se_xnT)
                st_xnT = self.st(
                    D["xnTD"].ap()[:, own_ti * TILE:(own_ti + 1) * TILE].rearrange("(kk p) t -> p kk t", p=128),
                    xnT[:, :].rearrange("p (kk t) -> p kk t", kk=8), xnT_r.stsem())
                self.wt(PE, ld_w0)
                for a in (range(4) if _F_GATE else []):
                    pg = pg_r.acquire(PE)
                    last = None
                    for kk in range(8):
                        last = PE.matmul(pg[:, :],
                                         xnT[:, kk * TILE + a * 128: kk * TILE + (a + 1) * 128],
                                         gatew_sb[:, kk * 8:(kk + 1) * 8],
                                         start=(kk == 0), stop=(kk == 7))
                    pe_g = self.sPe.tag(last)
                    VE.wait_ge(sPe.h, pe_g)
                    VE.tensor_tensor(gsc[:, 0:8], pg[:, :], gateb_sb[:, :], ALU.add)
                    ve_pg = self.sVe.tag(VE.reduce_max(gsc[:, 16:17], gsc[:, 0:8], AX.X))
                    pg_r.release((self.sVe, ve_pg))
                    ve_m = self.sVe.tag(VE.tensor_scalar(gsc[:, 17:18], gsc[:, 16:17], -1.0, None, ALU.mult))
                    SE.wait_ge(sVe.h, ve_m)
                    se_e = self.sSe.tag(SE.activation(gsc[:, 8:16], gsc[:, 0:8], AF.Exp, bias=gsc[:, 17:18]))
                    VE.wait_ge(sSe.h, se_e)
                    VE.tensor_reduce(gsc[:, 19:20], gsc[:, 8:16], AX.X, ALU.add)
                    VE.reciprocal(gsc[:, 20:21], gsc[:, 19:20])
                    VE.tensor_scalar(gsc[:, 8:16], gsc[:, 8:16], gsc[:, 20:21], None, ALU.mult)
                    gw = gsc[:, 8:16]
                    in_j = gw.rearrange("p (o j) -> p o j", o=1).broadcast_to([128, 8, 8])
                    in_e = gw.rearrange("p (e o) -> p e o", o=1).broadcast_to([128, 8, 8])
                    pair = gsc[:, 32:96].rearrange("p (e j) -> p e j", e=8)
                    VE.tensor_tensor(pair, in_j, in_e, ALU.is_gt)
                    VE.tensor_reduce(gsc[:, 96:104], pair, AX.X, ALU.add)
                    VE.tensor_tensor(pair, in_j, in_e, ALU.is_equal)
                    VE.tensor_tensor(gsc[:, 32:96], gsc[:, 32:96], lower_sb[:, :], ALU.mult)
                    VE.tensor_reduce(gsc[:, 104:112], pair, AX.X, ALU.add)
                    VE.tensor_tensor(gsc[:, 96:104], gsc[:, 96:104], gsc[:, 104:112], ALU.add)
                    VE.tensor_scalar(gsc[:, 96:104], gsc[:, 96:104], 2.0, None, ALU.is_lt)
                    tm = own_ti * 4 + a
                    self.sVe.tag(VE.tensor_tensor(P["ccol"][:, tm * 8:(tm + 1) * 8], gw, gsc[:, 96:104], ALU.mult))
                xnT_r.release((self.sPe, self.sPe.v), st_xnT)

            specs = [("xp", D["wxp_bfD"], True)]
            if do_z and _F_Z:
                specs.append(("z", D["wz_bfD"], False))
            pe_xd = None
            for name, wsrc, is_xp in specs:
                for half in range(2):
                    w = w_r.acquire(SY)
                    lw = self.ld(w[:, :].rearrange("p (kk c) -> p kk c", kk=8),
                                 wsrc.ap()[:, half * DM:(half + 1) * DM]
                                 .rearrange("(kk p) c -> p kk c", p=128), w_r.ldsem())
                    self.wt(PE, lw)
                    PE.wait_ge(sSe.h, se_rn0T)
                    last_mm_w = None
                    for gh in range(8):
                        g = half * 8 + gh
                        psx = pa_r.acquire(PE)
                        last = None
                        for kk in range(8):
                            last = PE.matmul(psx[:, :],
                                             w[:, kk * DM + gh * 128: kk * DM + (gh + 1) * 128],
                                             rn0T[:, kk * TILE:(kk + 1) * TILE],
                                             start=(kk == 0), stop=(kk == 7))
                        pe_xp = self.sPe.tag(last)
                        last_mm_w = pe_xp
                        SE.wait_ge(sPe.h, pe_xp)
                        if is_xp:
                            for sem, val in xpg_free[g]:
                                SE.wait_ge(sem.h, val)
                            xpg_free[g] = []
                            se_ev = self.sSe.tag(SE.activation(xpg[g][:, 3:3 + TILE], psx[:, :], AF.Copy))
                            pa_r.release((self.sSe, se_ev))
                            VE.wait_ge(sSe.h, se_ev)
                            xc = xcg_r.acquire(VE)
                            VE.tensor_scalar(xc[:, :], xpg[g][:, 0:TILE], P["convw"][:, g * 4:g * 4 + 1], None, ALU.mult)
                            for t in range(1, 4):
                                VE.scalar_tensor_tensor(xc[:, :], xpg[g][:, t:t + TILE],
                                                        P["convw"][:, g * 4 + t:g * 4 + t + 1], xc[:, :],
                                                        ALU.mult, ALU.add)
                            VE.tensor_scalar(xc[:, :], xc[:, :], P["convb"][:, g:g + 1], None, ALU.add)
                            ve_tail = self.sVe.tag(VE.tensor_copy(xpg[g][:, 0:3], xpg[g][:, TILE:TILE + 3]))
                            SE.wait_ge(sVe.h, ve_tail)
                            se_xc = self.sSe.tag(SE.activation(xc[:, :], xc[:, :], AF.Silu))
                            xpg_free[g] = [(self.sVe, ve_tail)]
                            if spill_col is not None and _F_XDBL:
                                xcbf = xcbf_r.acquire(VE)
                                VE.wait_ge(sSe.h, se_xc)
                                ve_xcb = self.sVe.tag(VE.tensor_copy(xcbf[:, :], xc[:, :]))
                                SC.wait_ge(sSe.h, se_xc)
                                st_xc = self.st(D["xcD"].ap()[g * 128:(g + 1) * 128, spill_col:spill_col + TILE],
                                                xc[:, :], xcg_r.stsem())
                                xc_spill_marks[g] = st_xc
                                xcg_r.release(st_xc)
                                PE.wait_ge(sVe.h, ve_xcb)
                                if g == 0:
                                    for sem, val in px_free:
                                        PE.wait_ge(sem.h, val)
                                    del px_free[:]
                                pe_xd = self.sPe.tag(PE.matmul(px[0:96, :], xprojw_sb[:, g * 96:(g + 1) * 96],
                                                               xcbf[:, :], start=(g == 0), stop=(g == G - 1),
                                                               skip_group_check=True))
                                xcbf_r.release((self.sPe, pe_xd))
                            else:
                                xcg_r.release((self.sSe, se_xc))
                        else:
                            zev = zev_r.acquire(SE)
                            se_ev = self.sSe.tag(SE.activation(zev[:, :], psx[:, :], AF.Copy))
                            pa_r.release((self.sSe, se_ev))
                            SC.wait_ge(sSe.h, se_ev)
                            st_z = self.st(D["zD"].ap()[g * 128:(g + 1) * 128,
                                                        spill_col - PREF:spill_col - PREF + TILE],
                                           zev[:, :], zev_r.stsem())
                            zev_r.release(st_z)
                    w_r.release((self.sPe, last_mm_w))
            rn0T_r.release((self.sPe, self.sPe.v))

            if not do_scan or not _F_SCAN:
                return

            if not _F_XDBL:
                return
            xdbl = xdbl_r.acquire(SE)
            SE.wait_ge(sPe.h, pe_xd)
            se_xd = self.sSe.tag(SE.activation(xdbl[0:96, :], px[0:96, :], AF.Copy))
            px_free.append((self.sSe, se_xd))
            SC.wait_ge(sSe.h, se_xd)
            st_b = self.st(D["BD"].ap()[:, spill_col:spill_col + TILE], xdbl[64:80, :], xdbl_r.stsem())
            if own_ti is not None:
                st_b = self.st(D["CD"].ap()[:, own_ti * TILE:(own_ti + 1) * TILE], xdbl[80:96, :], xdbl_r.stsem())
            xdblbf = xdblbf_r.acquire(VE)
            VE.wait_ge(sSe.h, se_xd)
            ve_xdb = self.sVe.tag(VE.tensor_copy(xdblbf[0:64, :], xdbl[0:64, :]))
            xdbl_r.release((self.sVe, ve_xdb), st_b)

            for g in (range(G) if _F_DT else []):
                psx = pa_r.acquire(PE)
                PE.wait_ge(sVe.h, ve_xdb)
                pe_dt = self.sPe.tag(PE.matmul(psx[:, :], dtww_sb[0:64, g * 128:(g + 1) * 128],
                                               xdblbf[0:64, :], start=True, stop=True))
                if g == G - 1:
                    xdblbf_r.release((self.sPe, pe_dt))
                if _F_DTSUB < 2:
                    se_sp = self.sSe.v
                    ve_dt0 = self.sVe.v
                    pa_r.release((self.sPe, pe_dt))
                    continue
                VE.wait_ge(sPe.h, pe_dt)
                ve_max = self.sVe.tag(VE.tensor_scalar(spm_sb[:, :], psx[:, :], P["dtb"][:, g:g + 1],
                                                       0.0, ALU.add, ALU.max))
                SE.wait_ge(sVe.h, ve_max)
                SE.activation(spl_sb[:, :], psx[:, :], AF.Abs, bias=P["dtb"][:, g:g + 1])
                SE.activation(spl_sb[:, :], spl_sb[:, :], AF.Exp, scale=-1.0)
                se_sp = self.sSe.tag(SE.activation(spl_sb[:, :], spl_sb[:, :], AF.Ln, bias=1.0))
                if _F_DTSUB < 3:
                    pa_r.release((self.sSe, se_sp))
                    continue
                dtg = dtg_r.acquire(VE)
                VE.wait_ge(sSe.h, se_sp)
                VE.tensor_tensor(dtg[:, :], spm_sb[:, :], spl_sb[:, :], ALU.add)
                if mask_col is not None:
                    self.wt(VE, lm)
                    VE.tensor_tensor(dtg[:, :], dtg[:, :], mk[:, :], ALU.mult)
                if _F_DTSUB < 4:
                    prev["ve_dt"] = self.sVe.tag(VE.tensor_copy(dtg[:, 0:1], dtg[:, 0:1]))
                    pa_r.release((self.sSe, se_sp), (self.sVe, prev["ve_dt"]))
                    dtg_r.release((self.sVe, prev["ve_dt"]))
                    continue
                xcl = xcl_r.acquire(SY)
                self.wt(SY, xc_spill_marks[g])
                lxc = self.ld(xcl[:, :], D["xcD"].ap()[g * 128:(g + 1) * 128, spill_col:spill_col + TILE],
                              xcl_r.ldsem())
                dtug = dtug_r.acquire(VE)
                self.wt(VE, lxc)
                ve_dt = self.sVe.tag(VE.tensor_tensor(dtug[:, :], dtg[:, :], xcl[:, :], ALU.mult))
                prev["ve_dt"] = ve_dt
                pa_r.release((self.sSe, se_sp), (self.sVe, ve_dt))
                xcl_r.release((self.sVe, ve_dt))
                SC.wait_ge(sVe.h, ve_dt)
                st1 = self.st(D["dtD"].ap()[g * 128:(g + 1) * 128, spill_col:spill_col + TILE], dtg[:, :], dtg_r.stsem())
                st2 = self.st(D["dtuD"].ap()[g * 128:(g + 1) * 128, spill_col:spill_col + TILE], dtug[:, :], dtug_r.stsem())
                dtg_r.release(st1)
                dtug_r.release(st2)
            if mask_col is not None:
                mask_r.release((self.sVe, self.sVe.v))

        for ti in range(_NT_OWN):
            src = D["x_own"].ap()[ti * TILE:(ti + 1) * TILE, :]
            if ti == 0:
                tile(src, None, None, None, do_scan=False, do_z=False)
            else:
                tile(src, PREF + (ti - 1) * TILE, ti - 1, None, do_scan=True, do_z=True)
        for g in range(G):
            for sem, val in xpg_free[g]:
                if sem is not self.sVe:
                    VE.wait_ge(sem.h, val)
            xpg_free[g] = []
            VE.memset(xpg[g][:, 0:3], 0.0)
        self.sVe.tag(VE.memset(st_sb[:, 0:1], 0.0))
        for ti in range(_NT_PREF):
            src = D["x_pref"].ap()[ti * TILE:(ti + 1) * TILE, :]
            tile(src, ti * TILE, None, ti * TILE, do_scan=True, do_z=False)

        self.join_all()
        es.close()

    def ph2(self):
        D, P = self.dram, self.persist
        es = ExitStack()
        SY, SC, VE, SE, PE, GP = self.SY, self.SC, self.VE, self.SE, self.PE, self.GP
        sLd, sSt, sVe, sSe, sPe, sGp = self.sLd, self.sSt, self.sVe, self.sSe, self.sPe, self.sGp

        dtq_r = self.rot(es, "s_dtq", 2, (128, 4 * CHK))
        dtuq_r = self.rot(es, "s_dtuq", 2, (128, 4 * CHK))
        Bs_r = self.rot(es, "s_B", 2, (128, CHK))
        Cs_r = self.rot(es, "s_C", 1, (128, CHK))
        ab_r = Rot([self.sb(es, f"s_a{i}", (128, CHK)) for i in range(2)])
        bb = self.sb(es, "s_b", (128, CHK))
        hb = self.sb(es, "s_h", (128, CHK))
        tmp = self.sb(es, "s_tmp", (128, CHK))
        yacc = [self.sb(es, f"s_y{i}", (128, CHK)) for i in range(4)]
        carry = self.sb(es, "s_carry", (128, G * DS))
        ygbf_r = self.rot(es, "s_ygbf", 1, (128, CHK), BF16, ld=False, st=True)
        xcg2_r = self.rot(es, "s_xcg", 1, (128, CHK))
        zg2_r = self.rot(es, "s_zg", 1, (128, CHK))

        m_xnT_r = self.rot(es, "m_xnT", 1, (128, 8 * TILE), BF16)
        m_w1_r = self.rot(es, "m_w1", 1, (128, 8 * CHK), BF16)
        m_hT = self.sb(es, "m_hT", (128, 32 * TILE), BF16)
        m_hT_free = []
        m_ctb_r = self.rot(es, "m_ctb", 2, (128, TILE), ld=False, st=True)
        m_b2_r = self.rot(es, "m_b2", 1, (1, TILE))
        m_ps1_r = Rot([self.psum(es, f"m_ps1{i}", (128, TILE)) for i in range(2)])
        m_ps2h = [self.psum(es, f"m_ps2h{i}", (128, TILE)) for i in range(4)]
        m_ps2h_free = [[] for _ in range(4)]
        m_w2b_r = self.rot(es, "m_w2b", 2, (128, 4 * TILE), BF16)

        self.sVe.tag(VE.memset(carry[:, :], 0.0))
        gacc = {}

        def moe_unit(e, nq):
            xnT = m_xnT_r.acquire(SY)
            lxn = self.ld(xnT[:, :].rearrange("p (kk t) -> p kk t", kk=8),
                          D["xnTD"].ap()[:, nq * TILE:(nq + 1) * TILE]
                          .rearrange("(kk p) t -> p kk t", p=128), m_xnT_r.ldsem())

            for sem, val in m_hT_free:
                SE.wait_ge(sem.h, val)
            del m_hT_free[:]
            for hq in range(4):
                w1 = m_w1_r.acquire(SY)
                lw1 = self.ld(w1[:, :].rearrange("p (kk h) -> p kk h", kk=8),
                              D["w1bfD"].ap()[e][:, hq * CHK:(hq + 1) * CHK]
                              .rearrange("(kk p) h -> p kk h", p=128), m_w1_r.ldsem())
                self.wt(PE, lw1, lxn)
                last_w1 = None
                for hm in range(8):
                    hmg = hq * 8 + hm
                    ps1 = m_ps1_r.acquire(PE)
                    last = None
                    for kk in range(8):
                        last = PE.matmul(ps1[:, :],
                                         w1[:, kk * CHK + hm * 128: kk * CHK + (hm + 1) * 128],
                                         xnT[:, kk * TILE:(kk + 1) * TILE],
                                         start=(kk == 0), stop=(kk == 7))
                    pe1 = self.sPe.tag(last)
                    last_w1 = pe1
                    SE.wait_ge(sPe.h, pe1)
                    se_ev = self.sSe.tag(SE.activation(
                        m_hT[:, hmg * TILE:(hmg + 1) * TILE],
                        ps1[:, :], AF.Relu, bias=P["b1"][:, e * 32 + hmg:e * 32 + hmg + 1]))
                    m_ps1_r.release((self.sSe, se_ev))
                m_w1_r.release((self.sPe, last_w1))
            se_h_done = self.sSe.v
            PE.wait_ge(sSe.h, se_h_done)
            for dmh in range(2):
                b2 = m_b2_r.acquire(SY)
                lb2 = self.ld(b2[0:1, :], D["moe_b2_r"].ap()[0:1, e * DM + dmh * TILE:e * DM + (dmh + 1) * TILE],
                              m_b2_r.ldsem())
                self.wt(PE, lb2)
                for hkb in range(8):
                    w2b = m_w2b_r.acquire(SY)
                    lw2 = self.ld(
                        w2b[:, :].rearrange("p (h c) -> p h c", h=4),
                        D["w2bfD"].ap()[e][hkb * 512:(hkb + 1) * 512, dmh * TILE:(dmh + 1) * TILE]
                        .rearrange("(h p) c -> p h c", p=128),
                        m_w2b_r.ldsem())
                    self.wt(PE, lw2)
                    last_b = None
                    for tl in range(4):
                        if hkb == 0:
                            for mk in m_ps2h_free[tl]:
                                PE.wait_ge(mk[0].h, mk[1])
                            m_ps2h_free[tl] = []
                        for hkin in range(4):
                            hk = hkb * 4 + hkin
                            last_b = PE.matmul(m_ps2h[tl][:, :],
                                               m_hT[:, hk * TILE + tl * 128:hk * TILE + (tl + 1) * 128],
                                               w2b[:, hkin * TILE:(hkin + 1) * TILE],
                                               start=(hk == 0), stop=False,
                                               skip_group_check=True)
                    pe_w2 = self.sPe.tag(last_b)
                    m_w2b_r.release((self.sPe, pe_w2))
                for tl in range(4):
                    tokg = nq * 4 + tl
                    pe2 = self.sPe.tag(PE.matmul(m_ps2h[tl][:, :],
                                                 P["ones"][0:1, 0:128],
                                                 b2[0:1, 0:TILE],
                                                 start=False, stop=True, skip_group_check=True))
                    ctb = m_ctb_r.acquire(VE)
                    VE.wait_ge(sPe.h, pe2)
                    ve_ct = self.sVe.tag(VE.tensor_scalar(
                        ctb[:, :], m_ps2h[tl][:, :], P["ccol"][:, tokg * 8 + e:tokg * 8 + e + 1],
                        None, ALU.mult))
                    m_ps2h_free[tl].append((self.sVe, ve_ct))
                    key = (tokg, dmh)
                    if key in gacc:
                        self.wt(GP, gacc[key])
                    GP.wait_ge(sVe.h, ve_ct)
                    csem = m_ctb_r.stsem()
                    GP.dma_start(out=D["gD"].ap()[tokg * 128:(tokg + 1) * 128, dmh * TILE:(dmh + 1) * TILE],
                                 in_=ctb[:, :],
                                 accum_op=(ALU.add if e > 0 else ALU.bypass)).then_inc(csem.h, 16)
                    csem.v += 16
                    gacc[key] = (csem, csem.v)
                    m_ctb_r.release((csem, csem.v))
                m_b2_r.release((self.sPe, self.sPe.v))
            m_hT_free.append((self.sPe, self.sPe.v))
            m_xnT_r.release((self.sPe, self.sPe.v))

        moe_units = iter([(e, nq) for e in range(8) for nq in range(4)] if _STAGE >= 3 else [])

        n_chunks = NTOK // CHK
        for c in range(n_chunks):
            own_c = c >= PREF // CHK
            for q in range(4):
                dtq = dtq_r.acquire(SY)
                ldt = self.ld(dtq[:, :].rearrange("p (g t) -> p g t", g=4),
                              D["dtD"].ap()[q * TILE:(q + 1) * TILE, c * CHK:(c + 1) * CHK]
                              .rearrange("(g p) t -> p g t", p=128), dtq_r.ldsem())
                dtuq = dtuq_r.acquire(SY)
                ldtu = self.ld(dtuq[:, :].rearrange("p (g t) -> p g t", g=4),
                               D["dtuD"].ap()[q * TILE:(q + 1) * TILE, c * CHK:(c + 1) * CHK]
                               .rearrange("(g p) t -> p g t", p=128), dtuq_r.ldsem())
                ve_last_y = 0
                for si in range(DS):
                    Bsb = Bs_r.acquire(SY)
                    lB = self.ld(Bsb[:, :], D["BD"].ap()[si:si + 1, c * CHK:(c + 1) * CHK].squeeze(0)
                                 .partition_broadcast(128), Bs_r.ldsem())
                    if own_c:
                        Csb = Cs_r.acquire(SY)
                        lC = self.ld(Csb[:, :], D["CD"].ap()[si:si + 1, (c - 6) * CHK:(c - 5) * CHK].squeeze(0)
                                     .partition_broadcast(128), Cs_r.ldsem())
                    for gi in range(4):
                        gg = q * 4 + gi
                        col = gg * DS + si
                        ab = ab_r.acquire(SE)
                        self.wt(SE, ldt)
                        se_a = self.sSe.tag(SE.activation(ab[:, :], dtq[:, gi * CHK:(gi + 1) * CHK],
                                                          AF.Exp, scale=P["A"][:, col:col + 1]))
                        self.wt(VE, ldtu, lB)
                        VE.tensor_tensor(bb[:, :], dtuq[:, gi * CHK:(gi + 1) * CHK], Bsb[:, :], ALU.mult)
                        VE.wait_ge(sSe.h, se_a)
                        ve_sc = self.sVe.tag(VE.tensor_tensor_scan(
                            hb[:, :], ab[:, :], bb[:, :], carry[:, col:col + 1], ALU.mult, ALU.add))
                        ab_r.release((self.sVe, ve_sc))
                        VE.tensor_copy(carry[:, col:col + 1], hb[:, CHK - 1:CHK])
                        if own_c:
                            self.wt(VE, lC)
                            if si == 0:
                                ve_last_y = self.sVe.tag(VE.tensor_tensor(yacc[gi][:, :], hb[:, :],
                                                                          Csb[:, :], ALU.mult))
                            else:
                                VE.tensor_tensor(tmp[:, :], hb[:, :], Csb[:, :], ALU.mult)
                                ve_last_y = self.sVe.tag(VE.tensor_tensor(yacc[gi][:, :], yacc[gi][:, :],
                                                                          tmp[:, :], ALU.add))
                        else:
                            ve_last_y = self.sVe.v
                    Bs_r.release((self.sVe, self.sVe.v))
                    if own_c:
                        Cs_r.release((self.sVe, self.sVe.v))
                dtq_r.release((self.sSe, self.sSe.v))
                dtuq_r.release((self.sVe, self.sVe.v))
                if own_c:
                    for gi in range(4):
                        gg = q * 4 + gi
                        ocol = (c - 6) * CHK
                        xcg2 = xcg2_r.acquire(SY)
                        lxc = self.ld(xcg2[:, :], D["xcD"].ap()[gg * 128:(gg + 1) * 128,
                                                                PREF + ocol:PREF + ocol + CHK], xcg2_r.ldsem())
                        zg2 = zg2_r.acquire(SY)
                        lz = self.ld(zg2[:, :], D["zD"].ap()[gg * 128:(gg + 1) * 128, ocol:ocol + CHK], zg2_r.ldsem())
                        self.wt(VE, lxc, lz)
                        VE.scalar_tensor_tensor(yacc[gi][:, :], xcg2[:, :], P["D"][:, gg:gg + 1],
                                                yacc[gi][:, :], ALU.mult, ALU.add)
                        SE.wait_ge(sVe.h, self.sVe.v)
                        se_t = self.sSe.tag(SE.activation(tmp[:, :], zg2[:, :], AF.Silu))
                        xcg2_r.release((self.sVe, self.sVe.v))
                        zg2_r.release((self.sSe, se_t))
                        ygbf = ygbf_r.acquire(VE)
                        VE.wait_ge(sSe.h, se_t)
                        ve_yg = self.sVe.tag(VE.tensor_tensor(ygbf[:, :], yacc[gi][:, :], tmp[:, :], ALU.mult))
                        SC.wait_ge(sVe.h, ve_yg)
                        st_yg = self.st(D["ygD"].ap()[gg * 128:(gg + 1) * 128, ocol:ocol + CHK], ygbf[:, :],
                                        ygbf_r.stsem())
                        ygbf_r.release(st_yg)
                nxt = next(moe_units, None)
                if nxt is not None:
                    moe_unit(*nxt)
        for nxt in moe_units:
            moe_unit(*nxt)

        self.join_all()
        es.close()

    def ph3(self):
        D, P = self.dram, self.persist
        es = ExitStack()
        SY, SC, VE, SE, PE = self.SY, self.SC, self.VE, self.SE, self.PE
        sLd, sSt, sVe, sSe, sPe = self.sLd, self.sSt, self.sVe, self.sSe, self.sPe

        opw = self.sb(es, "f_opw", (128, G * DM), BF16)
        opw_ld = self.dsem("f_opw_ld")
        ygT_r = self.rot(es, "f_ygT", 1, (128, G * TILE), BF16)
        xn4_r = self.rot(es, "f_xn", 1, (128, 4 * DM))
        x1 = self.sb(es, "f_x1", (128, 4 * DM))
        stt = self.sb(es, "f_st", (128, 16))
        rn1_r = Rot(self.sb(es, "f_rn1", (128, DM)))
        rn1T_r = Rot(self.sb(es, "f_rn1T", (128, 8 * TILE), BF16))
        w1c_r = self.rot(es, "f_w1c", 2, (128, 8 * 128), BF16)
        hT = self.sb(es, "f_hT", (128, 32 * TILE), BF16)
        hT_free = []
        w2c_r = self.rot(es, "f_w2c", 2, (128, 8 * TILE), BF16)
        qq = self.sb(es, "f_q", (128, DM))
        x2_r = Rot(self.sb(es, "f_x2", (128, DM)))
        gg_r = self.rot(es, "f_g", 2, (128, DM))
        xo_r = self.rot(es, "f_xo", 2, (128, DM))
        oo_r = self.rot(es, "f_oo", 2, (128, DM), ld=False, st=True)
        pA_r = Rot([self.psum(es, f"f_pA{i}", (128, DM)) for i in range(2)])
        pB_r = Rot([self.psum(es, f"f_pB{i}", (128, TILE)) for i in range(2)])
        pT_r = Rot(self.psum(es, "f_pT", (128, 128)))

        lop = self.ld(opw[:, :].rearrange("p (kk d) -> p kk d", kk=G),
                      D["outproj_bfD"].ap().rearrange("(kk p) d -> p kk d", p=128), opw_ld)

        for nt in range(4):
            ygT = ygT_r.acquire(SY)
            lyg = self.ld(ygT[:, :].rearrange("p (kk t) -> p kk t", kk=G),
                          D["ygD"].ap()[:, nt * TILE:(nt + 1) * TILE]
                          .rearrange("(kk p) t -> p kk t", p=128), ygT_r.ldsem())
            xn4 = xn4_r.acquire(SY)
            lxn = self.ld(xn4[:, :].rearrange("p (a d) -> p a d", a=4),
                          D["xnD"].ap()[nt * TILE:(nt + 1) * TILE, :]
                          .rearrange("(a p) d -> p a d", p=128), xn4_r.ldsem())
            self.wt(PE, lyg, lxn, lop)
            for a in range(4):
                pA = pA_r.acquire(PE)
                last = None
                for kk in range(G):
                    for half in range(2):
                        last = PE.matmul(pA[:, half * TILE:(half + 1) * TILE],
                                         ygT[:, kk * TILE + a * 128: kk * TILE + (a + 1) * 128],
                                         opw[:, kk * DM + half * TILE: kk * DM + (half + 1) * TILE],
                                         start=(kk == 0), stop=(kk == G - 1 and half == 1))
                pe_op = self.sPe.tag(last)
                VE.wait_ge(sPe.h, pe_op)
                self.wt(VE, lxn)
                ve_x1 = self.sVe.tag(VE.tensor_tensor(
                    x1[:, a * DM:(a + 1) * DM], xn4[:, a * DM:(a + 1) * DM], pA[:, :], ALU.add))
                pA_r.release((self.sVe, ve_x1))
            ve_x1_all = self.sVe.v
            ygT_r.release((self.sPe, self.sPe.v))
            xn4_r.release((self.sVe, ve_x1_all))

            rn1T = rn1T_r.acquire(SE)
            for a in range(4):
                rn1 = rn1_r.acquire(SE)
                SE.wait_ge(sVe.h, ve_x1_all)
                se_sq = self.sSe.tag(SE.activation(rn1[:, :], x1[:, a * DM:(a + 1) * DM], AF.Square))
                VE.wait_ge(sSe.h, se_sq)
                ve_q = self.sVe.tag(VE.tensor_reduce(
                    stt[:, 0:1], rn1[:, :].rearrange("p (o d) -> p o d", o=1), AX.X, ALU.add))
                SE.wait_ge(sVe.h, ve_q)
                SE.activation(stt[:, 1:2], stt[:, 0:1], AF.Ln, bias=EPS_RMS, scale=1.0 / DM)
                self.sSe.tag(SE.activation(stt[:, 1:2], stt[:, 1:2], AF.Exp, scale=-0.5))
                se_r1 = self.sSe.v
                VE.wait_ge(sSe.h, se_r1)
                se_rn = self.sVe.tag(VE.tensor_scalar(rn1[:, :], x1[:, a * DM:(a + 1) * DM],
                                                      stt[:, 1:2], None, ALU.mult))
                PE.wait_ge(sVe.h, se_rn)
                for dmt in range(8):
                    pT = pT_r.acquire(PE)
                    pe_tr = self.sPe.tag(PE.transpose(pT[:, :], rn1[:, dmt * 128:(dmt + 1) * 128],
                                                      P["ident"][:, :]))
                    SE.wait_ge(sPe.h, pe_tr)
                    se_ev = self.sSe.tag(SE.activation(
                        rn1T[:, dmt * TILE + a * 128: dmt * TILE + (a + 1) * 128], pT[:, :], AF.Copy))
                    pT_r.release((self.sSe, se_ev))
                rn1_r.release((self.sPe, self.sPe.v))
            se_rn1T = self.sSe.v

            for sem, val in hT_free:
                VE.wait_ge(sem.h, val)
            del hT_free[:]
            PE.wait_ge(sSe.h, se_rn1T)
            for hm in range(32):
                w1c = w1c_r.acquire(SY)
                lw1 = self.ld(w1c[:, :].rearrange("p (kk h) -> p kk h", kk=8),
                              D["mlpw1_bfD"].ap()[:, hm * 128:(hm + 1) * 128]
                              .rearrange("(kk p) h -> p kk h", p=128), w1c_r.ldsem())
                self.wt(PE, lw1)
                pB = pB_r.acquire(PE)
                last = None
                for kk in range(8):
                    last = PE.matmul(pB[:, :], w1c[:, kk * 128:(kk + 1) * 128],
                                     rn1T[:, kk * TILE:(kk + 1) * TILE],
                                     start=(kk == 0), stop=(kk == 7))
                pe_m1 = self.sPe.tag(last)
                w1c_r.release((self.sPe, pe_m1))
                SE.wait_ge(sPe.h, pe_m1)
                se_h = self.sSe.tag(SE.activation(
                    hT[:, hm * TILE:(hm + 1) * TILE], pB[:, :], AF.Silu))
                pB_r.release((self.sSe, se_h))
            ve_hT = self.sSe.v
            rn1T_r.release((self.sPe, self.sPe.v))

            PE.wait_ge(sSe.h, ve_hT)
            for a in range(4):
                tokg = nt * 4 + a
                pA = pA_r.acquire(PE)
                last = None
                for dmh in range(2):
                    for hkb in range(4):
                        w2c = w2c_r.acquire(SY)
                        lw2 = self.ld(
                            w2c[:, :].rearrange("p (h c) -> p h c", h=8),
                            D["mlpw2_bfD"].ap()[hkb * 1024:(hkb + 1) * 1024, dmh * TILE:(dmh + 1) * TILE]
                            .rearrange("(h p) c -> p h c", p=128),
                            w2c_r.ldsem())
                        self.wt(PE, lw2)
                        for hkin in range(8):
                            hk = hkb * 8 + hkin
                            last = PE.matmul(pA[:, dmh * TILE:(dmh + 1) * TILE],
                                             hT[:, hk * TILE + a * 128: hk * TILE + (a + 1) * 128],
                                             w2c[:, hkin * TILE:(hkin + 1) * TILE],
                                             start=(hk == 0), stop=(hk == 31),
                                             skip_group_check=True)
                        pe_w2 = self.sPe.tag(last)
                        w2c_r.release((self.sPe, pe_w2))
                pe_m2 = self.sPe.v
                gg = gg_r.acquire(SY)
                lg = self.ld(gg[:, :], D["gD"].ap()[tokg * 128:(tokg + 1) * 128, :], gg_r.ldsem())
                xo = xo_r.acquire(SY)
                lx = self.ld(xo[:, :], D["x_own"].ap()[TILE + tokg * 128: TILE + (tokg + 1) * 128, :], xo_r.ldsem())
                x2 = x2_r.acquire(VE)
                VE.wait_ge(sPe.h, pe_m2)
                ve_x2 = self.sVe.tag(VE.tensor_tensor(x2[:, :], x1[:, a * DM:(a + 1) * DM], pA[:, :], ALU.add))
                pA_r.release((self.sVe, ve_x2))
                oo = oo_r.acquire(SE)
                self.wt(SE, lg)
                se_t = self.sSe.tag(SE.activation(oo[:, :], gg[:, :], AF.Silu))
                gg_r.release((self.sSe, se_t))
                VE.wait_ge(sSe.h, se_t)
                VE.tensor_tensor(oo[:, :], oo[:, :], x2[:, :], ALU.mult)
                x2_r.release((self.sVe, self.sVe.v))
                self.wt(VE, lx)
                ve_o = self.sVe.tag(VE.tensor_tensor(oo[:, :], oo[:, :], xo[:, :], ALU.add))
                xo_r.release((self.sVe, ve_o))
                SC.wait_ge(sVe.h, ve_o)
                st_o = self.st(D["o_out"].ap()[tokg * 128:(tokg + 1) * 128, :], oo[:, :], oo_r.stsem())
                oo_r.release(st_o)

        if self.sSt.v:
            SC.wait_ge(self.sSt.h, self.sSt.v)
        for s in self.all_dma_sems:
            if s.v:
                SC.wait_ge(s.h, s.v)
        es.close()

    def finish(self):
        self.es.close()


def build_program():
    nc = bass.Bass("TRN2", target_bir_lowering=False)
    b = Builder(nc)
    b.ph0()
    if _STAGE >= 1:
        b.ph1()
    if _STAGE >= 2:
        b.ph2()
    if _STAGE >= 3:
        b.ph3()
    else:
        with nc.sbuf_tensor("z_oo", [128, DM], F32) as zoo:
            zv = b.sVe.tag(b.VE.memset(zoo[:, :], 0.0))
            b.SC.wait_ge(b.sVe.h, zv)
            for t in range(OWN // 128):
                b.st(b.dram["o_out"].ap()[t * 128:(t + 1) * 128, :], zoo[:, :])
            b.SC.wait_ge(b.sSt.h, b.sSt.v)
    if _DBG:
        D = b.dram
        b.join_all()
        pairs = [("d_dtp", D["dtD"].ap()[:, 0:2048]), ("d_dto", D["dtD"].ap()[:, PREF:]),
                 ("d_xc", D["xcD"].ap()[:, PREF:]), ("d_z", D["zD"].ap()),
                 ("d_xn", D["xnD"].ap()), ("d_B", D["BD"].ap()), ("d_C", D["CD"].ap())]
        if _STAGE >= 2:
            pairs.append(("d_yg", D["ygD"].ap()))
        if _STAGE >= 3:
            pairs.append(("d_g", D["gD"].ap()))
        for nm, srcap in pairs:
            b.st(D[nm].ap(), srcap)
        zv = b.sVe.tag(b.VE.tensor_copy(b.persist["ccol"][:, 0:1], b.persist["ccol"][:, 0:1]))
        b.SC.wait_ge(b.sVe.h, zv)
        b.st(D["d_cc"].ap()[:, 0:128], b.persist["ccol"][:, 0:128])
        b.SC.wait_ge(b.sSt.h, b.sSt.v)
    b.finish()
    return nc


def make_inmaps(inputs):
    x = np.ascontiguousarray(inputs["L0_attn_x"], dtype=np.float32)
    ipw = np.ascontiguousarray(inputs["L0_attn_in_proj_w"], dtype=np.float32)
    conv_w = np.ascontiguousarray(inputs["L0_attn_conv_w"], dtype=np.float32)
    conv_b = np.ascontiguousarray(inputs["L0_attn_conv_b"], dtype=np.float32)
    xpw = np.ascontiguousarray(inputs["L0_attn_x_proj_w"], dtype=np.float32)
    dtw = np.ascontiguousarray(inputs["L0_attn_dt_proj_w"], dtype=np.float32)
    dtb = np.ascontiguousarray(inputs["L0_attn_dt_proj_b"], dtype=np.float32)
    alog = np.ascontiguousarray(inputs["L0_attn_A_log"], dtype=np.float32)
    dd = np.ascontiguousarray(inputs["L0_attn_D"], dtype=np.float32)
    opw = np.ascontiguousarray(inputs["L0_attn_out_proj_w"], dtype=np.float32)
    mw1 = np.ascontiguousarray(inputs["L0_attn_mlp_w1"], dtype=np.float32)
    mw2 = np.ascontiguousarray(inputs["L0_attn_mlp_w2"], dtype=np.float32)
    gw = np.ascontiguousarray(inputs["L0_ffn_gate_w"], dtype=np.float32)
    gb = np.ascontiguousarray(inputs["L0_ffn_gate_b"], dtype=np.float32)
    w1 = np.ascontiguousarray(inputs["L0_ffn_w1"], dtype=np.float32)
    b1 = np.ascontiguousarray(inputs["L0_ffn_b1"], dtype=np.float32)
    w2 = np.ascontiguousarray(inputs["L0_ffn_w2"], dtype=np.float32)
    b2 = np.ascontiguousarray(inputs["L0_ffn_b2"], dtype=np.float32)

    low = np.zeros((64,), np.float32)
    for e in range(8):
        for j in range(8):
            low[e * 8 + j] = 1.0 if j < e else 0.0

    shared = dict(
        wxp_t=np.ascontiguousarray(ipw[:DI].T),
        wz_t=np.ascontiguousarray(ipw[DI:].T),
        convw_c=np.ascontiguousarray(conv_w.reshape(G, 128, 4).transpose(1, 0, 2).reshape(128, G * 4)),
        convb_c=np.ascontiguousarray(conv_b.reshape(G, 128).T),
        xproj_t=np.ascontiguousarray(xpw.T),
        dtw_t=np.ascontiguousarray(dtw.T),
        dtb_c=np.ascontiguousarray(dtb.reshape(G, 128).T),
        alog_c=np.ascontiguousarray(alog.reshape(G, 128, DS).transpose(1, 0, 2).reshape(128, G * DS)),
        d_c=np.ascontiguousarray(dd.reshape(G, 128).T),
        outproj_t=np.ascontiguousarray(opw.T),
        mlpw1_t=np.ascontiguousarray(mw1.T),
        mlpw2_t=np.ascontiguousarray(mw2.T),
        gate_w=gw,
        gate_b_r=np.ascontiguousarray(np.repeat(gb[None, :], 128, 0)),
        moe_w1=w1,
        moe_b1_c=np.ascontiguousarray(b1.reshape(8, 32, 128).transpose(2, 0, 1).reshape(128, 8 * 32)),
        moe_w2=w2,
        moe_b2_r=np.ascontiguousarray(b2.reshape(1, 8 * DM)),
        const_v=np.ascontiguousarray(np.repeat(np.array([CONSTS], np.float32), 128, 0)),
        ident=np.eye(128, dtype=np.float32),
        ones_r=np.ones((128, 128), np.float32),
        lower_r=np.ascontiguousarray(np.repeat(low[None, :], 128, 0)),
    )

    in_maps = []
    for c in range(8):
        b_, k = c // 4, c % 4
        Pk = k * OWN
        ext = np.zeros((TILE, DM), np.float32) if k == 0 else x[b_, Pk - TILE:Pk]
        m = dict(shared)
        m["x_pref"] = np.ascontiguousarray(x[b_, :PREF])
        m["x_own"] = np.ascontiguousarray(np.concatenate([ext, x[b_, Pk:Pk + OWN]], axis=0))
        m["mask_rep"] = np.ascontiguousarray(
            np.repeat((np.arange(PREF) < Pk).astype(np.float32)[None, :], 128, 0))
        in_maps.append(m)
    return in_maps


_NC_CACHE = [None]


def kernel(**inputs):
    in_maps = make_inmaps(inputs)
    if _NC_CACHE[0] is None:
        _NC_CACHE[0] = build_program()
    nc = _NC_CACHE[0]
    res = run_bass_kernel_spmd(nc, in_maps, core_ids=list(range(8)))
    out = np.empty((2, L, DM), np.float32)
    for c in range(8):
        b_, k = c // 4, c % 4
        out[b_, k * OWN:(k + 1) * OWN] = np.asarray(res.results[c]["o_out"])
    return out


# version 07
# speedup vs baseline: 1.9301x; 1.3745x over previous
import os
import numpy as np
from contextlib import ExitStack

import concourse.bass as bass
import concourse.mybir as mybir
from concourse.bass_utils import run_bass_kernel_spmd

F32 = mybir.dt.float32
BF16 = mybir.dt.bfloat16
AF = mybir.ActivationFunctionType
ALU = mybir.AluOpType
AX = mybir.AxisListType

L = 8192
DM = 1024
DI = 2048
DS = 16
DTR = 64
OWN = 2048
PREF = 6144
TILE = 512
G = 16
CHK = 1024
NTOK = OWN
EPS_LN = 1e-5
EPS_RMS = 1.1920929e-07

_STAGE = int(os.environ.get("MK_STAGE", "4"))
_DBG = os.environ.get("MK_DBG", "0") == "1"
_NT_OWN = int(os.environ.get("MK_NT_OWN", "5"))
_F_XNT = os.environ.get("MK_F_XNT", "1") == "1"
_F_GATE = os.environ.get("MK_F_GATE", "1") == "1"
_F_Z = os.environ.get("MK_F_Z", "1") == "1"
_F_SCAN = os.environ.get("MK_F_SCAN", "1") == "1"
_F_XDBL = os.environ.get("MK_F_XDBL", "1") == "1"
_F_DT = os.environ.get("MK_F_DT", "1") == "1"
_F_DTSUB = int(os.environ.get("MK_F_DTSUB", "4"))
_NT_PREF = int(os.environ.get("MK_NT_PREF", "12"))
CONSTS = [EPS_LN, EPS_RMS, 1.0, 0.5, -0.5, -1.0, 0.0, 2.0]


class Sem:
    def __init__(self, es, nc, name, auto=False):
        self.h = es.enter_context(nc.semaphore(name))
        self.v = 0
        self.auto = auto

    def tag(self, instr, n=1):
        if self.auto:
            return self.v
        instr.then_inc(self.h, n)
        self.v += n
        return self.v


class SelfSync:

    _PASSTHRU = {"wait_ge", "dma_start"}

    def __init__(self, eng, sem):
        self._eng = eng
        self._sem = sem

    def wait_ge(self, h, v):
        return self._eng.wait_ge(h, v)

    def dma_start(self, *a, **k):
        return self._eng.dma_start(*a, **k)

    def __getattr__(self, name):
        f = getattr(self._eng, name)
        if not callable(f):
            return f
        sem = self._sem
        eng = self._eng

        def wrapped(*a, **k):
            if sem.v:
                eng.wait_ge(sem.h, sem.v)
            instr = f(*a, **k)
            if hasattr(instr, "then_inc"):
                instr.then_inc(sem.h, 1)
                sem.v += 1
            return instr

        return wrapped


class Rot:

    def __init__(self, bufs, ldsems=None, stsems=None):
        self.bufs = bufs if isinstance(bufs, list) else [bufs]
        self.free = [[] for _ in self.bufs]
        self.i = -1
        self.ldsems = ldsems
        self.stsems = stsems

    def acquire(self, writer):
        self.i += 1
        j = self.i % len(self.bufs)
        for sem, val in self.free[j]:
            if val:
                writer.wait_ge(sem.h, val)
        self.free[j] = []
        return self.bufs[j]

    def ldsem(self):
        return self.ldsems[self.i % len(self.bufs)]

    def stsem(self):
        return self.stsems[self.i % len(self.bufs)]

    def cur(self):
        return self.bufs[self.i % len(self.bufs)]

    def release(self, *marks):
        j = self.i % len(self.bufs)
        self.free[j].extend(marks)


class Builder:
    def __init__(self, nc):
        self.nc = nc
        self.es = ExitStack()
        es = self.es
        self.SY = nc.sync
        self.SC = nc.scalar
        self.VE = nc.vector
        self.SE = nc.scalar
        self.PE = nc.tensor
        self.GP = nc.gpsimd

        self.sLd = Sem(es, nc, "sLd")
        self.sSt = Sem(es, nc, "sSt")
        self.sGp = Sem(es, nc, "sGp")
        self.sPe = Sem(es, nc, "sPe")
        self.sVe = Sem(es, nc, "sVe", auto=True)
        self.sSe = Sem(es, nc, "sSe", auto=True)
        self.VE = SelfSync(self.VE, self.sVe)
        self.SE = SelfSync(self.SE, self.sSe)

        self.all_dma_sems = []
        self.dram = {}
        self.decl_dram()
        self.persist = {}
        P = self.persist
        P["A"] = self.sb(es, "A_sb", (128, G * DS))
        P["convw"] = self.sb(es, "convw_sb", (128, G * 4))
        P["convb"] = self.sb(es, "convb_sb", (128, G))
        P["dtb"] = self.sb(es, "dtb_sb", (128, G))
        P["D"] = self.sb(es, "d_sb", (128, G))
        P["ident"] = self.sb(es, "ident_sb", (128, 128))
        P["ones"] = self.sb(es, "ones_sb", (128, 128))
        P["ccol"] = self.sb(es, "ccol_sb", (128, 16 * 8))
        P["b1"] = self.sb(es, "b1_sb", (128, 8 * 32))
        P["cv"] = self.sb(es, "cv_sb", (128, 8))
        P["negA"] = self.sb(es, "negA_sb", (128, G * DS))
        P["ssel"] = self.sb(es, "ssel_sb", (128, 4))
        P["carry0"] = self.sb(es, "carry0_sb", (128, G * DS))

    def decl_dram(self):
        nc = self.nc
        D = self.dram

        def inp(name, shape, dtype=F32):
            D[name] = nc.dram_tensor(name, list(shape), dtype, kind="ExternalInput")

        inp("x_own", (TILE + OWN, DM))
        inp("slot_sel", (128, 4))
        inp("wxp_t", (DM, DI))
        inp("wz_t", (DM, DI))
        inp("convw_c", (128, G * 4))
        inp("convb_c", (128, G))
        inp("xproj_t", (DI, 96))
        inp("dtw_t", (DTR, DI))
        inp("dtb_c", (128, G))
        inp("alog_c", (128, G * DS))
        inp("d_c", (128, G))
        inp("outproj_t", (DI, DM))
        inp("mlpw1_t", (DM, 4 * DM))
        inp("mlpw2_t", (4 * DM, DM))
        inp("gate_w", (DM, 8))
        inp("gate_b_r", (128, 8))
        inp("moe_w1", (8, DM, 4 * DM))
        inp("moe_b1_c", (128, 8 * 32))
        inp("moe_w2", (8, 4 * DM, DM))
        inp("moe_b2_r", (1, 8 * DM))
        inp("const_v", (128, 8))
        inp("ident", (128, 128))
        inp("ones_r", (128, 128))
        inp("lower_r", (128, 64))

        D["o_out"] = nc.dram_tensor("o_out", [OWN, DM], F32, kind="ExternalOutput")
        if _DBG:
            for nm, shape, dt in [("d_dtp", (DI, 2048), F32), ("d_dto", (DI, OWN), F32),
                                  ("d_xc", (DI, OWN), F32), ("d_z", (DI, OWN), F32),
                                  ("d_xn", (OWN, DM), F32), ("d_B", (DS, NTOK), F32),
                                  ("d_C", (DS, OWN), F32), ("d_yg", (DI, OWN), BF16),
                                  ("d_g", (OWN, DM), F32), ("d_cc", (128, 128), F32)]:
                D[nm] = nc.dram_tensor(nm, list(shape), dt, kind="ExternalOutput")

        def scr(name, shape, dtype=F32):
            D[name] = nc.dram_tensor(name, list(shape), dtype, kind="Internal")

        scr("dtD", (DI, NTOK))
        scr("dtuD", (DI, NTOK))
        scr("BD", (DS, NTOK))
        scr("CD", (DS, OWN))
        scr("ccinD", (2 * 128, G * DS))
        scr("ccoutD", (8 * 128, G * DS))
        scr("xcD", (DI, NTOK))
        scr("zD", (DI, OWN))
        scr("xnD", (OWN, DM))
        scr("xnTD", (DM, OWN), BF16)
        scr("ygD", (DI, OWN), BF16)
        scr("gD", (OWN, DM))
        scr("wxp_bfD", (DM, DI), BF16)
        scr("wz_bfD", (DM, DI), BF16)
        scr("xproj_bfD", (DI, 96), BF16)
        scr("dtw_bfD", (DTR, DI), BF16)
        scr("outproj_bfD", (DI, DM), BF16)
        scr("mlpw1_bfD", (DM, 4 * DM), BF16)
        scr("mlpw2_bfD", (4 * DM, DM), BF16)
        scr("gate_bfD", (DM, 8), BF16)
        scr("w1bfD", (8, DM, 4 * DM), BF16)
        scr("w2bfD", (8, 4 * DM, DM), BF16)

    def sb(self, es, name, shape, dtype=F32):
        return es.enter_context(self.nc.sbuf_tensor(name, list(shape), dtype))

    def psum(self, es, name, shape, dtype=F32):
        return es.enter_context(self.nc.psum_tensor(name, list(shape), dtype))

    def dsem(self, name):
        s = Sem(self.es, self.nc, name)
        self.all_dma_sems.append(s)
        return s

    def rot(self, es, name, n, shape, dtype=F32, ld=True, st=False):
        bufs = [self.sb(es, f"{name}{i}", shape, dtype) for i in range(n)]
        lds = [self.dsem(f"{name}{i}_ld") for i in range(n)] if ld else None
        sts = [self.dsem(f"{name}{i}_st") for i in range(n)] if st else None
        return Rot(bufs, lds, sts)

    def join_all(self):
        vals = [(s.h, s.v) for s in (self.sLd, self.sSt, self.sGp, self.sPe, self.sVe, self.sSe)]
        vals += [(s.h, s.v) for s in self.all_dma_sems if s.v]
        for eng in (self.SY, self.SC, self.VE, self.PE, self.GP):
            for h, v in vals:
                if v:
                    eng.wait_ge(h, v)

    def ld(self, out_ap, in_ap, sem=None):
        sem = sem or self.sLd
        self.SY.dma_start(out=out_ap, in_=in_ap).then_inc(sem.h, 16)
        sem.v += 16
        return (sem, sem.v)

    def st(self, out_ap, in_ap, sem=None):
        sem = sem or self.sSt
        self.SC.dma_start(out=out_ap, in_=in_ap).then_inc(sem.h, 16)
        sem.v += 16
        return (sem, sem.v)

    @staticmethod
    def wt(eng, *marks):
        for m in marks:
            if m:
                eng.wait_ge(m[0].h, m[1])

    def ph0(self):
        D, P = self.dram, self.persist
        GP, SE = self.GP, self.SE

        def cast(dst, src, nsplit=2):
            rows = src.shape[0]
            step = rows // nsplit
            for i in range(nsplit):
                GP.dma_start(out=dst.ap()[i * step:(i + 1) * step, :],
                             in_=src.ap()[i * step:(i + 1) * step, :]).then_inc(self.sGp.h, 16)
                self.sGp.v += 16

        cast(D["wxp_bfD"], D["wxp_t"])
        cast(D["wz_bfD"], D["wz_t"])
        cast(D["xproj_bfD"], D["xproj_t"], 1)
        cast(D["dtw_bfD"], D["dtw_t"], 1)
        cast(D["outproj_bfD"], D["outproj_t"])
        cast(D["mlpw1_bfD"], D["mlpw1_t"])
        cast(D["mlpw2_bfD"], D["mlpw2_t"])
        cast(D["gate_bfD"], D["gate_w"], 1)
        for e in range(8):
            GP.dma_start(out=D["w1bfD"].ap()[e], in_=D["moe_w1"].ap()[e]).then_inc(self.sGp.h, 16)
            self.sGp.v += 16
            GP.dma_start(out=D["w2bfD"].ap()[e], in_=D["moe_w2"].ap()[e]).then_inc(self.sGp.h, 16)
            self.sGp.v += 16
        self.gp_casts = self.sGp.v

        self.cA = self.dsem("cA")
        self.ld(P["cv"][:, :], D["const_v"].ap(), self.cA)
        for i, v in enumerate(CONSTS):
            self.nc.const_aps.register(F32, v, P["cv"][:, i:i + 1])
        for dst, src in [(P["convw"], "convw_c"), (P["convb"], "convb_c"),
                         (P["dtb"], "dtb_c"), (P["D"], "d_c"),
                         (P["ident"], "ident"), (P["ones"], "ones_r"),
                         (P["b1"], "moe_b1_c"), (P["A"], "alog_c"),
                         (P["ssel"], "slot_sel")]:
            self.ld(dst[:, :], D[src].ap(), self.cA)
        SE.wait_ge(self.cA.h, self.cA.v)
        SE.activation(P["A"][:, :], P["A"][:, :], AF.Exp)
        SE.activation(P["negA"][:, :], P["A"][:, :], AF.Copy)
        self.sSe.tag(SE.activation(P["A"][:, :], P["A"][:, :], AF.Copy, scale=-1.0))

    def ph1(self):
        D, P = self.dram, self.persist
        es = ExitStack()
        SY, SC, VE, SE, PE = self.SY, self.SC, self.VE, self.SE, self.PE
        sLd, sSt, sVe, sSe, sPe = self.sLd, self.sSt, self.sVe, self.sSe, self.sPe

        xr_r = self.rot(es, "t_xr", 2, (128, 4 * DM))
        xn_sb = self.sb(es, "t_xn", (128, 4 * DM))
        xn_st = self.dsem("t_xn_st")
        st_sb = self.sb(es, "t_st", (128, 64))
        rn0T_r = Rot([self.sb(es, f"t_rn0T{i}", (128, 8 * TILE), BF16) for i in range(2)])
        xpg = [self.sb(es, f"t_xpg{g}", (128, TILE + 3)) for g in range(G)]
        xpg_free = [[] for _ in range(G)]
        xcg_r = self.rot(es, "t_xcg", 2, (128, TILE), ld=False, st=True)
        xcbf_r = Rot([self.sb(es, f"t_xcbf{i}", (128, TILE), BF16) for i in range(2)])
        zev_r = self.rot(es, "t_zev", 2, (128, TILE), ld=False, st=True)
        xdbl_r = self.rot(es, "t_xdbl", 2, (96, TILE), ld=False, st=True)
        xdblbf_r = Rot([self.sb(es, f"t_xdblbf{i}", (64, TILE), BF16) for i in range(2)])
        dtg_r = self.rot(es, "t_dtg", 2, (128, TILE), ld=False, st=True)
        dtug_r = self.rot(es, "t_dtug", 2, (128, TILE), ld=False, st=True)
        xcl_r = self.rot(es, "t_xcl", 2, (128, TILE))
        spm_sb = self.sb(es, "t_spm", (128, TILE))
        spl_sb = self.sb(es, "t_spl", (128, TILE))
        w_r = self.rot(es, "t_w", 2, (128, 8 * DM), BF16)
        gsc = self.sb(es, "t_gsc", (128, 128))
        lower_sb = self.sb(es, "t_lower", (128, 64))
        gateb_sb = self.sb(es, "t_gateb", (128, 8))
        gatew_sb = self.sb(es, "t_gatew", (128, 8 * 8), BF16)
        xnT_r = self.rot(es, "t_xnT", 1, (128, 8 * TILE), BF16, ld=False, st=True)
        xprojw_sb = self.sb(es, "t_xprojw", (128, G * 96), BF16)
        dtww_sb = self.sb(es, "t_dtww", (64, DI), BF16)

        pa_r = Rot([self.psum(es, f"t_pa{i}", (128, TILE)) for i in range(2)])
        px = self.psum(es, "t_px", (128, TILE))
        px_free = []
        pt_r = Rot([self.psum(es, f"t_pt{i}", (128, 128)) for i in range(2)])
        pg_r = Rot(self.psum(es, "t_pg", (128, 8)))

        SY.wait_ge(self.sGp.h, self.gp_casts)
        cB = self.dsem("cB")
        self.ld(xprojw_sb[:, :].rearrange("p (g r) -> p g r", r=96), D["xproj_bfD"].ap().rearrange("(g p) r -> p g r", p=128), cB)
        self.ld(dtww_sb[0:64, :], D["dtw_bfD"].ap(), cB)
        self.ld(gatew_sb[:, :].rearrange("p (kk e) -> p kk e", e=8), D["gate_bfD"].ap().rearrange("(kk p) e -> p kk e", p=128), cB)
        self.ld(gateb_sb[:, :], D["gate_b_r"].ap(), cB)
        self.ld(lower_sb[:, :], D["lower_r"].ap(), cB)
        ld_w0 = (cB, cB.v)
        cA2 = (self.cA, self.cA.v)

        self.wt(VE, ld_w0, cA2)
        self.wt(PE, ld_w0, cA2)
        for g in range(G):
            VE.memset(xpg[g][:, 0:3], 0.0)
        self.sVe.tag(VE.memset(st_sb[:, 0:1], 0.0))
        prev = {"se_rn0": 0, "ve_dt": 0, "st_xn": 0}

        xc_spill_marks = [None] * G

        def tile(src_ap, spill_col, own_ti, mask_col, do_scan, do_z):
            xr = xr_r.acquire(SY)
            lx = self.ld(xr[:, :].rearrange("p (a d) -> p a d", a=4), src_ap.rearrange("(a p) d -> p a d", p=128),
                         xr_r.ldsem())

            self.wt(VE, lx)
            VE.tensor_reduce(st_sb[:, 0:4], xr[:, :].rearrange("p (a d) -> p a d", a=4), AX.X, ALU.add)
            self.sVe.tag(VE.tensor_scalar(st_sb[:, 8:12], st_sb[:, 0:4], 1.0 / DM, None, ALU.mult))
            self.wt(SE, lx)
            if prev["st_xn"]:
                self.wt(SE, prev["st_xn"])
            se_sq = self.sSe.tag(SE.activation(xn_sb[:, :], xr[:, :], AF.Square))
            VE.wait_ge(sSe.h, se_sq)
            VE.tensor_reduce(st_sb[:, 4:8], xn_sb[:, :].rearrange("p (a d) -> p a d", a=4), AX.X, ALU.add)
            VE.tensor_scalar(st_sb[:, 12:16], st_sb[:, 4:8], 1.0 / DM, None, ALU.mult)
            VE.tensor_tensor(st_sb[:, 16:20], st_sb[:, 8:12], st_sb[:, 8:12], ALU.mult)
            ve_var = self.sVe.tag(VE.tensor_tensor(st_sb[:, 20:24], st_sb[:, 12:16], st_sb[:, 16:20], ALU.subtract))
            SE.wait_ge(sVe.h, ve_var)
            SE.activation(st_sb[:, 24:28], st_sb[:, 20:24], AF.Ln, bias=EPS_LN)
            se_r = self.sSe.tag(SE.activation(st_sb[:, 24:28], st_sb[:, 24:28], AF.Exp, scale=-0.5))
            VE.wait_ge(sSe.h, se_r)
            last = None
            for a in range(4):
                last = VE.tensor_scalar(xn_sb[:, a * DM:(a + 1) * DM], xr[:, a * DM:(a + 1) * DM],
                                        st_sb[:, 8 + a:9 + a], st_sb[:, 24 + a:25 + a],
                                        ALU.subtract, ALU.mult)
            se_xn = self.sVe.tag(last)
            SE.wait_ge(sVe.h, se_xn)
            se_sq2 = self.sSe.tag(SE.activation(xr[:, :], xn_sb[:, :], AF.Square))
            VE.wait_ge(sSe.h, se_sq2)
            VE.tensor_reduce(st_sb[:, 36:40], xr[:, :].rearrange("p (a d) -> p a d", a=4), AX.X, ALU.add)
            ve_q2 = self.sVe.tag(VE.tensor_scalar(st_sb[:, 40:44], st_sb[:, 36:40], 1.0 / DM, None, ALU.mult))
            SE.wait_ge(sVe.h, ve_q2)
            SE.activation(st_sb[:, 44:48], st_sb[:, 40:44], AF.Ln, bias=EPS_RMS)
            se_r2 = self.sSe.tag(SE.activation(st_sb[:, 44:48], st_sb[:, 44:48], AF.Exp, scale=-0.5))
            VE.wait_ge(sSe.h, se_r2)
            last = None
            for a in range(4):
                last = VE.tensor_scalar(xr[:, a * DM:(a + 1) * DM], xn_sb[:, a * DM:(a + 1) * DM],
                                        st_sb[:, 44 + a:45 + a], None, ALU.mult)
            se_rn0 = self.sVe.tag(last)
            prev["se_rn0"] = se_rn0

            if own_ti is not None:
                SC.wait_ge(sVe.h, se_xn)
                prev["st_xn"] = self.st(
                    D["xnD"].ap()[own_ti * TILE:(own_ti + 1) * TILE, :].rearrange("(a p) d -> p a d", p=128),
                    xn_sb[:, :].rearrange("p (a d) -> p a d", a=4), xn_st)

            rn0T = rn0T_r.acquire(SE)
            PE.wait_ge(sVe.h, se_rn0)
            for a in range(4):
                for dmt in range(8):
                    pt = pt_r.acquire(PE)
                    pe_tr = self.sPe.tag(PE.transpose(
                        pt[:, :], xr[:, a * DM + dmt * 128: a * DM + (dmt + 1) * 128], P["ident"][:, :]))
                    SE.wait_ge(sPe.h, pe_tr)
                    se_ev = self.sSe.tag(SE.activation(
                        rn0T[:, dmt * TILE + a * 128: dmt * TILE + (a + 1) * 128], pt[:, :], AF.Copy))
                    pt_r.release((self.sSe, se_ev))
            se_rn0T = self.sSe.v
            xr_r.release((self.sSe, se_rn0T))

            if own_ti is not None and _F_XNT:
                xnT = xnT_r.acquire(SE)
                for a in range(4):
                    for dmt in range(8):
                        pt = pt_r.acquire(PE)
                        PE.wait_ge(sVe.h, se_xn)
                        pe_tr = self.sPe.tag(PE.transpose(
                            pt[:, :], xn_sb[:, a * DM + dmt * 128: a * DM + (dmt + 1) * 128], P["ident"][:, :]))
                        SE.wait_ge(sPe.h, pe_tr)
                        se_ev = self.sSe.tag(SE.activation(
                            xnT[:, dmt * TILE + a * 128: dmt * TILE + (a + 1) * 128], pt[:, :], AF.Copy))
                        pt_r.release((self.sSe, se_ev))
                se_xnT = self.sSe.v
                SC.wait_ge(sSe.h, se_xnT)
                st_xnT = self.st(
                    D["xnTD"].ap()[:, own_ti * TILE:(own_ti + 1) * TILE].rearrange("(kk p) t -> p kk t", p=128),
                    xnT[:, :].rearrange("p (kk t) -> p kk t", kk=8), xnT_r.stsem())
                self.wt(PE, ld_w0)
                for a in (range(4) if _F_GATE else []):
                    pg = pg_r.acquire(PE)
                    last = None
                    for kk in range(8):
                        last = PE.matmul(pg[:, :],
                                         xnT[:, kk * TILE + a * 128: kk * TILE + (a + 1) * 128],
                                         gatew_sb[:, kk * 8:(kk + 1) * 8],
                                         start=(kk == 0), stop=(kk == 7))
                    pe_g = self.sPe.tag(last)
                    VE.wait_ge(sPe.h, pe_g)
                    VE.tensor_tensor(gsc[:, 0:8], pg[:, :], gateb_sb[:, :], ALU.add)
                    ve_pg = self.sVe.tag(VE.reduce_max(gsc[:, 16:17], gsc[:, 0:8], AX.X))
                    pg_r.release((self.sVe, ve_pg))
                    ve_m = self.sVe.tag(VE.tensor_scalar(gsc[:, 17:18], gsc[:, 16:17], -1.0, None, ALU.mult))
                    SE.wait_ge(sVe.h, ve_m)
                    se_e = self.sSe.tag(SE.activation(gsc[:, 8:16], gsc[:, 0:8], AF.Exp, bias=gsc[:, 17:18]))
                    VE.wait_ge(sSe.h, se_e)
                    VE.tensor_reduce(gsc[:, 19:20], gsc[:, 8:16], AX.X, ALU.add)
                    VE.reciprocal(gsc[:, 20:21], gsc[:, 19:20])
                    VE.tensor_scalar(gsc[:, 8:16], gsc[:, 8:16], gsc[:, 20:21], None, ALU.mult)
                    gw = gsc[:, 8:16]
                    in_j = gw.rearrange("p (o j) -> p o j", o=1).broadcast_to([128, 8, 8])
                    in_e = gw.rearrange("p (e o) -> p e o", o=1).broadcast_to([128, 8, 8])
                    pair = gsc[:, 32:96].rearrange("p (e j) -> p e j", e=8)
                    VE.tensor_tensor(pair, in_j, in_e, ALU.is_gt)
                    VE.tensor_reduce(gsc[:, 96:104], pair, AX.X, ALU.add)
                    VE.tensor_tensor(pair, in_j, in_e, ALU.is_equal)
                    VE.tensor_tensor(gsc[:, 32:96], gsc[:, 32:96], lower_sb[:, :], ALU.mult)
                    VE.tensor_reduce(gsc[:, 104:112], pair, AX.X, ALU.add)
                    VE.tensor_tensor(gsc[:, 96:104], gsc[:, 96:104], gsc[:, 104:112], ALU.add)
                    VE.tensor_scalar(gsc[:, 96:104], gsc[:, 96:104], 2.0, None, ALU.is_lt)
                    tm = own_ti * 4 + a
                    self.sVe.tag(VE.tensor_tensor(P["ccol"][:, tm * 8:(tm + 1) * 8], gw, gsc[:, 96:104], ALU.mult))
                xnT_r.release((self.sPe, self.sPe.v), st_xnT)

            specs = [("xp", D["wxp_bfD"], True)]
            if do_z and _F_Z:
                specs.append(("z", D["wz_bfD"], False))
            pe_xd = None
            for name, wsrc, is_xp in specs:
                for half in range(2):
                    w = w_r.acquire(SY)
                    lw = self.ld(w[:, :].rearrange("p (kk c) -> p kk c", kk=8),
                                 wsrc.ap()[:, half * DM:(half + 1) * DM]
                                 .rearrange("(kk p) c -> p kk c", p=128), w_r.ldsem())
                    self.wt(PE, lw)
                    PE.wait_ge(sSe.h, se_rn0T)
                    last_mm_w = None
                    for gh in range(8):
                        g = half * 8 + gh
                        psx = pa_r.acquire(PE)
                        last = None
                        for kk in range(8):
                            last = PE.matmul(psx[:, :],
                                             w[:, kk * DM + gh * 128: kk * DM + (gh + 1) * 128],
                                             rn0T[:, kk * TILE:(kk + 1) * TILE],
                                             start=(kk == 0), stop=(kk == 7))
                        pe_xp = self.sPe.tag(last)
                        last_mm_w = pe_xp
                        SE.wait_ge(sPe.h, pe_xp)
                        if is_xp:
                            for sem, val in xpg_free[g]:
                                SE.wait_ge(sem.h, val)
                            xpg_free[g] = []
                            se_ev = self.sSe.tag(SE.activation(xpg[g][:, 3:3 + TILE], psx[:, :], AF.Copy))
                            pa_r.release((self.sSe, se_ev))
                            VE.wait_ge(sSe.h, se_ev)
                            xc = xcg_r.acquire(VE)
                            VE.tensor_scalar(xc[:, :], xpg[g][:, 0:TILE], P["convw"][:, g * 4:g * 4 + 1], None, ALU.mult)
                            for t in range(1, 4):
                                VE.scalar_tensor_tensor(xc[:, :], xpg[g][:, t:t + TILE],
                                                        P["convw"][:, g * 4 + t:g * 4 + t + 1], xc[:, :],
                                                        ALU.mult, ALU.add)
                            VE.tensor_scalar(xc[:, :], xc[:, :], P["convb"][:, g:g + 1], None, ALU.add)
                            ve_tail = self.sVe.tag(VE.tensor_copy(xpg[g][:, 0:3], xpg[g][:, TILE:TILE + 3]))
                            SE.wait_ge(sVe.h, ve_tail)
                            se_xc = self.sSe.tag(SE.activation(xc[:, :], xc[:, :], AF.Silu))
                            xpg_free[g] = [(self.sVe, ve_tail)]
                            if spill_col is not None and _F_XDBL:
                                xcbf = xcbf_r.acquire(VE)
                                VE.wait_ge(sSe.h, se_xc)
                                ve_xcb = self.sVe.tag(VE.tensor_copy(xcbf[:, :], xc[:, :]))
                                SC.wait_ge(sSe.h, se_xc)
                                st_xc = self.st(D["xcD"].ap()[g * 128:(g + 1) * 128, spill_col:spill_col + TILE],
                                                xc[:, :], xcg_r.stsem())
                                xc_spill_marks[g] = st_xc
                                xcg_r.release(st_xc)
                                PE.wait_ge(sVe.h, ve_xcb)
                                if g == 0:
                                    for sem, val in px_free:
                                        PE.wait_ge(sem.h, val)
                                    del px_free[:]
                                pe_xd = self.sPe.tag(PE.matmul(px[0:96, :], xprojw_sb[:, g * 96:(g + 1) * 96],
                                                               xcbf[:, :], start=(g == 0), stop=(g == G - 1),
                                                               skip_group_check=True))
                                xcbf_r.release((self.sPe, pe_xd))
                            else:
                                xcg_r.release((self.sSe, se_xc))
                        else:
                            zev = zev_r.acquire(SE)
                            se_ev = self.sSe.tag(SE.activation(zev[:, :], psx[:, :], AF.Copy))
                            pa_r.release((self.sSe, se_ev))
                            SC.wait_ge(sSe.h, se_ev)
                            st_z = self.st(D["zD"].ap()[g * 128:(g + 1) * 128,
                                                        spill_col:spill_col + TILE],
                                           zev[:, :], zev_r.stsem())
                            zev_r.release(st_z)
                    w_r.release((self.sPe, last_mm_w))
            rn0T_r.release((self.sPe, self.sPe.v))

            if not do_scan or not _F_SCAN:
                return

            if not _F_XDBL:
                return
            xdbl = xdbl_r.acquire(SE)
            SE.wait_ge(sPe.h, pe_xd)
            se_xd = self.sSe.tag(SE.activation(xdbl[0:96, :], px[0:96, :], AF.Copy))
            px_free.append((self.sSe, se_xd))
            SC.wait_ge(sSe.h, se_xd)
            st_b = self.st(D["BD"].ap()[:, spill_col:spill_col + TILE], xdbl[64:80, :], xdbl_r.stsem())
            if own_ti is not None:
                st_b = self.st(D["CD"].ap()[:, own_ti * TILE:(own_ti + 1) * TILE], xdbl[80:96, :], xdbl_r.stsem())
            xdblbf = xdblbf_r.acquire(VE)
            VE.wait_ge(sSe.h, se_xd)
            ve_xdb = self.sVe.tag(VE.tensor_copy(xdblbf[0:64, :], xdbl[0:64, :]))
            xdbl_r.release((self.sVe, ve_xdb), st_b)

            for g in (range(G) if _F_DT else []):
                psx = pa_r.acquire(PE)
                PE.wait_ge(sVe.h, ve_xdb)
                pe_dt = self.sPe.tag(PE.matmul(psx[:, :], dtww_sb[0:64, g * 128:(g + 1) * 128],
                                               xdblbf[0:64, :], start=True, stop=True))
                if g == G - 1:
                    xdblbf_r.release((self.sPe, pe_dt))
                if _F_DTSUB < 2:
                    se_sp = self.sSe.v
                    ve_dt0 = self.sVe.v
                    pa_r.release((self.sPe, pe_dt))
                    continue
                VE.wait_ge(sPe.h, pe_dt)
                ve_max = self.sVe.tag(VE.tensor_scalar(spm_sb[:, :], psx[:, :], P["dtb"][:, g:g + 1],
                                                       0.0, ALU.add, ALU.max))
                SE.wait_ge(sVe.h, ve_max)
                SE.activation(spl_sb[:, :], psx[:, :], AF.Abs, bias=P["dtb"][:, g:g + 1])
                SE.activation(spl_sb[:, :], spl_sb[:, :], AF.Exp, scale=-1.0)
                se_sp = self.sSe.tag(SE.activation(spl_sb[:, :], spl_sb[:, :], AF.Ln, bias=1.0))
                if _F_DTSUB < 3:
                    pa_r.release((self.sSe, se_sp))
                    continue
                dtg = dtg_r.acquire(VE)
                VE.wait_ge(sSe.h, se_sp)
                VE.tensor_tensor(dtg[:, :], spm_sb[:, :], spl_sb[:, :], ALU.add)
                if _F_DTSUB < 4:
                    prev["ve_dt"] = self.sVe.tag(VE.tensor_copy(dtg[:, 0:1], dtg[:, 0:1]))
                    pa_r.release((self.sSe, se_sp), (self.sVe, prev["ve_dt"]))
                    dtg_r.release((self.sVe, prev["ve_dt"]))
                    continue
                xcl = xcl_r.acquire(SY)
                self.wt(SY, xc_spill_marks[g])
                lxc = self.ld(xcl[:, :], D["xcD"].ap()[g * 128:(g + 1) * 128, spill_col:spill_col + TILE],
                              xcl_r.ldsem())
                dtug = dtug_r.acquire(VE)
                self.wt(VE, lxc)
                ve_dt = self.sVe.tag(VE.tensor_tensor(dtug[:, :], dtg[:, :], xcl[:, :], ALU.mult))
                prev["ve_dt"] = ve_dt
                pa_r.release((self.sSe, se_sp), (self.sVe, ve_dt))
                xcl_r.release((self.sVe, ve_dt))
                SC.wait_ge(sVe.h, ve_dt)
                st1 = self.st(D["dtD"].ap()[g * 128:(g + 1) * 128, spill_col:spill_col + TILE], dtg[:, :], dtg_r.stsem())
                st2 = self.st(D["dtuD"].ap()[g * 128:(g + 1) * 128, spill_col:spill_col + TILE], dtug[:, :], dtug_r.stsem())
                dtg_r.release(st1)
                dtug_r.release(st2)

        for ti in range(_NT_OWN):
            src = D["x_own"].ap()[ti * TILE:(ti + 1) * TILE, :]
            if ti == 0:
                tile(src, None, None, None, do_scan=False, do_z=False)
            else:
                tile(src, (ti - 1) * TILE, ti - 1, None, do_scan=True, do_z=True)

        self.join_all()
        es.close()

    def ph1b(self):
        D, P = self.dram, self.persist
        es = ExitStack()
        SY, SC, VE, SE, GP = self.SY, self.SC, self.VE, self.SE, self.GP
        sVe, sSe, sGp = self.sVe, self.sSe, self.sGp
        NS = G * DS

        dtq_r = self.rot(es, "u_dt", 2, (128, 4 * CHK))
        Bs_r = self.rot(es, "u_B", 2, (128, CHK))
        scqs = [self.sb(es, f"u_scq{i}", (128, 4 * CHK)) for i in range(2)]
        dtubs = [self.sb(es, f"u_dtub{i}", (128, 4 * CHK)) for i in range(2)]
        dtu_ld = [self.dsem("u_dtu0_ld"), self.dsem("u_dtu1_ld")]
        ebuf_r = Rot([self.sb(es, f"u_e{i}", (128, CHK)) for i in range(2)])
        onesb = self.sb(es, "u_ones", (128, CHK))
        cumc = self.sb(es, "u_cumc", (128, G))
        stot = self.sb(es, "u_stot", (128, G))
        bbs = self.sb(es, "u_bb", (128, CHK))
        ws = self.sb(es, "u_w", (128, CHK))
        hend = self.sb(es, "u_hend", (128, 2 * NS))
        bcol = self.sb(es, "u_bcol", (128, 2))
        redc = self.sb(es, "u_red", (128, 2))
        ccg = self.sb(es, "u_ccg", (128, 8 * NS))
        cf = self.sb(es, "u_cf", (128, 3 * NS))
        ccst = self.dsem("u_ccst")
        ccld = self.dsem("u_ccld")

        VE.memset(onesb[:, :], 1.0)
        self.sVe.tag(VE.memset(cumc[:, :], 0.0))

        prev_se = 0
        prev_ve = 0
        for q in range(4):
            if prev_se:
                VE.wait_ge(sSe.h, prev_se)
            ldtus = [None, None]
            for c in range(2):
                if prev_ve:
                    SY.wait_ge(sVe.h, prev_ve)
                ldtus[c] = self.ld(dtubs[c][:, :].rearrange("p (g t) -> p g t", g=4),
                                   D["dtuD"].ap()[q * TILE:(q + 1) * TILE, c * CHK:(c + 1) * CHK]
                                   .rearrange("(g p) t -> p g t", p=128), dtu_ld[c])
                dtq = dtq_r.acquire(SY)
                ldt = self.ld(dtq[:, :].rearrange("p (g t) -> p g t", g=4),
                              D["dtD"].ap()[q * TILE:(q + 1) * TILE, c * CHK:(c + 1) * CHK]
                              .rearrange("(g p) t -> p g t", p=128), dtq_r.ldsem())
                self.wt(VE, ldt)
                for gi in range(4):
                    g = q * 4 + gi
                    VE.tensor_tensor_scan(
                        scqs[c][:, gi * CHK:(gi + 1) * CHK], onesb[:, :],
                        dtq[:, gi * CHK:(gi + 1) * CHK],
                        cumc[:, g:g + 1], ALU.mult, ALU.add)
                    self.sVe.tag(VE.tensor_copy(cumc[:, g:g + 1],
                                                scqs[c][:, gi * CHK + CHK - 1:gi * CHK + CHK]))
                    if c == 1:
                        self.sVe.tag(VE.tensor_copy(stot[:, g:g + 1], cumc[:, g:g + 1]))
                dtq_r.release((self.sVe, self.sVe.v))
            for c in range(2):
                for gi in range(4):
                    g = q * 4 + gi
                    self.sVe.tag(VE.tensor_scalar(
                        scqs[c][:, gi * CHK:(gi + 1) * CHK], scqs[c][:, gi * CHK:(gi + 1) * CHK],
                        stot[:, g:g + 1], None, ALU.subtract))
            ve_sub = self.sVe.v
            for si in range(DS):
                for c in range(2):
                    Bsb = Bs_r.acquire(SY)
                    lB = self.ld(Bsb[:, :], D["BD"].ap()[si:si + 1, c * CHK:(c + 1) * CHK].squeeze(0)
                                 .partition_broadcast(128), Bs_r.ldsem())
                    for gi in range(4):
                        g = q * 4 + gi
                        col = g * DS + si
                        eb = ebuf_r.acquire(SE)
                        if si == 0 and gi == 0:
                            SE.wait_ge(sVe.h, ve_sub)
                        se_e = self.sSe.tag(SE.activation(
                            eb[:, :], scqs[c][:, gi * CHK:(gi + 1) * CHK], AF.Exp,
                            scale=P["negA"][:, col:col + 1]))
                        prev_se = se_e
                        self.wt(VE, ldtus[c], lB)
                        VE.tensor_tensor(bbs[:, :], dtubs[c][:, gi * CHK:(gi + 1) * CHK],
                                         Bsb[:, :], ALU.mult)
                        VE.wait_ge(sSe.h, se_e)
                        ve_r = self.sVe.tag(VE.tensor_tensor(ws[:, :], bbs[:, :], eb[:, :], ALU.mult))
                        if c == 0:
                            VE.tensor_reduce(hend[:, col:col + 1],
                                             ws[:, :].rearrange("p (o t) -> p o t", o=1),
                                             AX.X, ALU.add)
                        else:
                            VE.tensor_reduce(redc[:, 0:1],
                                             ws[:, :].rearrange("p (o t) -> p o t", o=1),
                                             AX.X, ALU.add)
                            VE.tensor_tensor(hend[:, col:col + 1], hend[:, col:col + 1],
                                             redc[:, 0:1], ALU.add)
                        ebuf_r.release((self.sVe, ve_r))
                    Bs_r.release((self.sVe, self.sVe.v))
            prev_ve = self.sVe.v

        self.sVe.tag(VE.tensor_tensor(
            hend[:, NS:2 * NS].rearrange("p (g s) -> p g s", g=G),
            P["A"][:, :].rearrange("p (g s) -> p g s", g=G),
            stot[:, :].rearrange("p (g o) -> p g o", o=1).broadcast_to([128, G, DS]),
            ALU.mult))
        SE.wait_ge(sVe.h, self.sVe.v)
        se_m = self.sSe.tag(SE.activation(hend[:, NS:2 * NS], hend[:, NS:2 * NS], AF.Exp))

        SC.wait_ge(sSe.h, se_m)
        SC.wait_ge(sVe.h, self.sVe.v)
        st_cc = self.st(D["ccinD"].ap().rearrange("(a p) c -> p a c", p=128),
                        hend[:, :].rearrange("p (a c) -> p a c", a=2), ccst)
        self.wt(GP, st_cc)
        cc_i = self.sGp.tag(GP.collective_compute(
            "AllGather", ALU.bypass, replica_groups=[[0, 1, 2, 3], [4, 5, 6, 7]],
            ins=[D["ccinD"].ap().opt()], outs=[D["ccoutD"].ap().opt()]))
        SY.wait_ge(sGp.h, cc_i)
        lcc = self.ld(ccg[:, :].rearrange("p (a c) -> p a c", a=8),
                      D["ccoutD"].ap().rearrange("(a p) c -> p a c", p=128), ccld)
        self.wt(VE, lcc)
        VE.tensor_copy(cf[:, 0:NS], ccg[:, 0:NS])
        VE.tensor_tensor(cf[:, NS:2 * NS], cf[:, 0:NS], ccg[:, 3 * NS:4 * NS], ALU.mult)
        VE.tensor_tensor(cf[:, NS:2 * NS], cf[:, NS:2 * NS], ccg[:, 2 * NS:3 * NS], ALU.add)
        VE.tensor_tensor(cf[:, 2 * NS:3 * NS], cf[:, NS:2 * NS], ccg[:, 5 * NS:6 * NS], ALU.mult)
        VE.tensor_tensor(cf[:, 2 * NS:3 * NS], cf[:, 2 * NS:3 * NS], ccg[:, 4 * NS:5 * NS], ALU.add)
        VE.tensor_scalar(P["carry0"][:, :], cf[:, 0:NS], P["ssel"][:, 1:2], None, ALU.mult)
        VE.scalar_tensor_tensor(P["carry0"][:, :], cf[:, NS:2 * NS], P["ssel"][:, 2:3],
                                P["carry0"][:, :], ALU.mult, ALU.add)
        self.sVe.tag(VE.scalar_tensor_tensor(P["carry0"][:, :], cf[:, 2 * NS:3 * NS],
                                             P["ssel"][:, 3:4], P["carry0"][:, :],
                                             ALU.mult, ALU.add))
        self.join_all()
        es.close()

    def ph2(self):
        D, P = self.dram, self.persist
        es = ExitStack()
        SY, SC, VE, SE, PE, GP = self.SY, self.SC, self.VE, self.SE, self.PE, self.GP
        sLd, sSt, sVe, sSe, sPe, sGp = self.sLd, self.sSt, self.sVe, self.sSe, self.sPe, self.sGp

        dtq_r = self.rot(es, "s_dtq", 2, (128, 4 * CHK))
        dtuq_r = self.rot(es, "s_dtuq", 2, (128, 4 * CHK))
        Bs_r = self.rot(es, "s_B", 2, (128, CHK))
        Cs_r = self.rot(es, "s_C", 1, (128, CHK))
        ab_r = Rot([self.sb(es, f"s_a{i}", (128, CHK)) for i in range(2)])
        bb = self.sb(es, "s_b", (128, CHK))
        hb = self.sb(es, "s_h", (128, CHK))
        tmp = self.sb(es, "s_tmp", (128, CHK))
        yacc = [self.sb(es, f"s_y{i}", (128, CHK)) for i in range(4)]
        carry = self.sb(es, "s_carry", (128, G * DS))
        ygbf_r = self.rot(es, "s_ygbf", 1, (128, CHK), BF16, ld=False, st=True)
        xcg2_r = self.rot(es, "s_xcg", 1, (128, CHK))
        zg2_r = self.rot(es, "s_zg", 1, (128, CHK))

        m_xnT_r = self.rot(es, "m_xnT", 1, (128, 8 * TILE), BF16)
        m_w1_r = self.rot(es, "m_w1", 1, (128, 8 * CHK), BF16)
        m_hT = self.sb(es, "m_hT", (128, 32 * TILE), BF16)
        m_hT_free = []
        m_ctb_r = self.rot(es, "m_ctb", 2, (128, TILE), ld=False, st=True)
        m_b2_r = self.rot(es, "m_b2", 1, (1, TILE))
        m_ps1_r = Rot([self.psum(es, f"m_ps1{i}", (128, TILE)) for i in range(2)])
        m_ps2h = [self.psum(es, f"m_ps2h{i}", (128, TILE)) for i in range(4)]
        m_ps2h_free = [[] for _ in range(4)]
        m_w2b_r = self.rot(es, "m_w2b", 2, (128, 4 * TILE), BF16)

        self.sVe.tag(VE.tensor_copy(carry[:, :], P["carry0"][:, :]))
        gacc = {}

        def moe_unit(e, nq):
            xnT = m_xnT_r.acquire(SY)
            lxn = self.ld(xnT[:, :].rearrange("p (kk t) -> p kk t", kk=8),
                          D["xnTD"].ap()[:, nq * TILE:(nq + 1) * TILE]
                          .rearrange("(kk p) t -> p kk t", p=128), m_xnT_r.ldsem())

            for sem, val in m_hT_free:
                SE.wait_ge(sem.h, val)
            del m_hT_free[:]
            for hq in range(4):
                w1 = m_w1_r.acquire(SY)
                lw1 = self.ld(w1[:, :].rearrange("p (kk h) -> p kk h", kk=8),
                              D["w1bfD"].ap()[e][:, hq * CHK:(hq + 1) * CHK]
                              .rearrange("(kk p) h -> p kk h", p=128), m_w1_r.ldsem())
                self.wt(PE, lw1, lxn)
                last_w1 = None
                for hm in range(8):
                    hmg = hq * 8 + hm
                    ps1 = m_ps1_r.acquire(PE)
                    last = None
                    for kk in range(8):
                        last = PE.matmul(ps1[:, :],
                                         w1[:, kk * CHK + hm * 128: kk * CHK + (hm + 1) * 128],
                                         xnT[:, kk * TILE:(kk + 1) * TILE],
                                         start=(kk == 0), stop=(kk == 7))
                    pe1 = self.sPe.tag(last)
                    last_w1 = pe1
                    SE.wait_ge(sPe.h, pe1)
                    se_ev = self.sSe.tag(SE.activation(
                        m_hT[:, hmg * TILE:(hmg + 1) * TILE],
                        ps1[:, :], AF.Relu, bias=P["b1"][:, e * 32 + hmg:e * 32 + hmg + 1]))
                    m_ps1_r.release((self.sSe, se_ev))
                m_w1_r.release((self.sPe, last_w1))
            se_h_done = self.sSe.v
            PE.wait_ge(sSe.h, se_h_done)
            for dmh in range(2):
                b2 = m_b2_r.acquire(SY)
                lb2 = self.ld(b2[0:1, :], D["moe_b2_r"].ap()[0:1, e * DM + dmh * TILE:e * DM + (dmh + 1) * TILE],
                              m_b2_r.ldsem())
                self.wt(PE, lb2)
                for hkb in range(8):
                    w2b = m_w2b_r.acquire(SY)
                    lw2 = self.ld(
                        w2b[:, :].rearrange("p (h c) -> p h c", h=4),
                        D["w2bfD"].ap()[e][hkb * 512:(hkb + 1) * 512, dmh * TILE:(dmh + 1) * TILE]
                        .rearrange("(h p) c -> p h c", p=128),
                        m_w2b_r.ldsem())
                    self.wt(PE, lw2)
                    last_b = None
                    for tl in range(4):
                        if hkb == 0:
                            for mk in m_ps2h_free[tl]:
                                PE.wait_ge(mk[0].h, mk[1])
                            m_ps2h_free[tl] = []
                        for hkin in range(4):
                            hk = hkb * 4 + hkin
                            last_b = PE.matmul(m_ps2h[tl][:, :],
                                               m_hT[:, hk * TILE + tl * 128:hk * TILE + (tl + 1) * 128],
                                               w2b[:, hkin * TILE:(hkin + 1) * TILE],
                                               start=(hk == 0), stop=False,
                                               skip_group_check=True)
                    pe_w2 = self.sPe.tag(last_b)
                    m_w2b_r.release((self.sPe, pe_w2))
                for tl in range(4):
                    tokg = nq * 4 + tl
                    pe2 = self.sPe.tag(PE.matmul(m_ps2h[tl][:, :],
                                                 P["ones"][0:1, 0:128],
                                                 b2[0:1, 0:TILE],
                                                 start=False, stop=True, skip_group_check=True))
                    ctb = m_ctb_r.acquire(VE)
                    VE.wait_ge(sPe.h, pe2)
                    ve_ct = self.sVe.tag(VE.tensor_scalar(
                        ctb[:, :], m_ps2h[tl][:, :], P["ccol"][:, tokg * 8 + e:tokg * 8 + e + 1],
                        None, ALU.mult))
                    m_ps2h_free[tl].append((self.sVe, ve_ct))
                    key = (tokg, dmh)
                    if key in gacc:
                        self.wt(GP, gacc[key])
                    GP.wait_ge(sVe.h, ve_ct)
                    csem = m_ctb_r.stsem()
                    GP.dma_start(out=D["gD"].ap()[tokg * 128:(tokg + 1) * 128, dmh * TILE:(dmh + 1) * TILE],
                                 in_=ctb[:, :],
                                 accum_op=(ALU.add if e > 0 else ALU.bypass)).then_inc(csem.h, 16)
                    csem.v += 16
                    gacc[key] = (csem, csem.v)
                    m_ctb_r.release((csem, csem.v))
                m_b2_r.release((self.sPe, self.sPe.v))
            m_hT_free.append((self.sPe, self.sPe.v))
            m_xnT_r.release((self.sPe, self.sPe.v))

        moe_units = iter([(e, nq) for e in range(8) for nq in range(4)] if _STAGE >= 3 else [])

        n_chunks = NTOK // CHK
        for c in range(n_chunks):
            own_c = True
            for q in range(4):
                dtq = dtq_r.acquire(SY)
                ldt = self.ld(dtq[:, :].rearrange("p (g t) -> p g t", g=4),
                              D["dtD"].ap()[q * TILE:(q + 1) * TILE, c * CHK:(c + 1) * CHK]
                              .rearrange("(g p) t -> p g t", p=128), dtq_r.ldsem())
                dtuq = dtuq_r.acquire(SY)
                ldtu = self.ld(dtuq[:, :].rearrange("p (g t) -> p g t", g=4),
                               D["dtuD"].ap()[q * TILE:(q + 1) * TILE, c * CHK:(c + 1) * CHK]
                               .rearrange("(g p) t -> p g t", p=128), dtuq_r.ldsem())
                ve_last_y = 0
                for si in range(DS):
                    Bsb = Bs_r.acquire(SY)
                    lB = self.ld(Bsb[:, :], D["BD"].ap()[si:si + 1, c * CHK:(c + 1) * CHK].squeeze(0)
                                 .partition_broadcast(128), Bs_r.ldsem())
                    if own_c:
                        Csb = Cs_r.acquire(SY)
                        lC = self.ld(Csb[:, :], D["CD"].ap()[si:si + 1, c * CHK:(c + 1) * CHK].squeeze(0)
                                     .partition_broadcast(128), Cs_r.ldsem())
                    for gi in range(4):
                        gg = q * 4 + gi
                        col = gg * DS + si
                        ab = ab_r.acquire(SE)
                        self.wt(SE, ldt)
                        se_a = self.sSe.tag(SE.activation(ab[:, :], dtq[:, gi * CHK:(gi + 1) * CHK],
                                                          AF.Exp, scale=P["A"][:, col:col + 1]))
                        self.wt(VE, ldtu, lB)
                        VE.tensor_tensor(bb[:, :], dtuq[:, gi * CHK:(gi + 1) * CHK], Bsb[:, :], ALU.mult)
                        VE.wait_ge(sSe.h, se_a)
                        ve_sc = self.sVe.tag(VE.tensor_tensor_scan(
                            hb[:, :], ab[:, :], bb[:, :], carry[:, col:col + 1], ALU.mult, ALU.add))
                        ab_r.release((self.sVe, ve_sc))
                        VE.tensor_copy(carry[:, col:col + 1], hb[:, CHK - 1:CHK])
                        if own_c:
                            self.wt(VE, lC)
                            if si == 0:
                                ve_last_y = self.sVe.tag(VE.tensor_tensor(yacc[gi][:, :], hb[:, :],
                                                                          Csb[:, :], ALU.mult))
                            else:
                                VE.tensor_tensor(tmp[:, :], hb[:, :], Csb[:, :], ALU.mult)
                                ve_last_y = self.sVe.tag(VE.tensor_tensor(yacc[gi][:, :], yacc[gi][:, :],
                                                                          tmp[:, :], ALU.add))
                        else:
                            ve_last_y = self.sVe.v
                    Bs_r.release((self.sVe, self.sVe.v))
                    if own_c:
                        Cs_r.release((self.sVe, self.sVe.v))
                dtq_r.release((self.sSe, self.sSe.v))
                dtuq_r.release((self.sVe, self.sVe.v))
                if own_c:
                    for gi in range(4):
                        gg = q * 4 + gi
                        ocol = c * CHK
                        xcg2 = xcg2_r.acquire(SY)
                        lxc = self.ld(xcg2[:, :], D["xcD"].ap()[gg * 128:(gg + 1) * 128,
                                                                ocol:ocol + CHK], xcg2_r.ldsem())
                        zg2 = zg2_r.acquire(SY)
                        lz = self.ld(zg2[:, :], D["zD"].ap()[gg * 128:(gg + 1) * 128, ocol:ocol + CHK], zg2_r.ldsem())
                        self.wt(VE, lxc, lz)
                        VE.scalar_tensor_tensor(yacc[gi][:, :], xcg2[:, :], P["D"][:, gg:gg + 1],
                                                yacc[gi][:, :], ALU.mult, ALU.add)
                        SE.wait_ge(sVe.h, self.sVe.v)
                        se_t = self.sSe.tag(SE.activation(tmp[:, :], zg2[:, :], AF.Silu))
                        xcg2_r.release((self.sVe, self.sVe.v))
                        zg2_r.release((self.sSe, se_t))
                        ygbf = ygbf_r.acquire(VE)
                        VE.wait_ge(sSe.h, se_t)
                        ve_yg = self.sVe.tag(VE.tensor_tensor(ygbf[:, :], yacc[gi][:, :], tmp[:, :], ALU.mult))
                        SC.wait_ge(sVe.h, ve_yg)
                        st_yg = self.st(D["ygD"].ap()[gg * 128:(gg + 1) * 128, ocol:ocol + CHK], ygbf[:, :],
                                        ygbf_r.stsem())
                        ygbf_r.release(st_yg)
                for _ in range(4):
                    nxt = next(moe_units, None)
                    if nxt is not None:
                        moe_unit(*nxt)
        for nxt in moe_units:
            moe_unit(*nxt)

        self.join_all()
        es.close()

    def ph3(self):
        D, P = self.dram, self.persist
        es = ExitStack()
        SY, SC, VE, SE, PE = self.SY, self.SC, self.VE, self.SE, self.PE
        sLd, sSt, sVe, sSe, sPe = self.sLd, self.sSt, self.sVe, self.sSe, self.sPe

        opw = self.sb(es, "f_opw", (128, G * DM), BF16)
        opw_ld = self.dsem("f_opw_ld")
        ygT_r = self.rot(es, "f_ygT", 1, (128, G * TILE), BF16)
        xn4_r = self.rot(es, "f_xn", 1, (128, 4 * DM))
        x1 = self.sb(es, "f_x1", (128, 4 * DM))
        stt = self.sb(es, "f_st", (128, 16))
        rn1_r = Rot(self.sb(es, "f_rn1", (128, DM)))
        rn1T_r = Rot(self.sb(es, "f_rn1T", (128, 8 * TILE), BF16))
        w1c_r = self.rot(es, "f_w1c", 2, (128, 8 * 128), BF16)
        hT = self.sb(es, "f_hT", (128, 32 * TILE), BF16)
        hT_free = []
        w2c_r = self.rot(es, "f_w2c", 2, (128, 8 * TILE), BF16)
        qq = self.sb(es, "f_q", (128, DM))
        x2_r = Rot(self.sb(es, "f_x2", (128, DM)))
        gg_r = self.rot(es, "f_g", 2, (128, DM))
        xo_r = self.rot(es, "f_xo", 2, (128, DM))
        oo_r = self.rot(es, "f_oo", 2, (128, DM), ld=False, st=True)
        pA_r = Rot([self.psum(es, f"f_pA{i}", (128, DM)) for i in range(2)])
        pB_r = Rot([self.psum(es, f"f_pB{i}", (128, TILE)) for i in range(2)])
        pT_r = Rot(self.psum(es, "f_pT", (128, 128)))

        lop = self.ld(opw[:, :].rearrange("p (kk d) -> p kk d", kk=G),
                      D["outproj_bfD"].ap().rearrange("(kk p) d -> p kk d", p=128), opw_ld)

        for nt in range(4):
            ygT = ygT_r.acquire(SY)
            lyg = self.ld(ygT[:, :].rearrange("p (kk t) -> p kk t", kk=G),
                          D["ygD"].ap()[:, nt * TILE:(nt + 1) * TILE]
                          .rearrange("(kk p) t -> p kk t", p=128), ygT_r.ldsem())
            xn4 = xn4_r.acquire(SY)
            lxn = self.ld(xn4[:, :].rearrange("p (a d) -> p a d", a=4),
                          D["xnD"].ap()[nt * TILE:(nt + 1) * TILE, :]
                          .rearrange("(a p) d -> p a d", p=128), xn4_r.ldsem())
            self.wt(PE, lyg, lxn, lop)
            for a in range(4):
                pA = pA_r.acquire(PE)
                last = None
                for kk in range(G):
                    for half in range(2):
                        last = PE.matmul(pA[:, half * TILE:(half + 1) * TILE],
                                         ygT[:, kk * TILE + a * 128: kk * TILE + (a + 1) * 128],
                                         opw[:, kk * DM + half * TILE: kk * DM + (half + 1) * TILE],
                                         start=(kk == 0), stop=(kk == G - 1 and half == 1))
                pe_op = self.sPe.tag(last)
                VE.wait_ge(sPe.h, pe_op)
                self.wt(VE, lxn)
                ve_x1 = self.sVe.tag(VE.tensor_tensor(
                    x1[:, a * DM:(a + 1) * DM], xn4[:, a * DM:(a + 1) * DM], pA[:, :], ALU.add))
                pA_r.release((self.sVe, ve_x1))
            ve_x1_all = self.sVe.v
            ygT_r.release((self.sPe, self.sPe.v))
            xn4_r.release((self.sVe, ve_x1_all))

            rn1T = rn1T_r.acquire(SE)
            for a in range(4):
                rn1 = rn1_r.acquire(SE)
                SE.wait_ge(sVe.h, ve_x1_all)
                se_sq = self.sSe.tag(SE.activation(rn1[:, :], x1[:, a * DM:(a + 1) * DM], AF.Square))
                VE.wait_ge(sSe.h, se_sq)
                ve_q = self.sVe.tag(VE.tensor_reduce(
                    stt[:, 0:1], rn1[:, :].rearrange("p (o d) -> p o d", o=1), AX.X, ALU.add))
                SE.wait_ge(sVe.h, ve_q)
                SE.activation(stt[:, 1:2], stt[:, 0:1], AF.Ln, bias=EPS_RMS, scale=1.0 / DM)
                self.sSe.tag(SE.activation(stt[:, 1:2], stt[:, 1:2], AF.Exp, scale=-0.5))
                se_r1 = self.sSe.v
                VE.wait_ge(sSe.h, se_r1)
                se_rn = self.sVe.tag(VE.tensor_scalar(rn1[:, :], x1[:, a * DM:(a + 1) * DM],
                                                      stt[:, 1:2], None, ALU.mult))
                PE.wait_ge(sVe.h, se_rn)
                for dmt in range(8):
                    pT = pT_r.acquire(PE)
                    pe_tr = self.sPe.tag(PE.transpose(pT[:, :], rn1[:, dmt * 128:(dmt + 1) * 128],
                                                      P["ident"][:, :]))
                    SE.wait_ge(sPe.h, pe_tr)
                    se_ev = self.sSe.tag(SE.activation(
                        rn1T[:, dmt * TILE + a * 128: dmt * TILE + (a + 1) * 128], pT[:, :], AF.Copy))
                    pT_r.release((self.sSe, se_ev))
                rn1_r.release((self.sPe, self.sPe.v))
            se_rn1T = self.sSe.v

            for sem, val in hT_free:
                VE.wait_ge(sem.h, val)
            del hT_free[:]
            PE.wait_ge(sSe.h, se_rn1T)
            for hm in range(32):
                w1c = w1c_r.acquire(SY)
                lw1 = self.ld(w1c[:, :].rearrange("p (kk h) -> p kk h", kk=8),
                              D["mlpw1_bfD"].ap()[:, hm * 128:(hm + 1) * 128]
                              .rearrange("(kk p) h -> p kk h", p=128), w1c_r.ldsem())
                self.wt(PE, lw1)
                pB = pB_r.acquire(PE)
                last = None
                for kk in range(8):
                    last = PE.matmul(pB[:, :], w1c[:, kk * 128:(kk + 1) * 128],
                                     rn1T[:, kk * TILE:(kk + 1) * TILE],
                                     start=(kk == 0), stop=(kk == 7))
                pe_m1 = self.sPe.tag(last)
                w1c_r.release((self.sPe, pe_m1))
                SE.wait_ge(sPe.h, pe_m1)
                se_h = self.sSe.tag(SE.activation(
                    hT[:, hm * TILE:(hm + 1) * TILE], pB[:, :], AF.Silu))
                pB_r.release((self.sSe, se_h))
            ve_hT = self.sSe.v
            rn1T_r.release((self.sPe, self.sPe.v))

            PE.wait_ge(sSe.h, ve_hT)
            for a in range(4):
                tokg = nt * 4 + a
                pA = pA_r.acquire(PE)
                last = None
                for dmh in range(2):
                    for hkb in range(4):
                        w2c = w2c_r.acquire(SY)
                        lw2 = self.ld(
                            w2c[:, :].rearrange("p (h c) -> p h c", h=8),
                            D["mlpw2_bfD"].ap()[hkb * 1024:(hkb + 1) * 1024, dmh * TILE:(dmh + 1) * TILE]
                            .rearrange("(h p) c -> p h c", p=128),
                            w2c_r.ldsem())
                        self.wt(PE, lw2)
                        for hkin in range(8):
                            hk = hkb * 8 + hkin
                            last = PE.matmul(pA[:, dmh * TILE:(dmh + 1) * TILE],
                                             hT[:, hk * TILE + a * 128: hk * TILE + (a + 1) * 128],
                                             w2c[:, hkin * TILE:(hkin + 1) * TILE],
                                             start=(hk == 0), stop=(hk == 31),
                                             skip_group_check=True)
                        pe_w2 = self.sPe.tag(last)
                        w2c_r.release((self.sPe, pe_w2))
                pe_m2 = self.sPe.v
                gg = gg_r.acquire(SY)
                lg = self.ld(gg[:, :], D["gD"].ap()[tokg * 128:(tokg + 1) * 128, :], gg_r.ldsem())
                xo = xo_r.acquire(SY)
                lx = self.ld(xo[:, :], D["x_own"].ap()[TILE + tokg * 128: TILE + (tokg + 1) * 128, :], xo_r.ldsem())
                x2 = x2_r.acquire(VE)
                VE.wait_ge(sPe.h, pe_m2)
                ve_x2 = self.sVe.tag(VE.tensor_tensor(x2[:, :], x1[:, a * DM:(a + 1) * DM], pA[:, :], ALU.add))
                pA_r.release((self.sVe, ve_x2))
                oo = oo_r.acquire(SE)
                self.wt(SE, lg)
                se_t = self.sSe.tag(SE.activation(oo[:, :], gg[:, :], AF.Silu))
                gg_r.release((self.sSe, se_t))
                VE.wait_ge(sSe.h, se_t)
                VE.tensor_tensor(oo[:, :], oo[:, :], x2[:, :], ALU.mult)
                x2_r.release((self.sVe, self.sVe.v))
                self.wt(VE, lx)
                ve_o = self.sVe.tag(VE.tensor_tensor(oo[:, :], oo[:, :], xo[:, :], ALU.add))
                xo_r.release((self.sVe, ve_o))
                SC.wait_ge(sVe.h, ve_o)
                st_o = self.st(D["o_out"].ap()[tokg * 128:(tokg + 1) * 128, :], oo[:, :], oo_r.stsem())
                oo_r.release(st_o)

        if self.sSt.v:
            SC.wait_ge(self.sSt.h, self.sSt.v)
        for s in self.all_dma_sems:
            if s.v:
                SC.wait_ge(s.h, s.v)
        es.close()

    def finish(self):
        self.es.close()


def build_program():
    nc = bass.Bass("TRN2", target_bir_lowering=False)
    b = Builder(nc)
    b.ph0()
    if _STAGE >= 1:
        b.ph1()
    if _STAGE >= 2:
        b.ph1b()
        b.ph2()
    if _STAGE >= 3:
        b.ph3()
    else:
        with nc.sbuf_tensor("z_oo", [128, DM], F32) as zoo:
            zv = b.sVe.tag(b.VE.memset(zoo[:, :], 0.0))
            b.SC.wait_ge(b.sVe.h, zv)
            for t in range(OWN // 128):
                b.st(b.dram["o_out"].ap()[t * 128:(t + 1) * 128, :], zoo[:, :])
            b.SC.wait_ge(b.sSt.h, b.sSt.v)
    if _DBG:
        D = b.dram
        b.join_all()
        pairs = [("d_dtp", D["dtD"].ap()[:, 0:2048]), ("d_dto", D["dtD"].ap()[:, PREF:]),
                 ("d_xc", D["xcD"].ap()[:, PREF:]), ("d_z", D["zD"].ap()),
                 ("d_xn", D["xnD"].ap()), ("d_B", D["BD"].ap()), ("d_C", D["CD"].ap())]
        if _STAGE >= 2:
            pairs.append(("d_yg", D["ygD"].ap()))
        if _STAGE >= 3:
            pairs.append(("d_g", D["gD"].ap()))
        for nm, srcap in pairs:
            b.st(D[nm].ap(), srcap)
        zv = b.sVe.tag(b.VE.tensor_copy(b.persist["ccol"][:, 0:1], b.persist["ccol"][:, 0:1]))
        b.SC.wait_ge(b.sVe.h, zv)
        b.st(D["d_cc"].ap()[:, 0:128], b.persist["ccol"][:, 0:128])
        b.SC.wait_ge(b.sSt.h, b.sSt.v)
    b.finish()
    return nc


def make_inmaps(inputs):
    x = np.ascontiguousarray(inputs["L0_attn_x"], dtype=np.float32)
    ipw = np.ascontiguousarray(inputs["L0_attn_in_proj_w"], dtype=np.float32)
    conv_w = np.ascontiguousarray(inputs["L0_attn_conv_w"], dtype=np.float32)
    conv_b = np.ascontiguousarray(inputs["L0_attn_conv_b"], dtype=np.float32)
    xpw = np.ascontiguousarray(inputs["L0_attn_x_proj_w"], dtype=np.float32)
    dtw = np.ascontiguousarray(inputs["L0_attn_dt_proj_w"], dtype=np.float32)
    dtb = np.ascontiguousarray(inputs["L0_attn_dt_proj_b"], dtype=np.float32)
    alog = np.ascontiguousarray(inputs["L0_attn_A_log"], dtype=np.float32)
    dd = np.ascontiguousarray(inputs["L0_attn_D"], dtype=np.float32)
    opw = np.ascontiguousarray(inputs["L0_attn_out_proj_w"], dtype=np.float32)
    mw1 = np.ascontiguousarray(inputs["L0_attn_mlp_w1"], dtype=np.float32)
    mw2 = np.ascontiguousarray(inputs["L0_attn_mlp_w2"], dtype=np.float32)
    gw = np.ascontiguousarray(inputs["L0_ffn_gate_w"], dtype=np.float32)
    gb = np.ascontiguousarray(inputs["L0_ffn_gate_b"], dtype=np.float32)
    w1 = np.ascontiguousarray(inputs["L0_ffn_w1"], dtype=np.float32)
    b1 = np.ascontiguousarray(inputs["L0_ffn_b1"], dtype=np.float32)
    w2 = np.ascontiguousarray(inputs["L0_ffn_w2"], dtype=np.float32)
    b2 = np.ascontiguousarray(inputs["L0_ffn_b2"], dtype=np.float32)

    low = np.zeros((64,), np.float32)
    for e in range(8):
        for j in range(8):
            low[e * 8 + j] = 1.0 if j < e else 0.0

    shared = dict(
        wxp_t=np.ascontiguousarray(ipw[:DI].T),
        wz_t=np.ascontiguousarray(ipw[DI:].T),
        convw_c=np.ascontiguousarray(conv_w.reshape(G, 128, 4).transpose(1, 0, 2).reshape(128, G * 4)),
        convb_c=np.ascontiguousarray(conv_b.reshape(G, 128).T),
        xproj_t=np.ascontiguousarray(xpw.T),
        dtw_t=np.ascontiguousarray(dtw.T),
        dtb_c=np.ascontiguousarray(dtb.reshape(G, 128).T),
        alog_c=np.ascontiguousarray(alog.reshape(G, 128, DS).transpose(1, 0, 2).reshape(128, G * DS)),
        d_c=np.ascontiguousarray(dd.reshape(G, 128).T),
        outproj_t=np.ascontiguousarray(opw.T),
        mlpw1_t=np.ascontiguousarray(mw1.T),
        mlpw2_t=np.ascontiguousarray(mw2.T),
        gate_w=gw,
        gate_b_r=np.ascontiguousarray(np.repeat(gb[None, :], 128, 0)),
        moe_w1=w1,
        moe_b1_c=np.ascontiguousarray(b1.reshape(8, 32, 128).transpose(2, 0, 1).reshape(128, 8 * 32)),
        moe_w2=w2,
        moe_b2_r=np.ascontiguousarray(b2.reshape(1, 8 * DM)),
        const_v=np.ascontiguousarray(np.repeat(np.array([CONSTS], np.float32), 128, 0)),
        ident=np.eye(128, dtype=np.float32),
        ones_r=np.ones((128, 128), np.float32),
        lower_r=np.ascontiguousarray(np.repeat(low[None, :], 128, 0)),
    )

    in_maps = []
    for c in range(8):
        b_, k = c // 4, c % 4
        Pk = k * OWN
        ext = np.zeros((TILE, DM), np.float32) if k == 0 else x[b_, Pk - TILE:Pk]
        m = dict(shared)
        m["x_own"] = np.ascontiguousarray(np.concatenate([ext, x[b_, Pk:Pk + OWN]], axis=0))
        sel = np.zeros((128, 4), np.float32)
        sel[:, k] = 1.0
        m["slot_sel"] = sel
        in_maps.append(m)
    return in_maps


_NC_CACHE = [None]


def kernel(**inputs):
    in_maps = make_inmaps(inputs)
    if _NC_CACHE[0] is None:
        _NC_CACHE[0] = build_program()
    nc = _NC_CACHE[0]
    res = run_bass_kernel_spmd(nc, in_maps, core_ids=list(range(8)))
    out = np.empty((2, L, DM), np.float32)
    for c in range(8):
        b_, k = c // 4, c % 4
        out[b_, k * OWN:(k + 1) * OWN] = np.asarray(res.results[c]["o_out"])
    return out


# version 09 (final)
# speedup vs baseline: 2.0131x; 1.0430x over previous
import os
import numpy as np
from contextlib import ExitStack

import concourse.bass as bass
import concourse.mybir as mybir
from concourse.bass_utils import run_bass_kernel_spmd

F32 = mybir.dt.float32
BF16 = mybir.dt.bfloat16
AF = mybir.ActivationFunctionType
ALU = mybir.AluOpType
AX = mybir.AxisListType

L = 8192
DM = 1024
DI = 2048
DS = 16
DTR = 64
OWN = 2048
PREF = 6144
TILE = 512
G = 16
CHK = 1024
NTOK = OWN
EPS_LN = 1e-5
EPS_RMS = 1.1920929e-07

_STAGE = int(os.environ.get("MK_STAGE", "4"))
_DBG = os.environ.get("MK_DBG", "0") == "1"
_NT_OWN = int(os.environ.get("MK_NT_OWN", "5"))
_F_XNT = os.environ.get("MK_F_XNT", "1") == "1"
_F_GATE = os.environ.get("MK_F_GATE", "1") == "1"
_F_Z = os.environ.get("MK_F_Z", "1") == "1"
_F_SCAN = os.environ.get("MK_F_SCAN", "1") == "1"
_F_XDBL = os.environ.get("MK_F_XDBL", "1") == "1"
_F_DT = os.environ.get("MK_F_DT", "1") == "1"
_F_DTSUB = int(os.environ.get("MK_F_DTSUB", "4"))
_NT_PREF = int(os.environ.get("MK_NT_PREF", "12"))
CONSTS = [EPS_LN, EPS_RMS, 1.0, 0.5, -0.5, -1.0, 0.0, 2.0]


class Sem:
    def __init__(self, es, nc, name, auto=False):
        self.h = es.enter_context(nc.semaphore(name))
        self.v = 0
        self.auto = auto

    def tag(self, instr, n=1):
        if self.auto:
            return self.v
        instr.then_inc(self.h, n)
        self.v += n
        return self.v


class SelfSync:

    _PASSTHRU = {"wait_ge", "dma_start"}

    def __init__(self, eng, sem):
        self._eng = eng
        self._sem = sem

    def wait_ge(self, h, v):
        return self._eng.wait_ge(h, v)

    def dma_start(self, *a, **k):
        return self._eng.dma_start(*a, **k)

    def __getattr__(self, name):
        f = getattr(self._eng, name)
        if not callable(f):
            return f
        sem = self._sem
        eng = self._eng

        def wrapped(*a, **k):
            if sem.v:
                eng.wait_ge(sem.h, sem.v)
            instr = f(*a, **k)
            if hasattr(instr, "then_inc"):
                instr.then_inc(sem.h, 1)
                sem.v += 1
            return instr

        return wrapped


class Rot:

    def __init__(self, bufs, ldsems=None, stsems=None):
        self.bufs = bufs if isinstance(bufs, list) else [bufs]
        self.free = [[] for _ in self.bufs]
        self.i = -1
        self.ldsems = ldsems
        self.stsems = stsems

    def acquire(self, writer):
        self.i += 1
        j = self.i % len(self.bufs)
        for sem, val in self.free[j]:
            if val:
                writer.wait_ge(sem.h, val)
        self.free[j] = []
        return self.bufs[j]

    def ldsem(self):
        return self.ldsems[self.i % len(self.bufs)]

    def stsem(self):
        return self.stsems[self.i % len(self.bufs)]

    def cur(self):
        return self.bufs[self.i % len(self.bufs)]

    def release(self, *marks):
        j = self.i % len(self.bufs)
        self.free[j].extend(marks)


class Builder:
    def __init__(self, nc):
        self.nc = nc
        self.es = ExitStack()
        es = self.es
        self.SY = nc.sync
        self.SC = nc.scalar
        self.VE = nc.vector
        self.SE = nc.scalar
        self.PE = nc.tensor
        self.GP = nc.gpsimd

        self.sLd = Sem(es, nc, "sLd")
        self.sSt = Sem(es, nc, "sSt")
        self.sGp = Sem(es, nc, "sGp")
        self.sPe = Sem(es, nc, "sPe")
        self.sVe = Sem(es, nc, "sVe", auto=True)
        self.sSe = Sem(es, nc, "sSe", auto=True)
        self.VE = SelfSync(self.VE, self.sVe)
        self.SE = SelfSync(self.SE, self.sSe)

        self.all_dma_sems = []
        self.dram = {}
        self.decl_dram()
        self.persist = {}
        P = self.persist
        P["A"] = self.sb(es, "A_sb", (128, G * DS))
        P["convw"] = self.sb(es, "convw_sb", (128, G * 4))
        P["convb"] = self.sb(es, "convb_sb", (128, G))
        P["dtb"] = self.sb(es, "dtb_sb", (128, G))
        P["D"] = self.sb(es, "d_sb", (128, G))
        P["ident"] = self.sb(es, "ident_sb", (128, 128))
        P["ones"] = self.sb(es, "ones_sb", (128, 128))
        P["ccol"] = self.sb(es, "ccol_sb", (128, 16 * 8))
        P["b1"] = self.sb(es, "b1_sb", (128, 8 * 32))
        P["cv"] = self.sb(es, "cv_sb", (128, 8))
        P["negA"] = self.sb(es, "negA_sb", (128, G * DS))
        P["ssel"] = self.sb(es, "ssel_sb", (128, 4))
        P["carry0"] = self.sb(es, "carry0_sb", (128, G * DS))

    def decl_dram(self):
        nc = self.nc
        D = self.dram

        def inp(name, shape, dtype=F32):
            D[name] = nc.dram_tensor(name, list(shape), dtype, kind="ExternalInput")

        inp("x_own", (TILE + OWN, DM))
        inp("slot_sel", (128, 4))
        inp("wxp_t", (DM, DI))
        inp("wz_t", (DM, DI))
        inp("convw_c", (128, G * 4))
        inp("convb_c", (128, G))
        inp("xproj_t", (DI, 96))
        inp("dtw_t", (DTR, DI))
        inp("dtb_c", (128, G))
        inp("alog_c", (128, G * DS))
        inp("d_c", (128, G))
        inp("outproj_t", (DI, DM))
        inp("mlpw1_t", (DM, 4 * DM))
        inp("mlpw2_t", (4 * DM, DM))
        inp("gate_w", (DM, 8))
        inp("gate_b_r", (128, 8))
        inp("moe_w1", (8, DM, 4 * DM))
        inp("moe_b1_c", (128, 8 * 32))
        inp("moe_w2", (8, 4 * DM, DM))
        inp("moe_b2_r", (1, 8 * DM))
        inp("const_v", (128, 8))
        inp("ident", (128, 128))
        inp("ones_r", (128, 128))
        inp("lower_r", (128, 64))

        D["o_out"] = nc.dram_tensor("o_out", [OWN, DM], F32, kind="ExternalOutput")
        if _DBG:
            for nm, shape, dt in [("d_dtp", (DI, 2048), F32), ("d_dto", (DI, OWN), F32),
                                  ("d_xc", (DI, OWN), F32), ("d_z", (DI, OWN), F32),
                                  ("d_xn", (OWN, DM), F32), ("d_B", (DS, NTOK), F32),
                                  ("d_C", (DS, OWN), F32), ("d_yg", (DI, OWN), BF16),
                                  ("d_g", (OWN, DM), F32), ("d_cc", (128, 128), F32)]:
                D[nm] = nc.dram_tensor(nm, list(shape), dt, kind="ExternalOutput")

        def scr(name, shape, dtype=F32):
            D[name] = nc.dram_tensor(name, list(shape), dtype, kind="Internal")

        scr("dtD", (DI, NTOK))
        scr("dtuD", (DI, NTOK))
        scr("BD", (DS, NTOK))
        scr("CD", (DS, OWN))
        scr("ccinD", (2 * 128, G * DS))
        scr("ccoutD", (8 * 128, G * DS))
        scr("xcD", (DI, NTOK))
        scr("zD", (DI, OWN))
        scr("xnD", (OWN, DM))
        scr("xnTD", (DM, OWN), BF16)
        scr("ygD", (DI, OWN), BF16)
        scr("gD", (OWN, DM))
        scr("wxp_bfD", (DM, DI), BF16)
        scr("wz_bfD", (DM, DI), BF16)
        scr("xproj_bfD", (DI, 96), BF16)
        scr("dtw_bfD", (DTR, DI), BF16)
        scr("outproj_bfD", (DI, DM), BF16)
        scr("mlpw1_bfD", (DM, 4 * DM), BF16)
        scr("mlpw2_bfD", (4 * DM, DM), BF16)
        scr("gate_bfD", (DM, 8), BF16)
        scr("w1bfD", (8, DM, 4 * DM), BF16)
        scr("w2bfD", (8, 4 * DM, DM), BF16)

    def sb(self, es, name, shape, dtype=F32):
        return es.enter_context(self.nc.sbuf_tensor(name, list(shape), dtype))

    def psum(self, es, name, shape, dtype=F32):
        return es.enter_context(self.nc.psum_tensor(name, list(shape), dtype))

    def dsem(self, name):
        s = Sem(self.es, self.nc, name)
        self.all_dma_sems.append(s)
        return s

    def rot(self, es, name, n, shape, dtype=F32, ld=True, st=False):
        bufs = [self.sb(es, f"{name}{i}", shape, dtype) for i in range(n)]
        lds = [self.dsem(f"{name}{i}_ld") for i in range(n)] if ld else None
        sts = [self.dsem(f"{name}{i}_st") for i in range(n)] if st else None
        return Rot(bufs, lds, sts)

    def join_all(self):
        vals = [(s.h, s.v) for s in (self.sLd, self.sSt, self.sGp, self.sPe, self.sVe, self.sSe)]
        vals += [(s.h, s.v) for s in self.all_dma_sems if s.v]
        for eng in (self.SY, self.SC, self.VE, self.PE, self.GP):
            for h, v in vals:
                if v:
                    eng.wait_ge(h, v)

    def ld(self, out_ap, in_ap, sem=None):
        sem = sem or self.sLd
        self.SY.dma_start(out=out_ap, in_=in_ap).then_inc(sem.h, 16)
        sem.v += 16
        return (sem, sem.v)

    def st(self, out_ap, in_ap, sem=None):
        sem = sem or self.sSt
        self.SC.dma_start(out=out_ap, in_=in_ap).then_inc(sem.h, 16)
        sem.v += 16
        return (sem, sem.v)

    def ldg(self, out_ap, in_ap, sem=None):
        sem = sem or self.sLd
        self.GP.dma_start(out=out_ap, in_=in_ap).then_inc(sem.h, 16)
        sem.v += 16
        return (sem, sem.v)

    @staticmethod
    def wt(eng, *marks):
        for m in marks:
            if m:
                eng.wait_ge(m[0].h, m[1])

    def ph0(self):
        D, P = self.dram, self.persist
        GP, SE = self.GP, self.SE

        def cast(dst, src, nsplit=2):
            rows = src.shape[0]
            step = rows // nsplit
            for i in range(nsplit):
                GP.dma_start(out=dst.ap()[i * step:(i + 1) * step, :],
                             in_=src.ap()[i * step:(i + 1) * step, :]).then_inc(self.sGp.h, 16)
                self.sGp.v += 16

        cast(D["wxp_bfD"], D["wxp_t"])
        cast(D["wz_bfD"], D["wz_t"])
        cast(D["xproj_bfD"], D["xproj_t"], 1)
        cast(D["dtw_bfD"], D["dtw_t"], 1)
        cast(D["gate_bfD"], D["gate_w"], 1)
        self.gp_casts_m = self.sGp.v
        cast(D["outproj_bfD"], D["outproj_t"])
        cast(D["mlpw1_bfD"], D["mlpw1_t"])
        cast(D["mlpw2_bfD"], D["mlpw2_t"])
        for e in range(8):
            GP.dma_start(out=D["w1bfD"].ap()[e], in_=D["moe_w1"].ap()[e]).then_inc(self.sGp.h, 16)
            self.sGp.v += 16
            GP.dma_start(out=D["w2bfD"].ap()[e], in_=D["moe_w2"].ap()[e]).then_inc(self.sGp.h, 16)
            self.sGp.v += 16
        self.gp_casts = self.sGp.v

        self.cA = self.dsem("cA")
        self.ld(P["cv"][:, :], D["const_v"].ap(), self.cA)
        for i, v in enumerate(CONSTS):
            self.nc.const_aps.register(F32, v, P["cv"][:, i:i + 1])
        for dst, src in [(P["convw"], "convw_c"), (P["convb"], "convb_c"),
                         (P["dtb"], "dtb_c"), (P["D"], "d_c"),
                         (P["ident"], "ident"), (P["ones"], "ones_r"),
                         (P["b1"], "moe_b1_c"), (P["A"], "alog_c"),
                         (P["ssel"], "slot_sel")]:
            self.ld(dst[:, :], D[src].ap(), self.cA)
        SE.wait_ge(self.cA.h, self.cA.v)
        SE.activation(P["A"][:, :], P["A"][:, :], AF.Exp)
        SE.activation(P["negA"][:, :], P["A"][:, :], AF.Copy)
        self.sSe.tag(SE.activation(P["A"][:, :], P["A"][:, :], AF.Copy, scale=-1.0))

    def ph1(self):
        D, P = self.dram, self.persist
        es = ExitStack()
        SY, SC, VE, SE, PE = self.SY, self.SC, self.VE, self.SE, self.PE
        sLd, sSt, sVe, sSe, sPe = self.sLd, self.sSt, self.sVe, self.sSe, self.sPe

        xr_r = self.rot(es, "t_xr", 2, (128, 4 * DM))
        xn_sb = self.sb(es, "t_xn", (128, 4 * DM))
        xn_st = self.dsem("t_xn_st")
        st_sb = self.sb(es, "t_st", (128, 64))
        rn0T_r = Rot([self.sb(es, f"t_rn0T{i}", (128, 8 * TILE), BF16) for i in range(2)])
        xpg = [self.sb(es, f"t_xpg{g}", (128, TILE + 3)) for g in range(G)]
        xpg_free = [[] for _ in range(G)]
        xcg_r = self.rot(es, "t_xcg", 2, (128, TILE), ld=False, st=True)
        xcbf_r = Rot([self.sb(es, f"t_xcbf{i}", (128, TILE), BF16) for i in range(2)])
        zev_r = self.rot(es, "t_zev", 2, (128, TILE), ld=False, st=True)
        xdbl_r = self.rot(es, "t_xdbl", 2, (96, TILE), ld=False, st=True)
        xdblbf_r = Rot([self.sb(es, f"t_xdblbf{i}", (64, TILE), BF16) for i in range(2)])
        dtg_r = self.rot(es, "t_dtg", 2, (128, TILE), ld=False, st=True)
        dtug_r = self.rot(es, "t_dtug", 2, (128, TILE), ld=False, st=True)
        xcl_r = self.rot(es, "t_xcl", 2, (128, TILE))
        spm_sb = self.sb(es, "t_spm", (128, TILE))
        spl_sb = self.sb(es, "t_spl", (128, TILE))
        w_r = self.rot(es, "t_w", 2, (128, 8 * DM), BF16)
        gsc = self.sb(es, "t_gsc", (128, 128))
        lower_sb = self.sb(es, "t_lower", (128, 64))
        gateb_sb = self.sb(es, "t_gateb", (128, 8))
        gatew_sb = self.sb(es, "t_gatew", (128, 8 * 8), BF16)
        xnT_r = self.rot(es, "t_xnT", 1, (128, 8 * TILE), BF16, ld=False, st=True)
        xprojw_sb = self.sb(es, "t_xprojw", (128, G * 96), BF16)
        dtww_sb = self.sb(es, "t_dtww", (64, DI), BF16)

        pa_r = Rot([self.psum(es, f"t_pa{i}", (128, TILE)) for i in range(2)])
        px = self.psum(es, "t_px", (128, TILE))
        px_free = []
        pt_r = Rot([self.psum(es, f"t_pt{i}", (128, 128)) for i in range(2)])
        pg_r = Rot(self.psum(es, "t_pg", (128, 8)))

        SY.wait_ge(self.sGp.h, self.gp_casts_m)
        cB = self.dsem("cB")
        self.ld(xprojw_sb[:, :].rearrange("p (g r) -> p g r", r=96), D["xproj_bfD"].ap().rearrange("(g p) r -> p g r", p=128), cB)
        self.ld(dtww_sb[0:64, :], D["dtw_bfD"].ap(), cB)
        self.ld(gatew_sb[:, :].rearrange("p (kk e) -> p kk e", e=8), D["gate_bfD"].ap().rearrange("(kk p) e -> p kk e", p=128), cB)
        self.ld(gateb_sb[:, :], D["gate_b_r"].ap(), cB)
        self.ld(lower_sb[:, :], D["lower_r"].ap(), cB)
        ld_w0 = (cB, cB.v)
        cA2 = (self.cA, self.cA.v)

        self.wt(VE, ld_w0, cA2)
        self.wt(PE, ld_w0, cA2)
        for g in range(G):
            VE.memset(xpg[g][:, 0:3], 0.0)
        self.sVe.tag(VE.memset(st_sb[:, 0:1], 0.0))
        prev = {"se_rn0": 0, "ve_dt": 0, "st_xn": 0}

        xc_spill_marks = [None] * G

        def tile(src_ap, spill_col, own_ti, mask_col, do_scan, do_z):
            xr = xr_r.acquire(SY)
            lx = self.ld(xr[:, :].rearrange("p (a d) -> p a d", a=4), src_ap.rearrange("(a p) d -> p a d", p=128),
                         xr_r.ldsem())

            self.wt(VE, lx)
            VE.tensor_reduce(st_sb[:, 0:4], xr[:, :].rearrange("p (a d) -> p a d", a=4), AX.X, ALU.add)
            self.sVe.tag(VE.tensor_scalar(st_sb[:, 8:12], st_sb[:, 0:4], 1.0 / DM, None, ALU.mult))
            self.wt(SE, lx)
            if prev["st_xn"]:
                self.wt(SE, prev["st_xn"])
            se_sq = self.sSe.tag(SE.activation(xn_sb[:, :], xr[:, :], AF.Square))
            VE.wait_ge(sSe.h, se_sq)
            VE.tensor_reduce(st_sb[:, 4:8], xn_sb[:, :].rearrange("p (a d) -> p a d", a=4), AX.X, ALU.add)
            VE.tensor_scalar(st_sb[:, 12:16], st_sb[:, 4:8], 1.0 / DM, None, ALU.mult)
            VE.tensor_tensor(st_sb[:, 16:20], st_sb[:, 8:12], st_sb[:, 8:12], ALU.mult)
            ve_var = self.sVe.tag(VE.tensor_tensor(st_sb[:, 20:24], st_sb[:, 12:16], st_sb[:, 16:20], ALU.subtract))
            SE.wait_ge(sVe.h, ve_var)
            SE.activation(st_sb[:, 24:28], st_sb[:, 20:24], AF.Ln, bias=EPS_LN)
            se_r = self.sSe.tag(SE.activation(st_sb[:, 24:28], st_sb[:, 24:28], AF.Exp, scale=-0.5))
            VE.wait_ge(sSe.h, se_r)
            last = None
            for a in range(4):
                last = VE.tensor_scalar(xn_sb[:, a * DM:(a + 1) * DM], xr[:, a * DM:(a + 1) * DM],
                                        st_sb[:, 8 + a:9 + a], st_sb[:, 24 + a:25 + a],
                                        ALU.subtract, ALU.mult)
            se_xn = self.sVe.tag(last)
            SE.wait_ge(sVe.h, se_xn)
            se_sq2 = self.sSe.tag(SE.activation(xr[:, :], xn_sb[:, :], AF.Square))
            VE.wait_ge(sSe.h, se_sq2)
            VE.tensor_reduce(st_sb[:, 36:40], xr[:, :].rearrange("p (a d) -> p a d", a=4), AX.X, ALU.add)
            ve_q2 = self.sVe.tag(VE.tensor_scalar(st_sb[:, 40:44], st_sb[:, 36:40], 1.0 / DM, None, ALU.mult))
            SE.wait_ge(sVe.h, ve_q2)
            SE.activation(st_sb[:, 44:48], st_sb[:, 40:44], AF.Ln, bias=EPS_RMS)
            se_r2 = self.sSe.tag(SE.activation(st_sb[:, 44:48], st_sb[:, 44:48], AF.Exp, scale=-0.5))
            VE.wait_ge(sSe.h, se_r2)
            last = None
            for a in range(4):
                last = VE.tensor_scalar(xr[:, a * DM:(a + 1) * DM], xn_sb[:, a * DM:(a + 1) * DM],
                                        st_sb[:, 44 + a:45 + a], None, ALU.mult)
            se_rn0 = self.sVe.tag(last)
            prev["se_rn0"] = se_rn0

            if own_ti is not None:
                SC.wait_ge(sVe.h, se_xn)
                prev["st_xn"] = self.st(
                    D["xnD"].ap()[own_ti * TILE:(own_ti + 1) * TILE, :].rearrange("(a p) d -> p a d", p=128),
                    xn_sb[:, :].rearrange("p (a d) -> p a d", a=4), xn_st)

            rn0T = rn0T_r.acquire(SE)
            PE.wait_ge(sVe.h, se_rn0)
            for a in range(4):
                for dmt in range(8):
                    pt = pt_r.acquire(PE)
                    pe_tr = self.sPe.tag(PE.transpose(
                        pt[:, :], xr[:, a * DM + dmt * 128: a * DM + (dmt + 1) * 128], P["ident"][:, :]))
                    SE.wait_ge(sPe.h, pe_tr)
                    se_ev = self.sSe.tag(SE.activation(
                        rn0T[:, dmt * TILE + a * 128: dmt * TILE + (a + 1) * 128], pt[:, :], AF.Copy))
                    pt_r.release((self.sSe, se_ev))
            se_rn0T = self.sSe.v
            xr_r.release((self.sSe, se_rn0T))

            if own_ti is not None and _F_XNT:
                xnT = xnT_r.acquire(SE)
                for a in range(4):
                    for dmt in range(8):
                        pt = pt_r.acquire(PE)
                        PE.wait_ge(sVe.h, se_xn)
                        pe_tr = self.sPe.tag(PE.transpose(
                            pt[:, :], xn_sb[:, a * DM + dmt * 128: a * DM + (dmt + 1) * 128], P["ident"][:, :]))
                        SE.wait_ge(sPe.h, pe_tr)
                        se_ev = self.sSe.tag(SE.activation(
                            xnT[:, dmt * TILE + a * 128: dmt * TILE + (a + 1) * 128], pt[:, :], AF.Copy))
                        pt_r.release((self.sSe, se_ev))
                se_xnT = self.sSe.v
                SC.wait_ge(sSe.h, se_xnT)
                st_xnT = self.st(
                    D["xnTD"].ap()[:, own_ti * TILE:(own_ti + 1) * TILE].rearrange("(kk p) t -> p kk t", p=128),
                    xnT[:, :].rearrange("p (kk t) -> p kk t", kk=8), xnT_r.stsem())
                self.wt(PE, ld_w0)
                for a in (range(4) if _F_GATE else []):
                    pg = pg_r.acquire(PE)
                    last = None
                    for kk in range(8):
                        last = PE.matmul(pg[:, :],
                                         xnT[:, kk * TILE + a * 128: kk * TILE + (a + 1) * 128],
                                         gatew_sb[:, kk * 8:(kk + 1) * 8],
                                         start=(kk == 0), stop=(kk == 7))
                    pe_g = self.sPe.tag(last)
                    VE.wait_ge(sPe.h, pe_g)
                    VE.tensor_tensor(gsc[:, 0:8], pg[:, :], gateb_sb[:, :], ALU.add)
                    ve_pg = self.sVe.tag(VE.reduce_max(gsc[:, 16:17], gsc[:, 0:8], AX.X))
                    pg_r.release((self.sVe, ve_pg))
                    ve_m = self.sVe.tag(VE.tensor_scalar(gsc[:, 17:18], gsc[:, 16:17], -1.0, None, ALU.mult))
                    SE.wait_ge(sVe.h, ve_m)
                    se_e = self.sSe.tag(SE.activation(gsc[:, 8:16], gsc[:, 0:8], AF.Exp, bias=gsc[:, 17:18]))
                    VE.wait_ge(sSe.h, se_e)
                    VE.tensor_reduce(gsc[:, 19:20], gsc[:, 8:16], AX.X, ALU.add)
                    VE.reciprocal(gsc[:, 20:21], gsc[:, 19:20])
                    VE.tensor_scalar(gsc[:, 8:16], gsc[:, 8:16], gsc[:, 20:21], None, ALU.mult)
                    gw = gsc[:, 8:16]
                    in_j = gw.rearrange("p (o j) -> p o j", o=1).broadcast_to([128, 8, 8])
                    in_e = gw.rearrange("p (e o) -> p e o", o=1).broadcast_to([128, 8, 8])
                    pair = gsc[:, 32:96].rearrange("p (e j) -> p e j", e=8)
                    VE.tensor_tensor(pair, in_j, in_e, ALU.is_gt)
                    VE.tensor_reduce(gsc[:, 96:104], pair, AX.X, ALU.add)
                    VE.tensor_tensor(pair, in_j, in_e, ALU.is_equal)
                    VE.tensor_tensor(gsc[:, 32:96], gsc[:, 32:96], lower_sb[:, :], ALU.mult)
                    VE.tensor_reduce(gsc[:, 104:112], pair, AX.X, ALU.add)
                    VE.tensor_tensor(gsc[:, 96:104], gsc[:, 96:104], gsc[:, 104:112], ALU.add)
                    VE.tensor_scalar(gsc[:, 96:104], gsc[:, 96:104], 2.0, None, ALU.is_lt)
                    tm = own_ti * 4 + a
                    self.sVe.tag(VE.tensor_tensor(P["ccol"][:, tm * 8:(tm + 1) * 8], gw, gsc[:, 96:104], ALU.mult))
                xnT_r.release((self.sPe, self.sPe.v), st_xnT)

            specs = [("xp", D["wxp_bfD"], True)]
            if do_z and _F_Z:
                specs.append(("z", D["wz_bfD"], False))
            pe_xd = None
            for name, wsrc, is_xp in specs:
                for half in range(2):
                    w = w_r.acquire(SY)
                    lw = self.ld(w[:, :].rearrange("p (kk c) -> p kk c", kk=8),
                                 wsrc.ap()[:, half * DM:(half + 1) * DM]
                                 .rearrange("(kk p) c -> p kk c", p=128), w_r.ldsem())
                    self.wt(PE, lw)
                    PE.wait_ge(sSe.h, se_rn0T)
                    last_mm_w = None
                    for gh in range(8):
                        g = half * 8 + gh
                        psx = pa_r.acquire(PE)
                        last = None
                        for kk in range(8):
                            last = PE.matmul(psx[:, :],
                                             w[:, kk * DM + gh * 128: kk * DM + (gh + 1) * 128],
                                             rn0T[:, kk * TILE:(kk + 1) * TILE],
                                             start=(kk == 0), stop=(kk == 7))
                        pe_xp = self.sPe.tag(last)
                        last_mm_w = pe_xp
                        SE.wait_ge(sPe.h, pe_xp)
                        if is_xp:
                            for sem, val in xpg_free[g]:
                                SE.wait_ge(sem.h, val)
                            xpg_free[g] = []
                            se_ev = self.sSe.tag(SE.activation(xpg[g][:, 3:3 + TILE], psx[:, :], AF.Copy))
                            pa_r.release((self.sSe, se_ev))
                            VE.wait_ge(sSe.h, se_ev)
                            xc = xcg_r.acquire(VE)
                            VE.tensor_scalar(xc[:, :], xpg[g][:, 0:TILE], P["convw"][:, g * 4:g * 4 + 1], None, ALU.mult)
                            for t in range(1, 4):
                                VE.scalar_tensor_tensor(xc[:, :], xpg[g][:, t:t + TILE],
                                                        P["convw"][:, g * 4 + t:g * 4 + t + 1], xc[:, :],
                                                        ALU.mult, ALU.add)
                            VE.tensor_scalar(xc[:, :], xc[:, :], P["convb"][:, g:g + 1], None, ALU.add)
                            ve_tail = self.sVe.tag(VE.tensor_copy(xpg[g][:, 0:3], xpg[g][:, TILE:TILE + 3]))
                            SE.wait_ge(sVe.h, ve_tail)
                            se_xc = self.sSe.tag(SE.activation(xc[:, :], xc[:, :], AF.Silu))
                            xpg_free[g] = [(self.sVe, ve_tail)]
                            if spill_col is not None and _F_XDBL:
                                xcbf = xcbf_r.acquire(VE)
                                VE.wait_ge(sSe.h, se_xc)
                                ve_xcb = self.sVe.tag(VE.tensor_copy(xcbf[:, :], xc[:, :]))
                                SC.wait_ge(sSe.h, se_xc)
                                st_xc = self.st(D["xcD"].ap()[g * 128:(g + 1) * 128, spill_col:spill_col + TILE],
                                                xc[:, :], xcg_r.stsem())
                                xc_spill_marks[g] = st_xc
                                xcg_r.release(st_xc)
                                PE.wait_ge(sVe.h, ve_xcb)
                                if g == 0:
                                    for sem, val in px_free:
                                        PE.wait_ge(sem.h, val)
                                    del px_free[:]
                                pe_xd = self.sPe.tag(PE.matmul(px[0:96, :], xprojw_sb[:, g * 96:(g + 1) * 96],
                                                               xcbf[:, :], start=(g == 0), stop=(g == G - 1),
                                                               skip_group_check=True))
                                xcbf_r.release((self.sPe, pe_xd))
                            else:
                                xcg_r.release((self.sSe, se_xc))
                        else:
                            zev = zev_r.acquire(SE)
                            se_ev = self.sSe.tag(SE.activation(zev[:, :], psx[:, :], AF.Copy))
                            pa_r.release((self.sSe, se_ev))
                            SC.wait_ge(sSe.h, se_ev)
                            st_z = self.st(D["zD"].ap()[g * 128:(g + 1) * 128,
                                                        spill_col:spill_col + TILE],
                                           zev[:, :], zev_r.stsem())
                            zev_r.release(st_z)
                    w_r.release((self.sPe, last_mm_w))
            rn0T_r.release((self.sPe, self.sPe.v))

            if not do_scan or not _F_SCAN:
                return

            if not _F_XDBL:
                return
            xdbl = xdbl_r.acquire(SE)
            SE.wait_ge(sPe.h, pe_xd)
            se_xd = self.sSe.tag(SE.activation(xdbl[0:96, :], px[0:96, :], AF.Copy))
            px_free.append((self.sSe, se_xd))
            SC.wait_ge(sSe.h, se_xd)
            st_b = self.st(D["BD"].ap()[:, spill_col:spill_col + TILE], xdbl[64:80, :], xdbl_r.stsem())
            if own_ti is not None:
                st_b = self.st(D["CD"].ap()[:, own_ti * TILE:(own_ti + 1) * TILE], xdbl[80:96, :], xdbl_r.stsem())
            xdblbf = xdblbf_r.acquire(VE)
            VE.wait_ge(sSe.h, se_xd)
            ve_xdb = self.sVe.tag(VE.tensor_copy(xdblbf[0:64, :], xdbl[0:64, :]))
            xdbl_r.release((self.sVe, ve_xdb), st_b)

            for g in (range(G) if _F_DT else []):
                psx = pa_r.acquire(PE)
                PE.wait_ge(sVe.h, ve_xdb)
                pe_dt = self.sPe.tag(PE.matmul(psx[:, :], dtww_sb[0:64, g * 128:(g + 1) * 128],
                                               xdblbf[0:64, :], start=True, stop=True))
                if g == G - 1:
                    xdblbf_r.release((self.sPe, pe_dt))
                if _F_DTSUB < 2:
                    se_sp = self.sSe.v
                    ve_dt0 = self.sVe.v
                    pa_r.release((self.sPe, pe_dt))
                    continue
                VE.wait_ge(sPe.h, pe_dt)
                ve_max = self.sVe.tag(VE.tensor_scalar(spm_sb[:, :], psx[:, :], P["dtb"][:, g:g + 1],
                                                       0.0, ALU.add, ALU.max))
                SE.wait_ge(sVe.h, ve_max)
                SE.activation(spl_sb[:, :], psx[:, :], AF.Abs, bias=P["dtb"][:, g:g + 1])
                SE.activation(spl_sb[:, :], spl_sb[:, :], AF.Exp, scale=-1.0)
                se_sp = self.sSe.tag(SE.activation(spl_sb[:, :], spl_sb[:, :], AF.Ln, bias=1.0))
                if _F_DTSUB < 3:
                    pa_r.release((self.sSe, se_sp))
                    continue
                dtg = dtg_r.acquire(VE)
                VE.wait_ge(sSe.h, se_sp)
                VE.tensor_tensor(dtg[:, :], spm_sb[:, :], spl_sb[:, :], ALU.add)
                if _F_DTSUB < 4:
                    prev["ve_dt"] = self.sVe.tag(VE.tensor_copy(dtg[:, 0:1], dtg[:, 0:1]))
                    pa_r.release((self.sSe, se_sp), (self.sVe, prev["ve_dt"]))
                    dtg_r.release((self.sVe, prev["ve_dt"]))
                    continue
                xcl = xcl_r.acquire(SY)
                self.wt(SY, xc_spill_marks[g])
                lxc = self.ld(xcl[:, :], D["xcD"].ap()[g * 128:(g + 1) * 128, spill_col:spill_col + TILE],
                              xcl_r.ldsem())
                dtug = dtug_r.acquire(VE)
                self.wt(VE, lxc)
                ve_dt = self.sVe.tag(VE.tensor_tensor(dtug[:, :], dtg[:, :], xcl[:, :], ALU.mult))
                prev["ve_dt"] = ve_dt
                pa_r.release((self.sSe, se_sp), (self.sVe, ve_dt))
                xcl_r.release((self.sVe, ve_dt))
                SC.wait_ge(sVe.h, ve_dt)
                st1 = self.st(D["dtD"].ap()[g * 128:(g + 1) * 128, spill_col:spill_col + TILE], dtg[:, :], dtg_r.stsem())
                st2 = self.st(D["dtuD"].ap()[g * 128:(g + 1) * 128, spill_col:spill_col + TILE], dtug[:, :], dtug_r.stsem())
                dtg_r.release(st1)
                dtug_r.release(st2)

        for ti in range(_NT_OWN):
            src = D["x_own"].ap()[ti * TILE:(ti + 1) * TILE, :]
            if ti == 0:
                tile(src, None, None, None, do_scan=False, do_z=False)
            else:
                tile(src, (ti - 1) * TILE, ti - 1, None, do_scan=True, do_z=True)

        self.join_all()
        es.close()

    def ph1b(self):
        D, P = self.dram, self.persist
        es = ExitStack()
        SY, SC, VE, SE, GP = self.SY, self.SC, self.VE, self.SE, self.GP
        sVe, sSe, sGp = self.sVe, self.sSe, self.sGp
        NS = G * DS

        dtq_r = self.rot(es, "u_dt", 2, (128, 4 * CHK))
        Bs_r = self.rot(es, "u_B", 2, (128, CHK))
        scqs = [self.sb(es, f"u_scq{i}", (128, 4 * CHK)) for i in range(2)]
        dtubs = [self.sb(es, f"u_dtub{i}", (128, 4 * CHK)) for i in range(2)]
        dtu_ld = [self.dsem("u_dtu0_ld"), self.dsem("u_dtu1_ld")]
        ebuf_r = Rot([self.sb(es, f"u_e{i}", (128, CHK)) for i in range(2)])
        onesb = self.sb(es, "u_ones", (128, CHK))
        cumc = self.sb(es, "u_cumc", (128, G))
        stot = self.sb(es, "u_stot", (128, G))
        bbs = self.sb(es, "u_bb", (128, CHK))
        ws = self.sb(es, "u_w", (128, CHK))
        hend = self.sb(es, "u_hend", (128, 2 * NS))
        bcol = self.sb(es, "u_bcol", (128, 2))
        redc = self.sb(es, "u_red", (128, 2))
        ccg = self.sb(es, "u_ccg", (128, 8 * NS))
        cf = self.sb(es, "u_cf", (128, 3 * NS))
        ccst = self.dsem("u_ccst")
        ccld = self.dsem("u_ccld")

        VE.memset(onesb[:, :], 1.0)
        self.sVe.tag(VE.memset(cumc[:, :], 0.0))

        prev_se = 0
        prev_ve = 0
        for q in range(4):
            if prev_se:
                VE.wait_ge(sSe.h, prev_se)
            ldtus = [None, None]
            for c in range(2):
                if prev_ve:
                    SY.wait_ge(sVe.h, prev_ve)
                ldtus[c] = self.ld(dtubs[c][:, :].rearrange("p (g t) -> p g t", g=4),
                                   D["dtuD"].ap()[q * TILE:(q + 1) * TILE, c * CHK:(c + 1) * CHK]
                                   .rearrange("(g p) t -> p g t", p=128), dtu_ld[c])
                dtq = dtq_r.acquire(SY)
                ldt = self.ld(dtq[:, :].rearrange("p (g t) -> p g t", g=4),
                              D["dtD"].ap()[q * TILE:(q + 1) * TILE, c * CHK:(c + 1) * CHK]
                              .rearrange("(g p) t -> p g t", p=128), dtq_r.ldsem())
                self.wt(VE, ldt)
                for gi in range(4):
                    g = q * 4 + gi
                    VE.tensor_tensor_scan(
                        scqs[c][:, gi * CHK:(gi + 1) * CHK], onesb[:, :],
                        dtq[:, gi * CHK:(gi + 1) * CHK],
                        cumc[:, g:g + 1], ALU.mult, ALU.add)
                    self.sVe.tag(VE.tensor_copy(cumc[:, g:g + 1],
                                                scqs[c][:, gi * CHK + CHK - 1:gi * CHK + CHK]))
                    if c == 1:
                        self.sVe.tag(VE.tensor_copy(stot[:, g:g + 1], cumc[:, g:g + 1]))
                dtq_r.release((self.sVe, self.sVe.v))
            for c in range(2):
                for gi in range(4):
                    g = q * 4 + gi
                    self.sVe.tag(VE.tensor_scalar(
                        scqs[c][:, gi * CHK:(gi + 1) * CHK], scqs[c][:, gi * CHK:(gi + 1) * CHK],
                        stot[:, g:g + 1], None, ALU.subtract))
            ve_sub = self.sVe.v
            for si in range(DS):
                for c in range(2):
                    Bsb = Bs_r.acquire(SY)
                    lB = self.ld(Bsb[:, :], D["BD"].ap()[si:si + 1, c * CHK:(c + 1) * CHK].squeeze(0)
                                 .partition_broadcast(128), Bs_r.ldsem())
                    for gi in range(4):
                        g = q * 4 + gi
                        col = g * DS + si
                        eb = ebuf_r.acquire(SE)
                        if si == 0 and gi == 0:
                            SE.wait_ge(sVe.h, ve_sub)
                        se_e = self.sSe.tag(SE.activation(
                            eb[:, :], scqs[c][:, gi * CHK:(gi + 1) * CHK], AF.Exp,
                            scale=P["negA"][:, col:col + 1]))
                        prev_se = se_e
                        self.wt(VE, ldtus[c], lB)
                        VE.tensor_tensor(bbs[:, :], dtubs[c][:, gi * CHK:(gi + 1) * CHK],
                                         Bsb[:, :], ALU.mult)
                        VE.wait_ge(sSe.h, se_e)
                        ve_r = self.sVe.tag(VE.tensor_tensor(ws[:, :], bbs[:, :], eb[:, :], ALU.mult))
                        if c == 0:
                            VE.tensor_reduce(hend[:, col:col + 1],
                                             ws[:, :].rearrange("p (o t) -> p o t", o=1),
                                             AX.X, ALU.add)
                        else:
                            VE.tensor_reduce(redc[:, 0:1],
                                             ws[:, :].rearrange("p (o t) -> p o t", o=1),
                                             AX.X, ALU.add)
                            VE.tensor_tensor(hend[:, col:col + 1], hend[:, col:col + 1],
                                             redc[:, 0:1], ALU.add)
                        ebuf_r.release((self.sVe, ve_r))
                    Bs_r.release((self.sVe, self.sVe.v))
            prev_ve = self.sVe.v

        self.sVe.tag(VE.tensor_tensor(
            hend[:, NS:2 * NS].rearrange("p (g s) -> p g s", g=G),
            P["A"][:, :].rearrange("p (g s) -> p g s", g=G),
            stot[:, :].rearrange("p (g o) -> p g o", o=1).broadcast_to([128, G, DS]),
            ALU.mult))
        SE.wait_ge(sVe.h, self.sVe.v)
        se_m = self.sSe.tag(SE.activation(hend[:, NS:2 * NS], hend[:, NS:2 * NS], AF.Exp))

        SC.wait_ge(sSe.h, se_m)
        SC.wait_ge(sVe.h, self.sVe.v)
        st_cc = self.st(D["ccinD"].ap().rearrange("(a p) c -> p a c", p=128),
                        hend[:, :].rearrange("p (a c) -> p a c", a=2), ccst)
        self.wt(GP, st_cc)
        cc_i = self.sGp.tag(GP.collective_compute(
            "AllGather", ALU.bypass, replica_groups=[[0, 1, 2, 3], [4, 5, 6, 7]],
            ins=[D["ccinD"].ap().opt()], outs=[D["ccoutD"].ap().opt()]))
        SY.wait_ge(sGp.h, cc_i)
        lcc = self.ld(ccg[:, :].rearrange("p (a c) -> p a c", a=8),
                      D["ccoutD"].ap().rearrange("(a p) c -> p a c", p=128), ccld)
        self.wt(VE, lcc)
        VE.tensor_copy(cf[:, 0:NS], ccg[:, 0:NS])
        VE.tensor_tensor(cf[:, NS:2 * NS], cf[:, 0:NS], ccg[:, 3 * NS:4 * NS], ALU.mult)
        VE.tensor_tensor(cf[:, NS:2 * NS], cf[:, NS:2 * NS], ccg[:, 2 * NS:3 * NS], ALU.add)
        VE.tensor_tensor(cf[:, 2 * NS:3 * NS], cf[:, NS:2 * NS], ccg[:, 5 * NS:6 * NS], ALU.mult)
        VE.tensor_tensor(cf[:, 2 * NS:3 * NS], cf[:, 2 * NS:3 * NS], ccg[:, 4 * NS:5 * NS], ALU.add)
        VE.tensor_scalar(P["carry0"][:, :], cf[:, 0:NS], P["ssel"][:, 1:2], None, ALU.mult)
        VE.scalar_tensor_tensor(P["carry0"][:, :], cf[:, NS:2 * NS], P["ssel"][:, 2:3],
                                P["carry0"][:, :], ALU.mult, ALU.add)
        self.sVe.tag(VE.scalar_tensor_tensor(P["carry0"][:, :], cf[:, 2 * NS:3 * NS],
                                             P["ssel"][:, 3:4], P["carry0"][:, :],
                                             ALU.mult, ALU.add))
        self.join_all()
        es.close()

    def ph2(self):
        D, P = self.dram, self.persist
        es = ExitStack()
        SY, SC, VE, SE, PE, GP = self.SY, self.SC, self.VE, self.SE, self.PE, self.GP
        sLd, sSt, sVe, sSe, sPe, sGp = self.sLd, self.sSt, self.sVe, self.sSe, self.sPe, self.sGp

        dtq_r = self.rot(es, "s_dtq", 2, (128, 4 * CHK))
        dtuq_r = self.rot(es, "s_dtuq", 2, (128, 4 * CHK))
        Bs_r = self.rot(es, "s_B", 2, (128, CHK))
        Cs_r = self.rot(es, "s_C", 1, (128, CHK))
        ab_r = Rot([self.sb(es, f"s_a{i}", (128, CHK)) for i in range(2)])
        bb = self.sb(es, "s_b", (128, CHK))
        hb = self.sb(es, "s_h", (128, CHK))
        tmp = self.sb(es, "s_tmp", (128, CHK))
        yacc = [self.sb(es, f"s_y{i}", (128, CHK)) for i in range(4)]
        carry = self.sb(es, "s_carry", (128, G * DS))
        ygbf_r = self.rot(es, "s_ygbf", 1, (128, CHK), BF16, ld=False, st=True)
        xcg2_r = self.rot(es, "s_xcg", 1, (128, CHK))
        zg2_r = self.rot(es, "s_zg", 1, (128, CHK))

        m_xnT_r = self.rot(es, "m_xnT", 1, (128, 8 * TILE), BF16)
        m_w1_r = self.rot(es, "m_w1", 2, (128, 8 * 512), BF16)
        m_hT = self.sb(es, "m_hT", (128, 32 * TILE), BF16)
        m_hT_free = []
        m_ctb_r = self.rot(es, "m_ctb", 2, (128, TILE), ld=False, st=True)
        m_b2_r = self.rot(es, "m_b2", 1, (1, TILE))
        m_ps1_r = Rot([self.psum(es, f"m_ps1{i}", (128, TILE)) for i in range(2)])
        m_ps2h = [self.psum(es, f"m_ps2h{i}", (128, TILE)) for i in range(4)]
        m_ps2h_free = [[] for _ in range(4)]
        m_w2b_r = self.rot(es, "m_w2b", 2, (128, 4 * TILE), BF16)

        self.sVe.tag(VE.tensor_copy(carry[:, :], P["carry0"][:, :]))
        gacc = {}

        def moe_unit(e, nq):
            xnT = m_xnT_r.acquire(GP)
            lxn = self.ldg(xnT[:, :].rearrange("p (kk t) -> p kk t", kk=8),
                          D["xnTD"].ap()[:, nq * TILE:(nq + 1) * TILE]
                          .rearrange("(kk p) t -> p kk t", p=128), m_xnT_r.ldsem())

            for sem, val in m_hT_free:
                SE.wait_ge(sem.h, val)
            del m_hT_free[:]
            for hq in range(8):
                w1 = m_w1_r.acquire(GP)
                lw1 = self.ldg(w1[:, :].rearrange("p (kk h) -> p kk h", kk=8),
                              D["w1bfD"].ap()[e][:, hq * 512:(hq + 1) * 512]
                              .rearrange("(kk p) h -> p kk h", p=128), m_w1_r.ldsem())
                self.wt(PE, lw1, lxn)
                last_w1 = None
                for hm in range(4):
                    hmg = hq * 4 + hm
                    ps1 = m_ps1_r.acquire(PE)
                    last = None
                    for kk in range(8):
                        last = PE.matmul(ps1[:, :],
                                         w1[:, kk * 512 + hm * 128: kk * 512 + (hm + 1) * 128],
                                         xnT[:, kk * TILE:(kk + 1) * TILE],
                                         start=(kk == 0), stop=(kk == 7))
                    pe1 = self.sPe.tag(last)
                    last_w1 = pe1
                    SE.wait_ge(sPe.h, pe1)
                    se_ev = self.sSe.tag(SE.activation(
                        m_hT[:, hmg * TILE:(hmg + 1) * TILE],
                        ps1[:, :], AF.Relu, bias=P["b1"][:, e * 32 + hmg:e * 32 + hmg + 1]))
                    m_ps1_r.release((self.sSe, se_ev))
                m_w1_r.release((self.sPe, last_w1))
            se_h_done = self.sSe.v
            PE.wait_ge(sSe.h, se_h_done)
            for dmh in range(2):
                b2 = m_b2_r.acquire(GP)
                lb2 = self.ldg(b2[0:1, :], D["moe_b2_r"].ap()[0:1, e * DM + dmh * TILE:e * DM + (dmh + 1) * TILE],
                              m_b2_r.ldsem())
                self.wt(PE, lb2)
                for hkb in range(8):
                    w2b = m_w2b_r.acquire(GP)
                    lw2 = self.ldg(
                        w2b[:, :].rearrange("p (h c) -> p h c", h=4),
                        D["w2bfD"].ap()[e][hkb * 512:(hkb + 1) * 512, dmh * TILE:(dmh + 1) * TILE]
                        .rearrange("(h p) c -> p h c", p=128),
                        m_w2b_r.ldsem())
                    self.wt(PE, lw2)
                    last_b = None
                    for tl in range(4):
                        if hkb == 0:
                            for mk in m_ps2h_free[tl]:
                                PE.wait_ge(mk[0].h, mk[1])
                            m_ps2h_free[tl] = []
                        for hkin in range(4):
                            hk = hkb * 4 + hkin
                            last_b = PE.matmul(m_ps2h[tl][:, :],
                                               m_hT[:, hk * TILE + tl * 128:hk * TILE + (tl + 1) * 128],
                                               w2b[:, hkin * TILE:(hkin + 1) * TILE],
                                               start=(hk == 0), stop=False,
                                               skip_group_check=True)
                    pe_w2 = self.sPe.tag(last_b)
                    m_w2b_r.release((self.sPe, pe_w2))
                for tl in range(4):
                    tokg = nq * 4 + tl
                    pe2 = self.sPe.tag(PE.matmul(m_ps2h[tl][:, :],
                                                 P["ones"][0:1, 0:128],
                                                 b2[0:1, 0:TILE],
                                                 start=False, stop=True, skip_group_check=True))
                    ctb = m_ctb_r.acquire(VE)
                    VE.wait_ge(sPe.h, pe2)
                    ve_ct = self.sVe.tag(VE.tensor_scalar(
                        ctb[:, :], m_ps2h[tl][:, :], P["ccol"][:, tokg * 8 + e:tokg * 8 + e + 1],
                        None, ALU.mult))
                    m_ps2h_free[tl].append((self.sVe, ve_ct))
                    key = (tokg, dmh)
                    if key in gacc:
                        self.wt(GP, gacc[key])
                    GP.wait_ge(sVe.h, ve_ct)
                    csem = m_ctb_r.stsem()
                    GP.dma_start(out=D["gD"].ap()[tokg * 128:(tokg + 1) * 128, dmh * TILE:(dmh + 1) * TILE],
                                 in_=ctb[:, :],
                                 accum_op=(ALU.add if e > 0 else ALU.bypass)).then_inc(csem.h, 16)
                    csem.v += 16
                    gacc[key] = (csem, csem.v)
                    m_ctb_r.release((csem, csem.v))
                m_b2_r.release((self.sPe, self.sPe.v))
            m_hT_free.append((self.sPe, self.sPe.v))
            m_xnT_r.release((self.sPe, self.sPe.v))

        moe_units = iter([(e, nq) for e in range(8) for nq in range(4)] if _STAGE >= 3 else [])

        n_chunks = NTOK // CHK
        for c in range(n_chunks):
            own_c = True
            for q in range(4):
                dtq = dtq_r.acquire(SY)
                ldt = self.ld(dtq[:, :].rearrange("p (g t) -> p g t", g=4),
                              D["dtD"].ap()[q * TILE:(q + 1) * TILE, c * CHK:(c + 1) * CHK]
                              .rearrange("(g p) t -> p g t", p=128), dtq_r.ldsem())
                dtuq = dtuq_r.acquire(SY)
                ldtu = self.ld(dtuq[:, :].rearrange("p (g t) -> p g t", g=4),
                               D["dtuD"].ap()[q * TILE:(q + 1) * TILE, c * CHK:(c + 1) * CHK]
                               .rearrange("(g p) t -> p g t", p=128), dtuq_r.ldsem())
                ve_last_y = 0
                for si in range(DS):
                    Bsb = Bs_r.acquire(SY)
                    lB = self.ld(Bsb[:, :], D["BD"].ap()[si:si + 1, c * CHK:(c + 1) * CHK].squeeze(0)
                                 .partition_broadcast(128), Bs_r.ldsem())
                    if own_c:
                        Csb = Cs_r.acquire(SY)
                        lC = self.ld(Csb[:, :], D["CD"].ap()[si:si + 1, c * CHK:(c + 1) * CHK].squeeze(0)
                                     .partition_broadcast(128), Cs_r.ldsem())
                    for gi in range(4):
                        gg = q * 4 + gi
                        col = gg * DS + si
                        ab = ab_r.acquire(SE)
                        self.wt(SE, ldt)
                        se_a = self.sSe.tag(SE.activation(ab[:, :], dtq[:, gi * CHK:(gi + 1) * CHK],
                                                          AF.Exp, scale=P["A"][:, col:col + 1]))
                        self.wt(VE, ldtu, lB)
                        VE.tensor_tensor(bb[:, :], dtuq[:, gi * CHK:(gi + 1) * CHK], Bsb[:, :], ALU.mult)
                        VE.wait_ge(sSe.h, se_a)
                        ve_sc = self.sVe.tag(VE.tensor_tensor_scan(
                            hb[:, :], ab[:, :], bb[:, :], carry[:, col:col + 1], ALU.mult, ALU.add))
                        ab_r.release((self.sVe, ve_sc))
                        VE.tensor_copy(carry[:, col:col + 1], hb[:, CHK - 1:CHK])
                        if own_c:
                            self.wt(VE, lC)
                            if si == 0:
                                ve_last_y = self.sVe.tag(VE.tensor_tensor(yacc[gi][:, :], hb[:, :],
                                                                          Csb[:, :], ALU.mult))
                            else:
                                VE.tensor_tensor(tmp[:, :], hb[:, :], Csb[:, :], ALU.mult)
                                ve_last_y = self.sVe.tag(VE.tensor_tensor(yacc[gi][:, :], yacc[gi][:, :],
                                                                          tmp[:, :], ALU.add))
                        else:
                            ve_last_y = self.sVe.v
                    Bs_r.release((self.sVe, self.sVe.v))
                    if own_c:
                        Cs_r.release((self.sVe, self.sVe.v))
                dtq_r.release((self.sSe, self.sSe.v))
                dtuq_r.release((self.sVe, self.sVe.v))
                if own_c:
                    for gi in range(4):
                        gg = q * 4 + gi
                        ocol = c * CHK
                        xcg2 = xcg2_r.acquire(SY)
                        lxc = self.ld(xcg2[:, :], D["xcD"].ap()[gg * 128:(gg + 1) * 128,
                                                                ocol:ocol + CHK], xcg2_r.ldsem())
                        zg2 = zg2_r.acquire(SY)
                        lz = self.ld(zg2[:, :], D["zD"].ap()[gg * 128:(gg + 1) * 128, ocol:ocol + CHK], zg2_r.ldsem())
                        self.wt(VE, lxc, lz)
                        VE.scalar_tensor_tensor(yacc[gi][:, :], xcg2[:, :], P["D"][:, gg:gg + 1],
                                                yacc[gi][:, :], ALU.mult, ALU.add)
                        SE.wait_ge(sVe.h, self.sVe.v)
                        se_t = self.sSe.tag(SE.activation(tmp[:, :], zg2[:, :], AF.Silu))
                        xcg2_r.release((self.sVe, self.sVe.v))
                        zg2_r.release((self.sSe, se_t))
                        ygbf = ygbf_r.acquire(VE)
                        VE.wait_ge(sSe.h, se_t)
                        ve_yg = self.sVe.tag(VE.tensor_tensor(ygbf[:, :], yacc[gi][:, :], tmp[:, :], ALU.mult))
                        SC.wait_ge(sVe.h, ve_yg)
                        st_yg = self.st(D["ygD"].ap()[gg * 128:(gg + 1) * 128, ocol:ocol + CHK], ygbf[:, :],
                                        ygbf_r.stsem())
                        ygbf_r.release(st_yg)
                for _ in range(4):
                    nxt = next(moe_units, None)
                    if nxt is not None:
                        moe_unit(*nxt)
        for nxt in moe_units:
            moe_unit(*nxt)

        self.join_all()
        es.close()

    def ph3(self):
        D, P = self.dram, self.persist
        es = ExitStack()
        SY, SC, VE, SE, PE = self.SY, self.SC, self.VE, self.SE, self.PE
        sLd, sSt, sVe, sSe, sPe = self.sLd, self.sSt, self.sVe, self.sSe, self.sPe

        opw = self.sb(es, "f_opw", (128, G * DM), BF16)
        opw_ld = self.dsem("f_opw_ld")
        ygT_r = self.rot(es, "f_ygT", 1, (128, G * TILE), BF16)
        xn4_r = self.rot(es, "f_xn", 1, (128, 4 * DM))
        x1 = self.sb(es, "f_x1", (128, 4 * DM))
        stt = self.sb(es, "f_st", (128, 16))
        rn1_r = Rot(self.sb(es, "f_rn1", (128, DM)))
        rn1T_r = Rot(self.sb(es, "f_rn1T", (128, 8 * TILE), BF16))
        w1c_r = self.rot(es, "f_w1c", 2, (128, 8 * 128), BF16)
        hT = self.sb(es, "f_hT", (128, 32 * TILE), BF16)
        hT_free = []
        w2c_r = self.rot(es, "f_w2c", 2, (128, 8 * TILE), BF16)
        qq = self.sb(es, "f_q", (128, DM))
        x2_r = Rot(self.sb(es, "f_x2", (128, DM)))
        gg_r = self.rot(es, "f_g", 2, (128, DM))
        xo_r = self.rot(es, "f_xo", 2, (128, DM))
        oo_r = self.rot(es, "f_oo", 2, (128, DM), ld=False, st=True)
        pA_r = Rot([self.psum(es, f"f_pA{i}", (128, DM)) for i in range(2)])
        pB_r = Rot([self.psum(es, f"f_pB{i}", (128, TILE)) for i in range(2)])
        pT_r = Rot(self.psum(es, "f_pT", (128, 128)))

        SY.wait_ge(self.sGp.h, self.gp_casts)
        lop = self.ld(opw[:, :].rearrange("p (kk d) -> p kk d", kk=G),
                      D["outproj_bfD"].ap().rearrange("(kk p) d -> p kk d", p=128), opw_ld)

        for nt in range(4):
            ygT = ygT_r.acquire(SY)
            lyg = self.ld(ygT[:, :].rearrange("p (kk t) -> p kk t", kk=G),
                          D["ygD"].ap()[:, nt * TILE:(nt + 1) * TILE]
                          .rearrange("(kk p) t -> p kk t", p=128), ygT_r.ldsem())
            xn4 = xn4_r.acquire(SY)
            lxn = self.ld(xn4[:, :].rearrange("p (a d) -> p a d", a=4),
                          D["xnD"].ap()[nt * TILE:(nt + 1) * TILE, :]
                          .rearrange("(a p) d -> p a d", p=128), xn4_r.ldsem())
            self.wt(PE, lyg, lxn, lop)
            for a in range(4):
                pA = pA_r.acquire(PE)
                last = None
                for kk in range(G):
                    for half in range(2):
                        last = PE.matmul(pA[:, half * TILE:(half + 1) * TILE],
                                         ygT[:, kk * TILE + a * 128: kk * TILE + (a + 1) * 128],
                                         opw[:, kk * DM + half * TILE: kk * DM + (half + 1) * TILE],
                                         start=(kk == 0), stop=(kk == G - 1 and half == 1))
                pe_op = self.sPe.tag(last)
                VE.wait_ge(sPe.h, pe_op)
                self.wt(VE, lxn)
                ve_x1 = self.sVe.tag(VE.tensor_tensor(
                    x1[:, a * DM:(a + 1) * DM], xn4[:, a * DM:(a + 1) * DM], pA[:, :], ALU.add))
                pA_r.release((self.sVe, ve_x1))
            ve_x1_all = self.sVe.v
            ygT_r.release((self.sPe, self.sPe.v))
            xn4_r.release((self.sVe, ve_x1_all))

            rn1T = rn1T_r.acquire(SE)
            for a in range(4):
                rn1 = rn1_r.acquire(SE)
                SE.wait_ge(sVe.h, ve_x1_all)
                se_sq = self.sSe.tag(SE.activation(rn1[:, :], x1[:, a * DM:(a + 1) * DM], AF.Square))
                VE.wait_ge(sSe.h, se_sq)
                ve_q = self.sVe.tag(VE.tensor_reduce(
                    stt[:, 0:1], rn1[:, :].rearrange("p (o d) -> p o d", o=1), AX.X, ALU.add))
                SE.wait_ge(sVe.h, ve_q)
                SE.activation(stt[:, 1:2], stt[:, 0:1], AF.Ln, bias=EPS_RMS, scale=1.0 / DM)
                self.sSe.tag(SE.activation(stt[:, 1:2], stt[:, 1:2], AF.Exp, scale=-0.5))
                se_r1 = self.sSe.v
                VE.wait_ge(sSe.h, se_r1)
                se_rn = self.sVe.tag(VE.tensor_scalar(rn1[:, :], x1[:, a * DM:(a + 1) * DM],
                                                      stt[:, 1:2], None, ALU.mult))
                PE.wait_ge(sVe.h, se_rn)
                for dmt in range(8):
                    pT = pT_r.acquire(PE)
                    pe_tr = self.sPe.tag(PE.transpose(pT[:, :], rn1[:, dmt * 128:(dmt + 1) * 128],
                                                      P["ident"][:, :]))
                    SE.wait_ge(sPe.h, pe_tr)
                    se_ev = self.sSe.tag(SE.activation(
                        rn1T[:, dmt * TILE + a * 128: dmt * TILE + (a + 1) * 128], pT[:, :], AF.Copy))
                    pT_r.release((self.sSe, se_ev))
                rn1_r.release((self.sPe, self.sPe.v))
            se_rn1T = self.sSe.v

            for sem, val in hT_free:
                VE.wait_ge(sem.h, val)
            del hT_free[:]
            PE.wait_ge(sSe.h, se_rn1T)
            for hm in range(32):
                w1c = w1c_r.acquire(SY)
                lw1 = self.ld(w1c[:, :].rearrange("p (kk h) -> p kk h", kk=8),
                              D["mlpw1_bfD"].ap()[:, hm * 128:(hm + 1) * 128]
                              .rearrange("(kk p) h -> p kk h", p=128), w1c_r.ldsem())
                self.wt(PE, lw1)
                pB = pB_r.acquire(PE)
                last = None
                for kk in range(8):
                    last = PE.matmul(pB[:, :], w1c[:, kk * 128:(kk + 1) * 128],
                                     rn1T[:, kk * TILE:(kk + 1) * TILE],
                                     start=(kk == 0), stop=(kk == 7))
                pe_m1 = self.sPe.tag(last)
                w1c_r.release((self.sPe, pe_m1))
                SE.wait_ge(sPe.h, pe_m1)
                se_h = self.sSe.tag(SE.activation(
                    hT[:, hm * TILE:(hm + 1) * TILE], pB[:, :], AF.Silu))
                pB_r.release((self.sSe, se_h))
            ve_hT = self.sSe.v
            rn1T_r.release((self.sPe, self.sPe.v))

            PE.wait_ge(sSe.h, ve_hT)
            for a in range(4):
                tokg = nt * 4 + a
                pA = pA_r.acquire(PE)
                last = None
                for dmh in range(2):
                    for hkb in range(4):
                        w2c = w2c_r.acquire(SY)
                        lw2 = self.ld(
                            w2c[:, :].rearrange("p (h c) -> p h c", h=8),
                            D["mlpw2_bfD"].ap()[hkb * 1024:(hkb + 1) * 1024, dmh * TILE:(dmh + 1) * TILE]
                            .rearrange("(h p) c -> p h c", p=128),
                            w2c_r.ldsem())
                        self.wt(PE, lw2)
                        for hkin in range(8):
                            hk = hkb * 8 + hkin
                            last = PE.matmul(pA[:, dmh * TILE:(dmh + 1) * TILE],
                                             hT[:, hk * TILE + a * 128: hk * TILE + (a + 1) * 128],
                                             w2c[:, hkin * TILE:(hkin + 1) * TILE],
                                             start=(hk == 0), stop=(hk == 31),
                                             skip_group_check=True)
                        pe_w2 = self.sPe.tag(last)
                        w2c_r.release((self.sPe, pe_w2))
                pe_m2 = self.sPe.v
                gg = gg_r.acquire(SY)
                lg = self.ld(gg[:, :], D["gD"].ap()[tokg * 128:(tokg + 1) * 128, :], gg_r.ldsem())
                xo = xo_r.acquire(SY)
                lx = self.ld(xo[:, :], D["x_own"].ap()[TILE + tokg * 128: TILE + (tokg + 1) * 128, :], xo_r.ldsem())
                x2 = x2_r.acquire(VE)
                VE.wait_ge(sPe.h, pe_m2)
                ve_x2 = self.sVe.tag(VE.tensor_tensor(x2[:, :], x1[:, a * DM:(a + 1) * DM], pA[:, :], ALU.add))
                pA_r.release((self.sVe, ve_x2))
                oo = oo_r.acquire(SE)
                self.wt(SE, lg)
                se_t = self.sSe.tag(SE.activation(oo[:, :], gg[:, :], AF.Silu))
                gg_r.release((self.sSe, se_t))
                VE.wait_ge(sSe.h, se_t)
                VE.tensor_tensor(oo[:, :], oo[:, :], x2[:, :], ALU.mult)
                x2_r.release((self.sVe, self.sVe.v))
                self.wt(VE, lx)
                ve_o = self.sVe.tag(VE.tensor_tensor(oo[:, :], oo[:, :], xo[:, :], ALU.add))
                xo_r.release((self.sVe, ve_o))
                SC.wait_ge(sVe.h, ve_o)
                st_o = self.st(D["o_out"].ap()[tokg * 128:(tokg + 1) * 128, :], oo[:, :], oo_r.stsem())
                oo_r.release(st_o)

        if self.sSt.v:
            SC.wait_ge(self.sSt.h, self.sSt.v)
        for s in self.all_dma_sems:
            if s.v:
                SC.wait_ge(s.h, s.v)
        es.close()

    def finish(self):
        self.es.close()


def build_program():
    nc = bass.Bass("TRN2", target_bir_lowering=False)
    b = Builder(nc)
    b.ph0()
    if _STAGE >= 1:
        b.ph1()
    if _STAGE >= 2:
        b.ph1b()
        b.ph2()
    if _STAGE >= 3:
        b.ph3()
    else:
        with nc.sbuf_tensor("z_oo", [128, DM], F32) as zoo:
            zv = b.sVe.tag(b.VE.memset(zoo[:, :], 0.0))
            b.SC.wait_ge(b.sVe.h, zv)
            for t in range(OWN // 128):
                b.st(b.dram["o_out"].ap()[t * 128:(t + 1) * 128, :], zoo[:, :])
            b.SC.wait_ge(b.sSt.h, b.sSt.v)
    if _DBG:
        D = b.dram
        b.join_all()
        pairs = [("d_dtp", D["dtD"].ap()[:, 0:2048]), ("d_dto", D["dtD"].ap()[:, PREF:]),
                 ("d_xc", D["xcD"].ap()[:, PREF:]), ("d_z", D["zD"].ap()),
                 ("d_xn", D["xnD"].ap()), ("d_B", D["BD"].ap()), ("d_C", D["CD"].ap())]
        if _STAGE >= 2:
            pairs.append(("d_yg", D["ygD"].ap()))
        if _STAGE >= 3:
            pairs.append(("d_g", D["gD"].ap()))
        for nm, srcap in pairs:
            b.st(D[nm].ap(), srcap)
        zv = b.sVe.tag(b.VE.tensor_copy(b.persist["ccol"][:, 0:1], b.persist["ccol"][:, 0:1]))
        b.SC.wait_ge(b.sVe.h, zv)
        b.st(D["d_cc"].ap()[:, 0:128], b.persist["ccol"][:, 0:128])
        b.SC.wait_ge(b.sSt.h, b.sSt.v)
    b.finish()
    return nc


def make_inmaps(inputs):
    x = np.ascontiguousarray(inputs["L0_attn_x"], dtype=np.float32)
    ipw = np.ascontiguousarray(inputs["L0_attn_in_proj_w"], dtype=np.float32)
    conv_w = np.ascontiguousarray(inputs["L0_attn_conv_w"], dtype=np.float32)
    conv_b = np.ascontiguousarray(inputs["L0_attn_conv_b"], dtype=np.float32)
    xpw = np.ascontiguousarray(inputs["L0_attn_x_proj_w"], dtype=np.float32)
    dtw = np.ascontiguousarray(inputs["L0_attn_dt_proj_w"], dtype=np.float32)
    dtb = np.ascontiguousarray(inputs["L0_attn_dt_proj_b"], dtype=np.float32)
    alog = np.ascontiguousarray(inputs["L0_attn_A_log"], dtype=np.float32)
    dd = np.ascontiguousarray(inputs["L0_attn_D"], dtype=np.float32)
    opw = np.ascontiguousarray(inputs["L0_attn_out_proj_w"], dtype=np.float32)
    mw1 = np.ascontiguousarray(inputs["L0_attn_mlp_w1"], dtype=np.float32)
    mw2 = np.ascontiguousarray(inputs["L0_attn_mlp_w2"], dtype=np.float32)
    gw = np.ascontiguousarray(inputs["L0_ffn_gate_w"], dtype=np.float32)
    gb = np.ascontiguousarray(inputs["L0_ffn_gate_b"], dtype=np.float32)
    w1 = np.ascontiguousarray(inputs["L0_ffn_w1"], dtype=np.float32)
    b1 = np.ascontiguousarray(inputs["L0_ffn_b1"], dtype=np.float32)
    w2 = np.ascontiguousarray(inputs["L0_ffn_w2"], dtype=np.float32)
    b2 = np.ascontiguousarray(inputs["L0_ffn_b2"], dtype=np.float32)

    low = np.zeros((64,), np.float32)
    for e in range(8):
        for j in range(8):
            low[e * 8 + j] = 1.0 if j < e else 0.0

    shared = dict(
        wxp_t=np.ascontiguousarray(ipw[:DI].T),
        wz_t=np.ascontiguousarray(ipw[DI:].T),
        convw_c=np.ascontiguousarray(conv_w.reshape(G, 128, 4).transpose(1, 0, 2).reshape(128, G * 4)),
        convb_c=np.ascontiguousarray(conv_b.reshape(G, 128).T),
        xproj_t=np.ascontiguousarray(xpw.T),
        dtw_t=np.ascontiguousarray(dtw.T),
        dtb_c=np.ascontiguousarray(dtb.reshape(G, 128).T),
        alog_c=np.ascontiguousarray(alog.reshape(G, 128, DS).transpose(1, 0, 2).reshape(128, G * DS)),
        d_c=np.ascontiguousarray(dd.reshape(G, 128).T),
        outproj_t=np.ascontiguousarray(opw.T),
        mlpw1_t=np.ascontiguousarray(mw1.T),
        mlpw2_t=np.ascontiguousarray(mw2.T),
        gate_w=gw,
        gate_b_r=np.ascontiguousarray(np.repeat(gb[None, :], 128, 0)),
        moe_w1=w1,
        moe_b1_c=np.ascontiguousarray(b1.reshape(8, 32, 128).transpose(2, 0, 1).reshape(128, 8 * 32)),
        moe_w2=w2,
        moe_b2_r=np.ascontiguousarray(b2.reshape(1, 8 * DM)),
        const_v=np.ascontiguousarray(np.repeat(np.array([CONSTS], np.float32), 128, 0)),
        ident=np.eye(128, dtype=np.float32),
        ones_r=np.ones((128, 128), np.float32),
        lower_r=np.ascontiguousarray(np.repeat(low[None, :], 128, 0)),
    )

    in_maps = []
    for c in range(8):
        b_, k = c // 4, c % 4
        Pk = k * OWN
        ext = np.zeros((TILE, DM), np.float32) if k == 0 else x[b_, Pk - TILE:Pk]
        m = dict(shared)
        m["x_own"] = np.ascontiguousarray(np.concatenate([ext, x[b_, Pk:Pk + OWN]], axis=0))
        sel = np.zeros((128, 4), np.float32)
        sel[:, k] = 1.0
        m["slot_sel"] = sel
        in_maps.append(m)
    return in_maps


_NC_CACHE = [None]


def kernel(**inputs):
    in_maps = make_inmaps(inputs)
    if _NC_CACHE[0] is None:
        _NC_CACHE[0] = build_program()
    nc = _NC_CACHE[0]
    res = run_bass_kernel_spmd(nc, in_maps, core_ids=list(range(8)))
    out = np.empty((2, L, DM), np.float32)
    for c in range(8):
        b_, k = c // 4, c % 4
        out[b_, k * OWN:(k + 1) * OWN] = np.asarray(res.results[c]["o_out"])
    return out
